# version 08
# baseline (speedup 1.0000x reference)
import math
import numpy as np
import ml_dtypes
import concourse.bass as bass
import concourse.mybir as mybir
from concourse.bass_utils import run_bass_kernel_spmd
from concourse.alu_op_type import AluOpType
from contextlib import ExitStack

F32 = mybir.dt.float32
BF16 = mybir.dt.bfloat16
I32 = mybir.dt.int32
AF = mybir.ActivationFunctionType
AX = mybir.AxisListType.X
bfloat16 = ml_dtypes.bfloat16

D = 2048
II = 7168
E = 8
R = 64
NH = 16
HD = 128
NTOK = 2048
TT = 512
NTT = NTOK // TT
NKC = D // 128
NIC = II // 128
NOC3 = 3 * NH
SCALING = 16.0 / 64.0
EXP_SHIFT = -12.0
INV_SQRT_HD = 1.0 / math.sqrt(HD)
LN_EPS = 1e-5
BIGNEG = -1.0e9

ENGINES = ("sync", "scalar", "vector", "tensor", "gpsimd")
TRUNC = {"ffn_tt": NTT, "parts": "ABC", "p0cast": True, "p0const": True}


class KB:
    def __init__(self):
        self.eng_ops = {e: [] for e in ENGINES}
        self.res = {}
        self.slot_count = {}
        self.marked = {e: set() for e in ENGINES}
        self.scope = None

    def _resolve(self, myeng, reads, writes):
        needs = set()
        for (name, lo, hi) in reads:
            for en in self.res.get(name, ()):
                if en[0] < hi and lo < en[1] and en[2] is not None:
                    needs.add(en[2])
        for (name, lo, hi) in writes:
            for en in self.res.get(name, ()):
                if en[0] < hi and lo < en[1]:
                    if en[2] is not None:
                        needs.add(en[2])
                    needs.update(en[3])
        return {n for n in needs
                if not (n[0] == "E" and n[1] == myeng and myeng == "tensor")}

    def _commit(self, reads, writes, ev):
        for (name, lo, hi) in reads:
            entries = self.res.setdefault(name, [])
            for en in entries:
                if en[0] == lo and en[1] == hi:
                    en[3].add(ev)
                    break
            else:
                entries.append([lo, hi, None, {ev}])
        for (name, lo, hi) in writes:
            entries = self.res.setdefault(name, [])
            keep = [en for en in entries if not (en[0] < hi and lo < en[1])]
            keep.append([lo, hi, ev, set()])
            self.res[name] = keep

    def op(self, eng, fn, reads=(), writes=()):
        needs = self._resolve(eng, reads, writes)
        idx = len(self.eng_ops[eng])
        ev = ("E", eng, idx)
        self._commit(reads, writes, ev)
        for n in needs:
            if n[0] == "E":
                self.marked[n[1]].add(n[2])
        self.eng_ops[eng].append(
            {"eng": eng, "fn": fn, "slot": None, "idx": idx,
             "needs": needs, "scope": self.scope})

    def dma(self, eng, slot, fn, ndma, reads=(), writes=()):
        needs = self._resolve(eng, reads, writes)
        cnt = self.slot_count.get(slot, 0) + 16 * ndma
        self.slot_count[slot] = cnt
        idx = len(self.eng_ops[eng])
        ev = ("S", slot, cnt)
        self._commit(reads, writes, ev)
        for n in needs:
            if n[0] == "E":
                self.marked[n[1]].add(n[2])
        self.eng_ops[eng].append(
            {"eng": eng, "fn": fn, "slot": slot, "idx": idx,
             "needs": needs, "scope": self.scope})

    def fence(self, eng, fn, resources):
        self.op(eng, fn, reads=resources, writes=resources)

    def emit(self, nc, block, sems_eng, sems_slot):
        val_at = {}
        for e in ENGINES:
            for rank, i in enumerate(sorted(self.marked[e])):
                val_at[(e, i)] = rank + 1

        for e in ENGINES:
            ops = self.eng_ops[e]
            if not ops:
                continue

            def make_section(e, ops):
                def section(engproxy):
                    know = {}
                    for op in ops:
                        for n in sorted(op["needs"], key=repr):
                            if n[0] == "E":
                                sem, val, key = sems_eng[n[1]], val_at[(n[1], n[2])], ("E", n[1])
                            else:
                                sem, val, key = sems_slot[n[1]], n[2], ("S", n[1])
                            if know.get(key, 0) >= val:
                                continue
                            engproxy.wait_ge(sem, val)
                            know[key] = val
                        h = EmitCtx(engproxy, sems_slot.get(op["slot"]))
                        last = op["fn"](h)
                        if op["slot"] is None and op["idx"] in self.marked[e]:
                            assert last is not None, (e, op["idx"], op["scope"])
                            last.then_inc(sems_eng[e], 1)
                return section
            getattr(block, e)(make_section(e, ops))


class EmitCtx:
    def __init__(self, eng, slot_sem):
        self.eng = eng
        self.slot_sem = slot_sem

    def dma(self, dst, src, **kw):
        inst = self.eng.dma_start(dst, src, **kw)
        inst.then_inc(self.slot_sem, 16)
        return inst


def rs(t, lo=0, hi=1 << 40):
    return (t, lo, hi)


def dump_arb(kb, t, c0, n, label, rows=128):
    if not hasattr(t, "DBG") or not TRUNC.get("dumps"):
        return
    off = t.dbgoff[0]
    assert off + n <= 65536, "DBG overflow"
    t.dbgoff[0] = off + n
    TRUNC.setdefault("dumplog", []).append((label, off, n, rows))

    def f(h, off=off, c0=c0, n=n, rows=rows):
        return h.dma(t.DBG[0:rows, off:off + n], t.ARB[0:rows, c0:c0 + n])
    kb.dma("gpsimd", "dbgd", f, 1,
           reads=[rs("ARB", c0 * 2, (c0 + n) * 2)], writes=[rs("DBG")])


def dump_t(kb, t, tile, res_name, n, label, rows=128):
    if not hasattr(t, "DBG") or not TRUNC.get("dumps"):
        return
    off = t.dbgoff[0]
    t.dbgoff[0] = off + n
    TRUNC.setdefault("dumplog", []).append((label, off, n, rows))

    def f(h, off=off, n=n, rows=rows, tile=tile):
        return h.dma(t.DBG[0:rows, off:off + n], tile[0:rows, 0:n])
    kb.dma("gpsimd", "dbgd", f, 1, reads=[rs(res_name)], writes=[rs("DBG")])


def _prep_shared(inp):
    d = {}
    wqkv = np.asarray(inp["L0_attn_w_qkv"], np.float32)
    d["p_qkv"] = np.ascontiguousarray(
        wqkv.reshape(NKC, 128, NOC3, 128).transpose(2, 1, 0, 3).reshape(NOC3, 128, D))
    d["p_wo"] = np.asarray(inp["L0_attn_w_o"], np.float32)
    gate = np.asarray(inp["L0_ffn_base_gate"], np.float32)
    up = np.asarray(inp["L0_ffn_base_up"], np.float32)
    d["p_gate"] = np.ascontiguousarray(
        gate.reshape(NKC, 128, NIC, 128).transpose(2, 1, 0, 3).reshape(NIC, 128, D))
    d["p_up"] = np.ascontiguousarray(
        up.reshape(NKC, 128, NIC, 128).transpose(2, 1, 0, 3).reshape(NIC, 128, D))
    d["p_down"] = np.asarray(inp["L0_ffn_base_down"], np.float32)
    gA = np.asarray(inp["L0_ffn_gate_A"], np.float32)
    uA = np.asarray(inp["L0_ffn_up_A"], np.float32)
    d["p_gA"] = np.ascontiguousarray(
        gA.reshape(E, NKC, 128, R).transpose(1, 2, 0, 3).reshape(NKC, 128, E * R))
    d["p_uA"] = np.ascontiguousarray(
        uA.reshape(E, NKC, 128, R).transpose(1, 2, 0, 3).reshape(NKC, 128, E * R))
    gB = np.asarray(inp["L0_ffn_gate_B"], np.float32)
    uB = np.asarray(inp["L0_ffn_up_B"], np.float32)
    d["p_gB"] = np.ascontiguousarray(
        gB.reshape(E, R, NIC, 128).transpose(2, 1, 0, 3).reshape(NIC, R, E * 128))
    d["p_uB"] = np.ascontiguousarray(
        uB.reshape(E, R, NIC, 128).transpose(2, 1, 0, 3).reshape(NIC, R, E * 128))
    dA = np.asarray(inp["L0_ffn_down_A"], np.float32)
    d["p_dA"] = np.ascontiguousarray(
        dA.reshape(E, NIC, 128, R).transpose(1, 2, 0, 3).reshape(NIC, 128, E * R))
    d["p_dB"] = np.asarray(inp["L0_ffn_down_B"], np.float32)
    d["p_confWT"] = np.ascontiguousarray(
        np.asarray(inp["L0_ffn_conf_W"], np.float32).T.reshape(NKC, 128, E))
    d["p_confb"] = np.ascontiguousarray(np.broadcast_to(
        np.asarray(inp["L0_ffn_conf_b"], np.float32).reshape(1, E), (128, E)))
    d["p_wealth"] = np.ascontiguousarray(np.broadcast_to(
        np.asarray(inp["L0_ffn_expert_wealth"], np.float32).reshape(1, E), (128, E)))

    d["c_identbf"] = np.eye(128, dtype=bfloat16)
    d["c_ones128bf"] = np.ones((128, 128), dtype=bfloat16)
    d["c_ones1bf"] = np.ones((1, 128), dtype=bfloat16)
    d["c_iota8p8"] = np.broadcast_to(
        np.arange(E, dtype=np.float32)[None, :] + 8.0, (128, E)).copy()
    sel = np.zeros((E, E * 128), dtype=bfloat16)
    for e in range(E):
        sel[e, e * 128:(e + 1) * 128] = 1.0
    d["c_sel"] = sel
    Rl = np.zeros((128, 128), dtype=np.float32)
    for m in range(64):
        Rl[m + 64, m] = -1.0
        Rl[m, m + 64] = 1.0
    d["c_ropeR"] = Rl.astype(bfloat16)
    kk = np.arange(128)[:, None]
    qq = np.arange(TT)[None, :]
    masks = np.concatenate(
        [((m * 128 + kk) <= qq).astype(bfloat16) for m in range(4)], axis=1)
    d["c_masks"] = np.ascontiguousarray(masks)
    fc = np.zeros((128, 2), np.float32)
    fc[:, 0] = LN_EPS
    fc[:, 1] = EXP_SHIFT
    d["c_fconst"] = fc
    return d


def _chunk_owner(g):
    r = g if g < 4 else 7 - g
    return r, (0 if g == r else 8)


def _prep_core(inp, core):
    b, j = core // 4, core % 4
    chunks = (j, 7 - j)
    x = np.asarray(inp["L0_attn_x"], np.float32)
    xs = np.concatenate([x[b, c * 1024:(c + 1) * 1024, :] for c in chunks], axis=0)
    pos = np.concatenate([np.arange(c * 1024, (c + 1) * 1024) for c in chunks])
    inv_freq = 1.0 / (10000.0 ** (np.arange(0, HD, 2, dtype=np.float64) / HD))
    fr = pos[:, None].astype(np.float64) * inv_freq[None, :]
    cos_t = np.concatenate([np.cos(fr).T, np.cos(fr).T], 0).astype(bfloat16)
    sin_t = np.concatenate([np.sin(fr).T, np.sin(fr).T], 0).astype(bfloat16)

    offs = []
    for hh in range(4):
        offs.append((2048 + (4 * j + hh) * 128) * D)
    for hh in range(4):
        offs.append((4 * j + hh) * D)
    for hh in range(4):
        offs.append((4 * j + hh) * 128 * D)
    for r in range(4):
        for half in range(2):
            ch = chunks[half]
            offs.append((r * 512) * 8192 + ch * 1024)
    offtab = np.zeros((1, 256), np.int32)
    offtab[0, :len(offs)] = np.array(offs, np.int32)
    return {"x0": np.ascontiguousarray(xs), "ropec": np.ascontiguousarray(cos_t),
            "ropes": np.ascontiguousarray(sin_t), "offtab": offtab}


class T:
    pass


def build_nc(debug=False, phases=("ffn1", "attn", "ffn2")):
    nc = bass.Bass("TRN2", target_bir_lowering=False, detect_race_conditions=False)
    t = T()

    def din(name, shape, dt=F32):
        setattr(t, name, nc.dram_tensor(name, shape, dt, kind="ExternalInput"))

    din("x0", [NTOK, D])
    din("p_qkv", [NOC3, 128, D]); din("p_wo", [D, D])
    din("p_gate", [NIC, 128, D]); din("p_up", [NIC, 128, D]); din("p_down", [II, D])
    din("p_gA", [NKC, 128, E * R]); din("p_uA", [NKC, 128, E * R])
    din("p_gB", [NIC, R, E * 128]); din("p_uB", [NIC, R, E * 128])
    din("p_dA", [NIC, 128, E * R]); din("p_dB", [E, R, D])
    din("p_confWT", [NKC, 128, E]); din("p_confb", [128, E]); din("p_wealth", [128, E])
    din("c_identbf", [128, 128], BF16)
    din("c_ones128bf", [128, 128], BF16); din("c_ones1bf", [1, 128], BF16)
    din("c_iota8p8", [128, E]); din("c_sel", [E, E * 128], BF16)
    din("c_ropeR", [128, 128], BF16)
    din("c_masks", [128, 4 * TT], BF16); din("c_fconst", [128, 2])
    din("ropec", [128, NTOK], BF16); din("ropes", [128, NTOK], BF16)
    din("offtab", [1, 256], I32)

    t.s_qkv = nc.dram_tensor("s_qkv", [NOC3, 128, D], BF16)
    t.s_gate = nc.dram_tensor("s_gate", [NIC, 128, D], BF16)
    t.s_up = nc.dram_tensor("s_up", [NIC, 128, D], BF16)
    t.s_down = nc.dram_tensor("s_down", [II, D], BF16)
    t.s_wo = nc.dram_tensor("s_wo", [D, D], BF16)
    t.s_gA = nc.dram_tensor("s_gA", [NKC, 128, E * R], BF16)
    t.s_uA = nc.dram_tensor("s_uA", [NKC, 128, E * R], BF16)
    t.s_gB = nc.dram_tensor("s_gB", [NIC, R, E * 128], BF16)
    t.s_uB = nc.dram_tensor("s_uB", [NIC, R, E * 128], BF16)
    t.s_dA = nc.dram_tensor("s_dA", [NIC, 128, E * R], BF16)
    t.s_dB = nc.dram_tensor("s_dB", [E, R, D], BF16)
    if debug:
        t.xB = nc.dram_tensor("xB", [NTOK, D], F32, kind="ExternalOutput")
        t.xC = nc.dram_tensor("xC", [NTOK, D], F32, kind="ExternalOutput")
    else:
        t.xB = nc.dram_tensor("xB", [NTOK, D], F32)
        t.xC = nc.dram_tensor("xC", [NTOK, D], F32)
    t.qk_in = nc.dram_tensor("qk_in", [2 * D, NTOK], BF16)
    t.qk_out = nc.dram_tensor("qk_out", [8 * D, NTOK], BF16)
    t.v_in = nc.dram_tensor("v_in", [128, NH * NTOK], BF16)
    t.v_out = nc.dram_tensor("v_out", [512, NH * NTOK], BF16)
    t.o_in = nc.dram_tensor("o_in", [512, 8192], BF16)
    t.o_out = nc.dram_tensor("o_out", [2048, 8192], BF16)
    t.OUT = nc.dram_tensor("OUT", [NTOK, D], F32, kind="ExternalOutput")
    if debug:
        t.DBG = nc.dram_tensor("DBG", [128, 65536], BF16, kind="ExternalOutput")
        t.qk_dbg = nc.dram_tensor("qk_dbg", [2 * D, NTOK], BF16, kind="ExternalOutput")
        t.o_dbg = nc.dram_tensor("o_dbg", [512, 8192], BF16, kind="ExternalOutput")
    t.dbgoff = [0]

    kb = KB()

    with ExitStack() as st:
        en = st.enter_context

        def sb(name, shape, dt):
            setattr(t, name, en(nc.sbuf_tensor(name, shape, dt)))

        for n, sh, dt in [
            ("identbf", [128, 128], BF16),
            ("ones128bf", [128, 128], BF16), ("ones1bf", [1, 128], BF16),
            ("iota8p8", [128, E], F32), ("sel", [E, E * 128], BF16),
            ("ropeR", [128, 128], BF16), ("masks", [128, 4 * TT], BF16),
            ("cosb", [128, NTOK], BF16), ("sinb", [128, NTOK], BF16),
            ("confWT", [128, NKC * E], BF16), ("confb", [128, E], F32),
            ("fconst", [128, 2], F32),
            ("wealth", [128, E], F32), ("offt", [1, 256], I32),
        ]:
            sb(n, sh, dt)
        sb("ARB", [128, 67584], BF16)
        sb("fencet", [1, 8], F32)
        sb("xt", [128, 4 * D], F32)
        sb("ybf", [128, D], BF16)
        sb("sq", [128, TT], F32)
        sb("sf32", [128, 2 * TT], F32)
        sb("hf32", [128, TT], F32)
        sb("rlbl", [1, TT], BF16)
        sb("stats", [128, 64], F32)
        sb("rt", [128, 96], F32)
        sb("bids", [E, TT], F32)
        sb("lrow", [1, 2 * TT], F32)
        sb("rl", [128, TT], F32)
        sb("webf8", [128, E], BF16)
        sb("rlbf", [1, TT], BF16)
        t.pb = [en(nc.psum_tensor(f"pb{i}", [128, TT], F32)) for i in range(7)]
        t.pbh = en(nc.psum_tensor("pbh", [128, 1024], BF16))

        sems_eng = {e: en(nc.semaphore(f"se_{e}")) for e in ENGINES}
        t.roff = en(nc.gpsimd.register("roff"))

        _build_plan(kb, t, debug, phases)

        slots = sorted(kb.slot_count)
        sems_slot = {s: en(nc.semaphore(f"ss_{s}")) for s in slots}
        block = en(nc.Block())
        kb.emit(nc, block, sems_eng, sems_slot)

    return nc


A_YT = 0
A_H = 8192
A_WG = 36864
A_WU = 40960
A_GB = 45056
A_UB = 47104
A_DA = 49152
A_TG = 50176
A_TU = 54272
A_WEB = 58368
A_WER = 62464
A_BG = 62976
A_BU = 63488
A_SB = 64000
A_HB = 64512
A_WEH = 65024


def WEHS(e):
    return A_SB + (e % 3) * TT
A_GAS = 65536
A_UAS = 66560
A_DN = 65536
ARBW = 67584
def C_DBLOC(e):
    return A_WG + e * D if e < 6 else A_WEB + (e - 6) * D
A_Y2T = 0
A_KS = 0
A_VS = 16384
A_QS = 32768
A_OST = 49152
A_PT = 49664
A_QP = 50688
A_RT1 = 51712
A_VSB = 52736
A_WQ = 53760
A_OT = 0
A_WO = 32768


def _build_plan(kb, t, debug, phases):
    _p0(kb, t)
    if TRUNC.get("touch", True):
        def finit(h):
            return h.eng.memset(t.fencet[:], 0.0)
        kb.op("gpsimd", finit, reads=(), writes=[rs("fencet")])
        outs = [t.OUT] + ([t.xB, t.xC] if debug else [])
        def ftouch(h, outs=outs):
            last = None
            for o in outs:
                last = h.dma(o[0:1, 0:8], t.fencet[0:1, 0:8])
            return last
        kb.dma("gpsimd", "touch", ftouch, len(outs),
               reads=[rs("fencet")], writes=[])
    last = phases[-1]
    if "ffn1" in phases:
        kb.scope = "ffn1"
        dst = t.OUT if (last == "ffn1" and not debug) else t.xB
        _ffn(kb, t, t.x0, dst, tag="f1")
    if "attn" in phases:
        kb.scope = "attn"
        dst = t.OUT if (last == "attn" and not debug) else t.xC
        _attn(kb, t, t.xB, dst)
    if "ffn2" in phases:
        kb.scope = "ffn2"
        _ffn(kb, t, t.xC, t.OUT, tag="f2")
    if TRUNC.get("fin", True):
        def fin(h):
            return h.eng.engine_nop()
        reads = [rs("OUT")]
        if debug:
            reads += [rs("xB"), rs("xC")]
        kb.op("vector", fin, reads=reads)


def _p0(kb, t):
    kb.scope = None if TRUNC.get("noscope") else "p0"

    pairs = [
        (t.identbf, t.c_identbf),
        (t.ones128bf, t.c_ones128bf), (t.ones1bf, t.c_ones1bf),
        (t.iota8p8, t.c_iota8p8), (t.sel, t.c_sel), (t.ropeR, t.c_ropeR),
        (t.masks, t.c_masks), (t.cosb, t.ropec), (t.sinb, t.ropes),
        (t.confb, t.p_confb), (t.wealth, t.p_wealth), (t.offt, t.offtab),
        (t.fconst, t.c_fconst),
    ]
    for dst, src in (pairs if TRUNC.get("p0const", True) else []):
        def f(h, dst=dst, src=src):
            return h.dma(dst[:], src[:])
        kb.dma("sync", "const", f, 1, writes=[rs(dst.name)])
    for kc in range(NKC if TRUNC.get("p0const", True) else 0):
        def f(h, kc=kc):
            return h.dma(t.confWT[:, kc * E:(kc + 1) * E], t.p_confWT[kc])
        kb.dma("gpsimd", "const2", f, 1, writes=[rs("confWT", kc * E, (kc + 1) * E)])

    cast_groups = [
        [(t.s_gA, t.p_gA), (t.s_uA, t.p_uA), (t.s_gB, t.p_gB),
         (t.s_uB, t.p_uB), (t.s_dA, t.p_dA), (t.s_gate, t.p_gate),
         (t.s_up, t.p_up)],
        [(t.s_down, t.p_down), (t.s_dB, t.p_dB)],
        [(t.s_qkv, t.p_qkv), (t.s_wo, t.p_wo)],
    ]

    def chunks_of(dst, src):
        shape = [int(x) for x in src.shape]
        cols = shape[-1]
        rows = 1
        for s in shape[:-1]:
            rows *= s
        out = []
        o = 0
        while o < rows:
            rchunk = min(512, rows - o)
            out.append((dst, src, o, rchunk, cols))
            o += rchunk
        return out

    for gi, casts in enumerate(cast_groups):
        todo = []
        if TRUNC.get("p0cast", True):
            if gi == 0:
                for dst, src in casts[:5]:
                    todo += chunks_of(dst, src)
                ga = chunks_of(t.s_gate, t.p_gate)
                up = chunks_of(t.s_up, t.p_up)
                for a, b in zip(ga, up):
                    todo += [a, b]
            else:
                for dst, src in casts:
                    todo += chunks_of(dst, src)
        for dst, src, o, rchunk, cols in todo:
            def f(h, dst=dst, src=src, o=o, rchunk=rchunk, cols=cols):
                return h.dma(
                    bass.AP(dst, o * cols, [[cols, rchunk], [1, cols]]),
                    bass.AP(src, o * cols, [[cols, rchunk], [1, cols]]))
            kb.dma("gpsimd", "wcast", f, 1,
                   writes=[rs(dst.name, o * cols, (o + rchunk) * cols)])



def _ln_tile(kb, t, x_src, tt, tc, tag, out_bf16_direct):
    slot = tc
    row0 = tt * TT + tc * 128

    def fld(h):
        return h.dma(t.xt[:, slot * D:(slot + 1) * D], x_src[row0:row0 + 128, :])
    kb.dma("scalar", f"xld{tag}{slot}", fld, 1,
           reads=[rs(x_src.name, row0 * D, (row0 + 128) * D)],
           writes=[rs("xt", slot * D * 4, (slot + 1) * D * 4)])

    xr = [rs("xt", slot * D * 4, (slot + 1) * D * 4)]

    def fsum(h):
        return h.eng.tensor_reduce(
            t.stats[:, tc:tc + 1], t.xt[:, slot * D:(slot + 1) * D], axis=AX,
            op=AluOpType.add)
    kb.op("vector", fsum, reads=xr, writes=[rs("stats", tc * 4, tc * 4 + 4)])

    for j in range(4):
        def fsq(h, j=j):
            return h.eng.activation(
                t.sq[:], t.xt[:, slot * D + j * TT:slot * D + (j + 1) * TT],
                AF.Square, accum_out=t.stats[:, 32 + tc * 4 + j:33 + tc * 4 + j])
        kb.op("scalar", fsq, reads=xr,
              writes=[rs("sq"), rs("stats", (32 + tc * 4 + j) * 4,
                                   (33 + tc * 4 + j) * 4)])

    def fsqs(h):
        return h.eng.tensor_reduce(
            t.stats[:, 4 + tc:5 + tc], t.stats[:, 32 + tc * 4:32 + tc * 4 + 4],
            axis=AX, op=AluOpType.add)
    kb.op("vector", fsqs,
          reads=[rs("stats", (32 + tc * 4) * 4, (36 + tc * 4) * 4)],
          writes=[rs("stats", (4 + tc) * 4, (5 + tc) * 4)])

    def fm(h):
        return h.eng.tensor_scalar(
            t.stats[:, 8 + tc:9 + tc], t.stats[:, tc:tc + 1], 1.0 / D, None,
            AluOpType.mult)
    kb.op("vector", fm, reads=[rs("stats", tc * 4, tc * 4 + 4)],
          writes=[rs("stats", (8 + tc) * 4, (9 + tc) * 4)])

    def fq2(h):
        return h.eng.tensor_scalar(
            t.stats[:, 12 + tc:13 + tc], t.stats[:, 4 + tc:5 + tc], 1.0 / D, None,
            AluOpType.mult)
    kb.op("vector", fq2, reads=[rs("stats", (4 + tc) * 4, (5 + tc) * 4)],
          writes=[rs("stats", (12 + tc) * 4, (13 + tc) * 4)])

    def fvar(h):
        return h.eng.scalar_tensor_tensor(
            t.stats[:, 16 + tc:17 + tc], t.stats[:, 8 + tc:9 + tc],
            t.stats[:, 8 + tc:9 + tc], t.stats[:, 12 + tc:13 + tc],
            AluOpType.mult, AluOpType.subtract)
    kb.op("vector", fvar,
          reads=[rs("stats", (8 + tc) * 4, (9 + tc) * 4),
                 rs("stats", (12 + tc) * 4, (13 + tc) * 4)],
          writes=[rs("stats", (16 + tc) * 4, (17 + tc) * 4)])
    return slot


def _ln_finish(kb, t):
    def fsq(h):
        return h.eng.activation(t.stats[:, 20:24], t.stats[:, 16:20], AF.Sqrt,
                                bias=t.fconst[:, 0:1], scale=-1.0)
    kb.op("scalar", fsq, reads=[rs("stats", 64, 80), rs("fconst")],
          writes=[rs("stats", 80, 96)])

    def fr(h):
        return h.eng.reciprocal(t.stats[:, 24:28], t.stats[:, 20:24])
    kb.op("vector", fr, reads=[rs("stats", 80, 96)], writes=[rs("stats", 96, 112)])


def _ln_norm(kb, t, tc, slot):
    def fn(h):
        return h.eng.tensor_scalar(
            t.ybf[:], t.xt[:, slot * D:(slot + 1) * D],
            t.stats[:, 8 + tc:9 + tc], t.stats[:, 24 + tc:25 + tc],
            AluOpType.subtract, AluOpType.mult)
    kb.op("vector", fn,
          reads=[rs("xt", slot * D * 4, (slot + 1) * D * 4),
                 rs("stats", (8 + tc) * 4, (9 + tc) * 4),
                 rs("stats", (24 + tc) * 4, (25 + tc) * 4)],
          writes=[rs("ybf")])


def _stt_res_add(kb, t, psum, psum_key, xcol0, fg):
    c0 = xcol0 + fg * TT

    def fn(h):
        return h.eng.tensor_tensor(
            t.xt[:, c0:c0 + TT], psum[:], t.xt[:, c0:c0 + TT], AluOpType.add)
    kb.op("vector", fn,
          reads=[rs(psum_key), rs("xt", c0 * 4, (c0 + TT) * 4)],
          writes=[rs("xt", c0 * 4, (c0 + TT) * 4)])


def _ffn(kb, t, x_src, x_dst, tag):
    pb = t.pb
    arb = t.ARB

    def AB(c0, n=TT):
        return arb[:, c0:c0 + n], rs("ARB", c0 * 2, (c0 + n) * 2)

    def fz(h):
        return h.eng.memset(arb[:, A_TU:A_TU + 4096], 0.0)
    kb.op("vector", fz, reads=(),
          writes=[rs("ARB", A_TU * 2, (A_TU + 4096) * 2)])

    for tt in range(min(NTT, TRUNC["ffn_tt"])):
        sc0 = kb.scope
        for tc in range(4):
            _ln_tile(kb, t, x_src, tt, tc, tag, False)
        _ln_finish(kb, t)
        AL = TRUNC.get("a_parts", 4)
        for tc in range(4):
            slot = tc
            _ln_norm(kb, t, tc, slot)
            zb = 5 + tc % 2
            for grp in range(2 if AL >= 2 else 0):
                for g8 in range(8):
                    fc = grp * 8 + g8

                    def ftr(h, fc=fc, g8=g8):
                        return h.eng.transpose(
                            t.pbh[:, g8 * 128:(g8 + 1) * 128],
                            t.ybf[:, fc * 128:(fc + 1) * 128], t.identbf[:])
                    kb.op("tensor", ftr, reads=[rs("ybf"), rs("identbf")],
                          writes=[rs("pbh", g8, g8 + 1)])

                yg0 = grp * 8 * TT + tc * 128

                def fe2(h, yg0=yg0):
                    return h.eng.copy(
                        bass.AP(arb, yg0, [[ARBW, 128], [TT, 8], [1, 128]]),
                        t.pbh[:].rearrange("p (a b) -> p a b", a=8))
                kb.op("scalar", fe2, reads=[rs("pbh", 0, 8)],
                      writes=[rs("ARB", yg0 * 2, (yg0 + 8 * TT) * 2)])

                if TRUNC.get("a_sub", 4) >= 4:
                    for g8 in range(8):
                        fc = grp * 8 + g8
                        ytc0 = fc * TT + tc * 128

                        def fz(h, fc=fc, ytc0=ytc0, zb=zb):
                            return h.eng.matmul(
                                pb[zb][:, 0:E],
                                arb[:, ytc0:ytc0 + 128],
                                t.confWT[:, fc * E:(fc + 1) * E],
                                start=(fc == 0), stop=(fc == NKC - 1))
                        kb.op("tensor", fz,
                              reads=[rs("confWT", fc * E, (fc + 1) * E),
                                     rs("ARB", ytc0 * 2, (ytc0 + 128) * 2)],
                              writes=[rs(f"pb{zb}", 0, 1)])

            if AL < 3:
                continue
            c = 32 + tc * 8
            bidsc = tc * 8
            selc = 64 + (tc % 2) * 16
            bids = t.rt[:, bidsc:bidsc + 8]
            m1 = t.rt[:, c:c + 1]; i1 = t.rt[:, c + 1:c + 2]
            m2 = t.rt[:, c + 2:c + 3]; i2 = t.rt[:, c + 3:c + 4]
            d12 = t.rt[:, c + 4:c + 6]; r12 = t.rt[:, c + 6:c + 8]
            sel1 = t.rt[:, selc:selc + 8]
            bids2 = t.rt[:, selc + 8:selc + 16]
            bidr = rs("rt", bidsc * 4, (bidsc + 8) * 4)
            rtr = rs("rt", c * 4, (c + 8) * 4)
            rtr2 = rs("rt", selc * 4, (selc + 16) * 4)

            def step(fn, reads, writes, eng="vector"):
                kb.op(eng, fn, reads=reads, writes=writes)

            step(lambda h, bids=bids, zb=zb: h.eng.tensor_tensor(
                bids, pb[zb][:, 0:E], t.confb[:], AluOpType.add),
                 [rs(f"pb{zb}", 0, 1), rs("confb")], [bidr])
            step(lambda h, bids=bids: h.eng.activation(bids, bids, AF.Sigmoid),
                 [bidr], [bidr], eng="scalar")
            step(lambda h, bids=bids: h.eng.tensor_tensor(
                bids, bids, t.wealth[:], AluOpType.mult),
                 [bidr, rs("wealth")], [bidr])
            step(lambda h, bids=bids, m1=m1: h.eng.tensor_reduce(
                m1, bids, axis=AX, op=AluOpType.max), [bidr], [rtr])
            step(lambda h, bids=bids, m1=m1, sel1=sel1: h.eng.tensor_scalar(
                sel1, bids, m1, None, AluOpType.is_equal), [bidr, rtr], [rtr2])
            step(lambda h, sel1=sel1, bids2=bids2: h.eng.scalar_tensor_tensor(
                bids2, sel1, -8.0, t.iota8p8[:], AluOpType.mult, AluOpType.add),
                 [rtr2, rs("iota8p8")], [rtr2])
            step(lambda h, bids2=bids2, i1=i1: h.eng.tensor_reduce(
                i1, bids2, axis=AX, op=AluOpType.min), [rtr2], [rtr])
            step(lambda h, i1=i1, sel1=sel1: h.eng.tensor_scalar(
                sel1, t.iota8p8[:], 8.0, i1, AluOpType.subtract, AluOpType.is_equal),
                 [rs("iota8p8"), rtr], [rtr2])
            step(lambda h, sel1=sel1, bids=bids, bids2=bids2: h.eng.scalar_tensor_tensor(
                bids2, sel1, BIGNEG, bids, AluOpType.mult, AluOpType.add),
                 [rtr2, bidr], [rtr2])
            step(lambda h, bids2=bids2, m2=m2: h.eng.tensor_reduce(
                m2, bids2, axis=AX, op=AluOpType.max), [rtr2], [rtr])
            step(lambda h, bids2=bids2, m2=m2: h.eng.tensor_scalar(
                bids2, bids2, m2, None, AluOpType.is_equal), [rtr2, rtr], [rtr2])
            step(lambda h, bids2=bids2: h.eng.scalar_tensor_tensor(
                bids2, bids2, -8.0, t.iota8p8[:], AluOpType.mult, AluOpType.add),
                 [rtr2, rs("iota8p8")], [rtr2])
            step(lambda h, bids2=bids2, i2=i2: h.eng.tensor_reduce(
                i2, bids2, axis=AX, op=AluOpType.min), [rtr2], [rtr])
            step(lambda h, i2=i2, bids2=bids2: h.eng.tensor_scalar(
                bids2, t.iota8p8[:], 8.0, i2, AluOpType.subtract, AluOpType.is_equal),
                 [rs("iota8p8"), rtr], [rtr2])
            step(lambda h, m1=m1, m2=m2, d12=d12: h.eng.tensor_tensor(
                d12[:, 0:1], m1, m2, AluOpType.subtract), [rtr], [rtr])
            step(lambda h, d12=d12: h.eng.tensor_scalar(
                d12[:, 1:2], d12[:, 0:1], -1.0, None, AluOpType.mult), [rtr], [rtr])
            step(lambda h, d12=d12, r12=r12: h.eng.activation(r12, d12, AF.Sigmoid),
                 [rtr], [rtr], eng="scalar")
            step(lambda h, r12=r12: h.eng.tensor_scalar(
                r12, r12, 0.5, None, AluOpType.mult), [rtr], [rtr])
            step(lambda h, sel1=sel1, r12=r12: h.eng.tensor_scalar(
                sel1, sel1, r12[:, 0:1], None, AluOpType.mult), [rtr2, rtr], [rtr2])
            step(lambda h, bids2=bids2, r12=r12, sel1=sel1: h.eng.scalar_tensor_tensor(
                sel1, bids2, r12[:, 1:2], sel1, AluOpType.mult, AluOpType.add),
                 [rtr2, rtr], [rtr2])
            step(lambda h, sel1=sel1: h.eng.tensor_copy(t.webf8[:], sel1),
                 [rtr2], [rs("webf8")])

            def fwt(h, tc=tc):
                return h.eng.transpose(
                    t.pbh[0:E, 0:128], t.webf8[:], t.identbf[:])
            kb.op("tensor", fwt, reads=[rs("webf8"), rs("identbf")],
                  writes=[rs("pbh", 0, 1)])

            def fwe(h, tc=tc):
                return h.eng.copy(arb[0:E, A_WER + tc * 128:A_WER + (tc + 1) * 128],
                                  t.pbh[0:E, 0:128])
            kb.op("scalar", fwe, reads=[rs("pbh", 0, 1)],
                  writes=[rs("ARB", (A_WER + tc * 128) * 2,
                             (A_WER + (tc + 1) * 128) * 2)])

        if AL < 3:
            continue
        for e in range(E):
            def fbc(h, e=e):
                return h.eng.matmul(
                    pb[6][:], t.sel[:, e * 128:(e + 1) * 128],
                    arb[0:E, A_WER:A_WER + TT], start=True, stop=True)
            kb.op("tensor", fbc,
                  reads=[rs("sel"), rs("ARB", A_WER * 2, (A_WER + TT) * 2)],
                  writes=[rs("pb6")])

            wc0 = A_WEB + e * TT

            def fbe(h, wc0=wc0):
                return h.eng.copy(arb[:, wc0:wc0 + TT], pb[6][:])
            kb.op("scalar", fbe, reads=[rs("pb6")],
                  writes=[rs("ARB", wc0 * 2, (wc0 + TT) * 2)])

        if AL < 4:
            continue
        for wave in range(2):
            for kc in range(NKC):
                buf = kc % 2
                sc = A_GAS + buf * 1024

                def fld(h, kc=kc, sc=sc):
                    h.dma(bass.AP(arb, sc, [[ARBW, 128], [128, 8], [1, 64]]),
                          bass.AP(t.s_gA, kc * 128 * TT,
                                  [[TT, 128], [64, 8], [1, 64]]))
                    return h.dma(
                        bass.AP(arb, sc + 64, [[ARBW, 128], [128, 8], [1, 64]]),
                        bass.AP(t.s_uA, kc * 128 * TT,
                                [[TT, 128], [64, 8], [1, 64]]))
                kb.dma("sync", f"tI{buf}", fld, 2,
                       reads=[rs(t.s_gA.name, kc * 128 * TT, (kc + 1) * 128 * TT),
                              rs(t.s_uA.name, kc * 128 * TT, (kc + 1) * 128 * TT)],
                       writes=[rs("ARB", sc * 2, (sc + 1024) * 2)])
                for e in range(wave * 4, wave * 4 + 4):
                    bank = e % 4

                    def fmm(h, kc=kc, e=e, sc=sc, bank=bank):
                        return h.eng.matmul(
                            pb[bank][:],
                            arb[:, sc + e * 128:sc + (e + 1) * 128],
                            arb[:, kc * TT:(kc + 1) * TT],
                            start=(kc == 0), stop=(kc == NKC - 1))
                    kb.op("tensor", fmm,
                          reads=[rs("ARB", sc * 2, (sc + 1024) * 2),
                                 rs("ARB", kc * TT * 2, (kc + 1) * TT * 2)],
                          writes=[rs(f"pb{bank}")])
            for e in range(wave * 4, wave * 4 + 4):
                bank = e % 4
                dc0 = A_TG + e * TT

                def fev(h, bank=bank, dc0=dc0):
                    return h.eng.mul(arb[:, dc0:dc0 + TT], pb[bank][:], SCALING)
                kb.op("scalar", fev, reads=[rs(f"pb{bank}")],
                      writes=[rs("ARB", dc0 * 2, (dc0 + TT) * 2)])

        if TRUNC.get("dumpffn"):
            dump_arb(kb, t, A_YT, 8192, f"yT_tt{tt}")
            dump_t(kb, t, t.stats, "stats", 28, f"stats_tt{tt}")
            dump_arb(kb, t, A_WER, 512, f"weR_tt{tt}", rows=8)
            dump_arb(kb, t, A_WEB, 4096, f"web_tt{tt}")
            dump_arb(kb, t, A_TG, 4096, f"tg_tt{tt}", rows=128)
        for ic in range(NIC if "B" in TRUNC["parts"] else 0):
            buf = ic % 2
            wg0 = A_WG + buf * D
            wu0 = A_WU + buf * D
            gb0 = A_GB + buf * 1024
            ub0 = A_UB + buf * 1024
            da0 = A_DA + buf * TT

            lpq = A_TU + buf * 2048

            def fldw(h, ic=ic, wg0=wg0, wu0=wu0, lpq=lpq, da0=da0):
                h.dma(arb[:, wg0:wg0 + D], t.s_gate[ic])
                h.dma(arb[:, wu0:wu0 + D], t.s_up[ic])
                h.dma(bass.AP(arb, lpq, [[ARBW, 64], [256, 8], [1, 128]]),
                      bass.AP(t.s_gB, ic * 64 * 1024,
                              [[1024, 64], [128, 8], [1, 128]]))
                h.dma(bass.AP(arb, 64 * ARBW + lpq + 128,
                              [[ARBW, 64], [256, 8], [1, 128]]),
                      bass.AP(t.s_uB, ic * 64 * 1024,
                              [[1024, 64], [128, 8], [1, 128]]))
                return h.dma(arb[:, da0:da0 + TT], t.s_dA[ic])
            kb.dma("sync", f"wB{buf}", fldw, 5,
                   reads=[rs(t.s_gate.name, ic * 128 * D, (ic + 1) * 128 * D),
                          rs(t.s_up.name, ic * 128 * D, (ic + 1) * 128 * D),
                          rs(t.s_gB.name, ic * 64 * 1024, (ic + 1) * 64 * 1024),
                          rs(t.s_uB.name, ic * 64 * 1024, (ic + 1) * 64 * 1024),
                          rs(t.s_dA.name, ic * 128 * TT, (ic + 1) * 128 * TT)],
                   writes=[rs("ARB", wg0 * 2, (wg0 + D) * 2),
                           rs("ARB", wu0 * 2, (wu0 + D) * 2),
                           rs("ARB", lpq * 2, (lpq + 2048) * 2),
                           rs("ARB", da0 * 2, (da0 + TT) * 2)])

            for bank, w0 in ((0, wg0), (1, wu0)):
                for kc in range(NKC):
                    def fmm(h, bank=bank, w0=w0, kc=kc):
                        return h.eng.matmul(
                            pb[bank][:], arb[:, w0 + kc * 128:w0 + (kc + 1) * 128],
                            arb[:, kc * TT:(kc + 1) * TT],
                            start=(kc == 0), stop=(kc == NKC - 1))
                    kb.op("tensor", fmm,
                          reads=[rs("ARB", w0 * 2, (w0 + D) * 2),
                                 rs("ARB", kc * TT * 2, (kc + 1) * TT * 2)],
                          writes=[rs(f"pb{bank}")])
            for bank, b0 in ((0, A_BG), (1, A_BU)):
                def fev(h, bank=bank, b0=b0):
                    return h.eng.copy(arb[:, b0:b0 + TT], pb[bank][:])
                kb.op("scalar", fev, reads=[rs(f"pb{bank}")],
                      writes=[rs("ARB", b0 * 2, (b0 + TT) * 2)])

            def emit_hatd(e, ic=ic, da0=da0):
                wc = WEHS(e)

                def fha(h, e=e, wc=wc):
                    return h.eng.matmul(pb[2][:], t.identbf[:],
                                        arb[:, wc:wc + TT],
                                        start=(e == 0), stop=(e == E - 1))
                kb.op("tensor", fha,
                      reads=[rs("identbf"), rs("ARB", wc * 2, (wc + TT) * 2)],
                      writes=[rs("pb2")])

                bank = 3 + e // 2
                p0 = (e % 2) * 64

                def ftd(h, e=e, ic=ic, da0=da0, bank=bank, p0=p0, wc=wc):
                    return h.eng.matmul(
                        pb[bank][p0:p0 + 64, :],
                        arb[:, da0 + e * R:da0 + (e + 1) * R],
                        arb[:, wc:wc + TT],
                        start=(ic == 0), stop=(ic == NIC - 1),
                        skip_group_check=True)
                kb.op("tensor", ftd,
                      reads=[rs("ARB", (da0 + e * R) * 2, (da0 + (e + 1) * R) * 2),
                             rs("ARB", wc * 2, (wc + TT) * 2)],
                      writes=[rs(f"pb{bank}", p0, p0 + 64)])

            for e in range(E):
                sfc = (e % 2) * TT
                qa = lpq + e * 256
                def fdg(h, e=e, qa=qa):
                    return h.eng.matmul(
                        pb[0][:], arb[:, qa:qa + 128],
                        arb[:, A_TG + e * TT:A_TG + (e + 1) * TT],
                        start=True, stop=False)
                kb.op("tensor", fdg,
                      reads=[rs("ARB", qa * 2, (qa + 128) * 2),
                             rs("ARB", (A_TG + e * TT) * 2, (A_TG + (e + 1) * TT) * 2)],
                      writes=[rs("pb0")])

                def fig(h):
                    return h.eng.matmul(pb[0][:], t.identbf[:],
                                        arb[:, A_BG:A_BG + TT],
                                        start=False, stop=True)
                kb.op("tensor", fig,
                      reads=[rs("identbf"), rs("ARB", A_BG * 2, (A_BG + TT) * 2)],
                      writes=[rs("pb0")])

                if e == 0 and ic == 0 and tt == 0 and TRUNC.get("dumpg"):
                    def fgd(h):
                        return h.eng.copy(t.sq[:], pb[0][:])
                    kb.op("scalar", fgd, reads=[rs("pb0")], writes=[rs("sq")])
                    dump_t(kb, t, t.sq, "sq", TT, "g_e0")

                def fsl(h, sfc=sfc):
                    return h.eng.activation(t.sf32[:, sfc:sfc + TT], pb[0][:],
                                            AF.Silu)
                kb.op("scalar", fsl, reads=[rs("pb0")],
                      writes=[rs("sf32", sfc, sfc + TT)])

                def fdu(h, e=e, qa=qa):
                    return h.eng.matmul(
                        pb[1][:], arb[:, qa + 128:qa + 256],
                        arb[:, A_TG + e * TT:A_TG + (e + 1) * TT],
                        start=True, stop=False)
                kb.op("tensor", fdu,
                      reads=[rs("ARB", (qa + 128) * 2, (qa + 256) * 2),
                             rs("ARB", (A_TG + e * TT) * 2, (A_TG + (e + 1) * TT) * 2)],
                      writes=[rs("pb1")])

                def fiu(h):
                    return h.eng.matmul(pb[1][:], t.identbf[:],
                                        arb[:, A_BU:A_BU + TT],
                                        start=False, stop=True)
                kb.op("tensor", fiu,
                      reads=[rs("identbf"), rs("ARB", A_BU * 2, (A_BU + TT) * 2)],
                      writes=[rs("pb1")])

                if e > 1:
                    emit_hatd(e - 2)

                def fh(h, sfc=sfc):
                    return h.eng.tensor_tensor(
                        t.hf32[:], t.sf32[:, sfc:sfc + TT], pb[1][:],
                        AluOpType.mult)
                kb.op("vector", fh, reads=[rs("sf32", sfc, sfc + TT), rs("pb1")],
                      writes=[rs("hf32")])

                def fweh(h, e=e):
                    wc = WEHS(e)
                    return h.eng.tensor_tensor(
                        arb[:, wc:wc + TT], t.hf32[:],
                        arb[:, A_WEB + e * TT:A_WEB + (e + 1) * TT], AluOpType.mult)
                kb.op("vector", fweh,
                      reads=[rs("hf32"),
                             rs("ARB", (A_WEB + e * TT) * 2, (A_WEB + (e + 1) * TT) * 2)],
                      writes=[rs("ARB", WEHS(e) * 2, (WEHS(e) + TT) * 2)])

                if e == 0 and ic == 0 and tt == 0 and TRUNC.get("dumpg"):
                    def fud(h):
                        return h.eng.copy(t.sq[:], pb[1][:])
                    kb.op("scalar", fud, reads=[rs("pb1")], writes=[rs("sq")])
                    dump_t(kb, t, t.sq, "sq", TT, "u_e0")
                    dump_arb(kb, t, WEHS(0), TT, "weh_e0")
                    dump_arb(kb, t, A_BG, TT, "bg_ic0")
                    dump_arb(kb, t, A_BU, TT, "bu_ic0")
            emit_hatd(E - 2)
            emit_hatd(E - 1)

            def fhe(h, ic=ic):
                return h.eng.copy(arb[:, A_H + ic * TT:A_H + (ic + 1) * TT], pb[2][:])
            kb.op("scalar", fhe, reads=[rs("pb2")],
                  writes=[rs("ARB", (A_H + ic * TT) * 2, (A_H + (ic + 1) * TT) * 2)])

        if TRUNC.get("dumpffn"):
            dump_arb(kb, t, A_H, NIC * TT, f"H_tt{tt}")
        if "C" not in TRUNC["parts"]:
            continue
        for e in range(E):
            bank = 3 + e // 2
            p0 = (e % 2) * 64
            dc0 = A_TG + e * TT

            def fte(h, bank=bank, p0=p0, dc0=dc0):
                return h.eng.mul(arb[0:64, dc0:dc0 + TT], pb[bank][p0:p0 + 64, :],
                                 SCALING)
            kb.op("scalar", fte, reads=[rs(f"pb{bank}", p0, p0 + 64)],
                  writes=[rs("ARB", dc0 * 2, (dc0 + TT) * 2)])
        if TRUNC.get("dumpffn"):
            dump_arb(kb, t, A_TG, 4096, f"td_tt{tt}", rows=64)
        def fdb(h):
            last = None
            for e in range(E):
                c0 = C_DBLOC(e)
                last = h.dma(arb[0:64, c0:c0 + D], t.s_dB[e])
            return last
        kb.dma("sync", "dB", fdb, E, reads=[rs(t.s_dB.name)],
               writes=[rs("ARB", A_WG * 2, (A_WG + 6 * D) * 2),
                       rs("ARB", A_WEB * 2, (A_WEB + 2 * D) * 2)])
        def fxl(h, tt=tt):
            last = None
            for tc in range(4):
                row0 = tt * TT + tc * 128
                last = h.dma(t.xt[:, tc * D:(tc + 1) * D],
                             x_src[row0:row0 + 128, :])
            return last
        kb.dma("scalar", f"xr{tag}", fxl, 4,
               reads=[rs(x_src.name, tt * TT * D, (tt + 1) * TT * D)],
               writes=[rs("xt", 0, 4 * D * 4)])

        for fg in range(4):
            for ic in range(NIC):
                buf = ic % 4
                dn0 = A_DN + buf * TT

                def fdl(h, ic=ic, fg=fg, dn0=dn0):
                    return h.dma(arb[:, dn0:dn0 + TT],
                                 t.s_down[ic * 128:(ic + 1) * 128,
                                          fg * TT:(fg + 1) * TT])
                kb.dma("sync", f"dn{buf}", fdl, 1,
                       reads=[rs(t.s_down.name, ic * 128 * D,
                                 (ic + 1) * 128 * D)],
                       writes=[rs("ARB", dn0 * 2, (dn0 + TT) * 2)])
                for tc in range(4):
                    def fmm(h, ic=ic, tc=tc, dn0=dn0):
                        return h.eng.matmul(
                            pb[tc][:],
                            arb[:, A_H + ic * TT + tc * 128:A_H + ic * TT + (tc + 1) * 128],
                            arb[:, dn0:dn0 + TT],
                            start=(ic == 0), stop=False)
                    kb.op("tensor", fmm,
                          reads=[rs("ARB", (A_H + ic * TT) * 2, (A_H + (ic + 1) * TT) * 2),
                                 rs("ARB", dn0 * 2, (dn0 + TT) * 2)],
                          writes=[rs(f"pb{tc}")])
            for e in range(E):
                tdc = A_TG + e * TT
                dbc = C_DBLOC(e) + fg * TT
                for tc in range(4):
                    def fmm(h, e=e, tc=tc, tdc=tdc, dbc=dbc):
                        return h.eng.matmul(
                            pb[tc][:],
                            arb[0:64, tdc + tc * 128:tdc + (tc + 1) * 128],
                            arb[0:64, dbc:dbc + TT],
                            start=False, stop=(e == E - 1))
                    kb.op("tensor", fmm,
                          reads=[rs("ARB", tdc * 2, (tdc + TT) * 2),
                                 rs("ARB", dbc * 2, (dbc + TT) * 2)],
                          writes=[rs(f"pb{tc}")])
            for tc in range(4):
                _stt_res_add(kb, t, pb[tc], f"pb{tc}", tc * D, fg)
        for tc in range(4):
            row0 = tt * TT + tc * 128

            def fxs(h, tc=tc, row0=row0):
                return h.dma(x_dst[row0:row0 + 128, :], t.xt[:, tc * D:(tc + 1) * D])
            kb.dma("gpsimd", f"xs{tag}", fxs, 1,
                   reads=[rs("xt", tc * D * 4, (tc + 1) * D * 4)],
                   writes=[rs(x_dst.name, row0 * D, (row0 + 128) * D)])

    def ffence(h):
        return h.eng.memset(t.fencet[:], 0.0)
    kb.fence("gpsimd", ffence, [rs(x_dst.name)])


def _attn(kb, t, x_src, x_dst):
    pb = t.pb
    arb = t.ARB

    for tt in range(NTT):
        for tc in range(4):
            _ln_tile(kb, t, x_src, tt, tc, "a", True)
        _ln_finish(kb, t)
        for tc in range(4):
            slot = tc
            _ln_norm(kb, t, tc, slot)
            for grp in range(2):
                for g8 in range(8):
                    fc = grp * 8 + g8

                    def ftr(h, fc=fc, g8=g8):
                        return h.eng.transpose(
                            t.pbh[:, g8 * 128:(g8 + 1) * 128],
                            t.ybf[:, fc * 128:(fc + 1) * 128], t.identbf[:])
                    kb.op("tensor", ftr, reads=[rs("ybf"), rs("identbf")],
                          writes=[rs("pbh", g8, g8 + 1)])

                yg0 = A_Y2T + grp * 8 * TT + tc * 128

                def fe2(h, yg0=yg0):
                    return h.eng.copy(
                        bass.AP(arb, yg0, [[ARBW, 128], [TT, 8], [1, 128]]),
                        t.pbh[:].rearrange("p (a b) -> p a b", a=8))
                kb.op("scalar", fe2, reads=[rs("pbh", 0, 8)],
                      writes=[rs("ARB", yg0 * 2, (yg0 + 8 * TT) * 2)])

        for oc in range(NOC3):
            buf = oc % 2
            wq0 = A_WQ + buf * D
            bank = buf

            def fldw(h, oc=oc, wq0=wq0):
                return h.dma(arb[:, wq0:wq0 + D], t.s_qkv[oc])
            kb.dma("sync", f"qw{buf}", fldw, 1, reads=[rs(t.s_qkv.name)],
                   writes=[rs("ARB", wq0 * 2, (wq0 + D) * 2)])
            for kc in range(NKC):
                def fmm(h, kc=kc, wq0=wq0, bank=bank):
                    return h.eng.matmul(
                        pb[bank][:], arb[:, wq0 + kc * 128:wq0 + (kc + 1) * 128],
                        arb[:, A_Y2T + kc * TT:A_Y2T + (kc + 1) * TT],
                        start=(kc == 0), stop=(kc == NKC - 1))
                kb.op("tensor", fmm,
                      reads=[rs("ARB", wq0 * 2, (wq0 + D) * 2),
                             rs("ARB", (A_Y2T + kc * TT) * 2, (A_Y2T + (kc + 1) * TT) * 2)],
                      writes=[rs(f"pb{bank}")])

            h_ = oc % NH
            kind = oc // NH
            qp0 = A_QP + buf * TT

            def fev(h, bank=bank, qp0=qp0):
                return h.eng.copy(arb[:, qp0:qp0 + TT], pb[bank][:])
            kb.op("scalar", fev, reads=[rs(f"pb{bank}")],
                  writes=[rs("ARB", qp0 * 2, (qp0 + TT) * 2)])

            if kind < 2:
                def frot(h, qp0=qp0):
                    return h.eng.matmul(pb[2][:], t.ropeR[:],
                                        arb[:, qp0:qp0 + TT],
                                        start=True, stop=True)
                kb.op("tensor", frot,
                      reads=[rs("ropeR"), rs("ARB", qp0 * 2, (qp0 + TT) * 2)],
                      writes=[rs("pb2")])

                rt0 = A_RT1 + buf * TT

                def ft1(h, qp0=qp0, rt0=rt0, tt=tt):
                    return h.eng.tensor_tensor(
                        arb[:, rt0:rt0 + TT], arb[:, qp0:qp0 + TT],
                        t.cosb[:, tt * TT:(tt + 1) * TT], AluOpType.mult)
                kb.op("vector", ft1,
                      reads=[rs("ARB", qp0 * 2, (qp0 + TT) * 2), rs("cosb")],
                      writes=[rs("ARB", rt0 * 2, (rt0 + TT) * 2)])

                def ft2(h, qp0=qp0, tt=tt):
                    return h.eng.tensor_tensor(
                        arb[:, qp0:qp0 + TT], pb[2][:],
                        t.sinb[:, tt * TT:(tt + 1) * TT], AluOpType.mult)
                kb.op("vector", ft2, reads=[rs("pb2"), rs("sinb")],
                      writes=[rs("ARB", qp0 * 2, (qp0 + TT) * 2)])

                def ft3(h, qp0=qp0, rt0=rt0):
                    return h.eng.tensor_tensor(
                        arb[:, rt0:rt0 + TT], arb[:, rt0:rt0 + TT],
                        arb[:, qp0:qp0 + TT], AluOpType.add)
                kb.op("vector", ft3,
                      reads=[rs("ARB", rt0 * 2, (rt0 + TT) * 2),
                             rs("ARB", qp0 * 2, (qp0 + TT) * 2)],
                      writes=[rs("ARB", rt0 * 2, (rt0 + TT) * 2)])

                row0 = h_ * 128 + (0 if kind == 0 else D)

                def fst(h, rt0=rt0, row0=row0, tt=tt):
                    return h.dma(t.qk_in[row0:row0 + 128, tt * TT:(tt + 1) * TT],
                                 arb[:, rt0:rt0 + TT])
                kb.dma("gpsimd", "qkst", fst, 1,
                       reads=[rs("ARB", rt0 * 2, (rt0 + TT) * 2)],
                       writes=[rs("qk_in", (row0 * NTOK + tt * TT) * 2,
                                  ((row0 + 128) * NTOK) * 2)])
            else:
                vsb0 = A_VSB + buf * TT
                for lt in range(4):
                    def fvt(h, lt=lt, qp0=qp0):
                        return h.eng.transpose(
                            t.pbh[:, lt * 128:(lt + 1) * 128],
                            arb[:, qp0 + lt * 128:qp0 + (lt + 1) * 128],
                            t.identbf[:])
                    kb.op("tensor", fvt,
                          reads=[rs("ARB", qp0 * 2, (qp0 + TT) * 2),
                                 rs("identbf")],
                          writes=[rs("pbh", lt, lt + 1)])

                def fve(h, vsb0=vsb0):
                    return h.eng.copy(
                        arb[:, vsb0:vsb0 + TT], t.pbh[:, 0:TT])
                kb.op("scalar", fve, reads=[rs("pbh", 0, 4)],
                      writes=[rs("ARB", vsb0 * 2, (vsb0 + TT) * 2)])

                def fst(h, vsb0=vsb0, h_=h_, tt=tt):
                    return h.dma(
                        t.v_in[:, h_ * NTOK + tt * TT:h_ * NTOK + (tt + 1) * TT],
                        arb[:, vsb0:vsb0 + TT])
                kb.dma("gpsimd", "vst", fst, 1,
                       reads=[rs("ARB", vsb0 * 2, (vsb0 + TT) * 2)],
                       writes=[rs("v_in", (h_ * NTOK + tt * TT) * 2,
                                  (h_ * NTOK + (tt + 1) * TT) * 2)])

    if hasattr(t, "qk_dbg"):
        def fqkd(h):
            last = None
            for rr in range(8):
                last = h.dma(t.qk_dbg[rr * 512:(rr + 1) * 512, :],
                             t.qk_in[rr * 512:(rr + 1) * 512, :])
            return last
        kb.dma("gpsimd", "qkdbg", fqkd, 8, reads=[rs("qk_in")], writes=[rs("qk_dbg")])

    def fcc1(h):
        return h.eng.collective_compute(
            "AllGather", mybir.AluOpType.bypass,
            replica_groups=[[0, 1, 2, 3], [4, 5, 6, 7]],
            ins=[t.qk_in.ap().opt()], outs=[t.qk_out.ap().opt()])
    kb.op("gpsimd", fcc1, reads=[rs("qk_in")], writes=[rs("qk_out")])

    def fcc2(h):
        return h.eng.collective_compute(
            "AllGather", mybir.AluOpType.bypass,
            replica_groups=[[0, 1, 2, 3], [4, 5, 6, 7]],
            ins=[t.v_in.ap().opt()], outs=[t.v_out.ap().opt()])
    kb.op("gpsimd", fcc2, reads=[rs("v_in")], writes=[rs("v_out")])

    OFF_K = 0
    OFF_V = 4
    OFF_Q = 8
    OFF_O = 12

    for hh in range(4):
        kbuf = hh % 2
        ks0 = A_KS + kbuf * 8192
        vs0 = A_VS + kbuf * 8192
        qs0 = A_QS + kbuf * 8192
        def fkl(h, hh=hh, ks0=ks0):
            h.eng.reg_load(t.roff, t.offt[0:1, OFF_K + hh:OFF_K + hh + 1])
            return h.dma(
                bass.AP(arb, ks0, [[ARBW, 128], [NTOK, 4], [1, NTOK]]),
                bass.AP(t.qk_out, t.roff, [[NTOK, 128], [2 * D * NTOK, 4], [1, NTOK]]))
        kb.dma("gpsimd", f"kld{kbuf}", fkl, 1,
               reads=[rs("qk_out"), rs("offt")],
               writes=[rs("ARB", ks0 * 2, (ks0 + 4 * NTOK) * 2)])

        def fvl(h, hh=hh, vs0=vs0):
            h.eng.reg_load(t.roff, t.offt[0:1, OFF_V + hh:OFF_V + hh + 1])
            return h.dma(
                bass.AP(arb, vs0, [[ARBW, 128], [NTOK, 4], [1, NTOK]]),
                bass.AP(t.v_out, t.roff,
                        [[NH * NTOK, 128], [128 * NH * NTOK, 4], [1, NTOK]]))
        kb.dma("gpsimd", f"vld{kbuf}", fvl, 1,
               reads=[rs("v_out"), rs("offt")],
               writes=[rs("ARB", vs0 * 2, (vs0 + 4 * NTOK) * 2)])

        def fql(h, hh=hh, qs0=qs0):
            h.eng.reg_load(t.roff, t.offt[0:1, OFF_Q + hh:OFF_Q + hh + 1])
            return h.dma(
                bass.AP(arb, qs0, [[ARBW, 128], [NTOK, 4], [1, NTOK]]),
                bass.AP(t.qk_out, t.roff, [[NTOK, 128], [2 * D * NTOK, 4], [1, NTOK]]))
        kb.dma("gpsimd", f"qld{kbuf}", fql, 1, reads=[rs("qk_out"), rs("offt")],
               writes=[rs("ARB", qs0 * 2, (qs0 + 4 * NTOK) * 2)])

        for qt in range(16):
            g_q = qt // 2
            r_q, ltb_q = _chunk_owner(g_q)
            qcol = qs0 + r_q * NTOK + (0 if ltb_q == 0 else 1024) + (qt % 2) * TT

            nkc_q = 4 * qt + 4
            lbank = 3 if qt % 2 == 0 else 5

            def emit_lav(kc, vs0=vs0, lbank=lbank, nkc_q=nkc_q):
                g = kc // 8
                r, ltb = _chunk_owner(g)
                scol = r * NTOK + (ltb + kc % 8) * 128
                pt0 = A_PT + (kc % 3) * TT
                last = (kc == nkc_q - 1)

                def fl(h, kc=kc, pt0=pt0, lbank=lbank, last=last):
                    return h.eng.matmul(
                        pb[lbank][:], t.ones128bf[:], arb[:, pt0:pt0 + TT],
                        start=(kc == 0), stop=last)
                kb.op("tensor", fl,
                      reads=[rs("ones128bf"), rs("ARB", pt0 * 2, (pt0 + TT) * 2)],
                      writes=[rs(f"pb{lbank}")])

                def fav(h, scol=scol, vs0=vs0, pt0=pt0, kc=kc, last=last):
                    return h.eng.matmul(
                        pb[2][:], arb[:, vs0 + scol:vs0 + scol + 128],
                        arb[:, pt0:pt0 + TT], start=(kc == 0), stop=last)
                kb.op("tensor", fav,
                      reads=[rs("ARB", (vs0 + scol) * 2, (vs0 + scol + 128) * 2),
                             rs("ARB", pt0 * 2, (pt0 + TT) * 2)],
                      writes=[rs("pb2")])

            for kc in range(nkc_q):
                g = kc // 8
                r, ltb = _chunk_owner(g)
                scol = r * NTOK + (ltb + kc % 8) * 128
                sbank = kc % 2
                pt0 = A_PT + (kc % 3) * TT

                def fsm(h, sbank=sbank, scol=scol, ks0=ks0, qcol=qcol):
                    return h.eng.matmul(
                        pb[sbank][:], arb[:, ks0 + scol:ks0 + scol + 128],
                        arb[:, qcol:qcol + TT], start=True, stop=True)
                kb.op("tensor", fsm,
                      reads=[rs("ARB", (ks0 + scol) * 2, (ks0 + scol + 128) * 2),
                             rs("ARB", qcol * 2, (qcol + TT) * 2)],
                      writes=[rs(f"pb{sbank}")])

                def fex(h, sbank=sbank, pt0=pt0):
                    return h.eng.activation(
                        arb[:, pt0:pt0 + TT], pb[sbank][:], AF.Exp,
                        bias=t.fconst[:, 1:2], scale=INV_SQRT_HD)
                kb.op("scalar", fex, reads=[rs(f"pb{sbank}"), rs("fconst")],
                      writes=[rs("ARB", pt0 * 2, (pt0 + TT) * 2)])

                if hh == 0 and qt == 0 and kc == 0 and TRUNC.get("dumps"):
                    dump_arb(kb, t, pt0, TT, "pt_q0k0_premask")
                m = kc - 4 * qt
                if m >= 0:
                    def fmk(h, pt0=pt0, m=m):
                        return h.eng.tensor_tensor(
                            arb[:, pt0:pt0 + TT], arb[:, pt0:pt0 + TT],
                            t.masks[:, m * TT:(m + 1) * TT], AluOpType.mult)
                    kb.op("vector", fmk,
                          reads=[rs("ARB", pt0 * 2, (pt0 + TT) * 2), rs("masks")],
                          writes=[rs("ARB", pt0 * 2, (pt0 + TT) * 2)])

                if hh == 0 and qt == 0 and kc == 0 and TRUNC.get("dumps"):
                    dump_arb(kb, t, pt0, TT, "pt_q0k0_postmask")
                    dump_arb(kb, t, ks0, TT, "k_h0")
                    dump_arb(kb, t, vs0, TT, "v_h0")
                    dump_arb(kb, t, qcol, TT, "q_h0")

                if kc > 0:
                    emit_lav(kc - 1)
            emit_lav(nkc_q - 1)

            def frc(h, lbank=lbank):
                return h.eng.reciprocal(t.rl[:], pb[lbank][:])
            kb.op("vector", frc, reads=[rs(f"pb{lbank}")], writes=[rs("rl")])

            def foe(h):
                return h.eng.tensor_tensor(
                    arb[:, A_OST:A_OST + TT], pb[2][:], t.rl[:], AluOpType.mult)
            kb.op("vector", foe, reads=[rs("pb2"), rs("rl")],
                  writes=[rs("ARB", A_OST * 2, (A_OST + TT) * 2)])

            row0 = hh * 128

            def fos(h, row0=row0, qt=qt):
                return h.dma(t.o_in[row0:row0 + 128, qt * TT:(qt + 1) * TT],
                             arb[:, A_OST:A_OST + TT])
            kb.dma("gpsimd", "ost", fos, 1,
                   reads=[rs("ARB", A_OST * 2, (A_OST + TT) * 2)],
                   writes=[rs("o_in", (row0 * 8192 + qt * TT) * 2,
                              ((row0 + 128) * 8192) * 2)])

    if hasattr(t, "o_dbg"):
        def fod(h):
            last = None
            for rr in range(2):
                last = h.dma(t.o_dbg[rr * 256:(rr + 1) * 256, :],
                             t.o_in[rr * 256:(rr + 1) * 256, :])
            return last
        kb.dma("gpsimd", "odbg", fod, 2, reads=[rs("o_in")], writes=[rs("o_dbg")])

    def fcc3(h):
        return h.eng.collective_compute(
            "AllGather", mybir.AluOpType.bypass,
            replica_groups=[[0, 1, 2, 3], [4, 5, 6, 7]],
            ins=[t.o_in.ap().opt()], outs=[t.o_out.ap().opt()])
    kb.op("gpsimd", fcc3, reads=[rs("o_in")], writes=[rs("o_out")])

    def fol(h):
        last = None
        for r in range(4):
            for half in range(2):
                oi = OFF_O + r * 2 + half
                c0 = A_OT + (4 * r) * NTOK + half * 1024
                h.eng.reg_load(t.roff, t.offt[0:1, oi:oi + 1])
                last = h.dma(
                    bass.AP(arb, c0, [[ARBW, 128], [NTOK, 4], [1, 1024]]),
                    bass.AP(t.o_out, t.roff,
                            [[8192, 128], [128 * 8192, 4], [1, 1024]]))
        return last
    kb.dma("gpsimd", "old", fol, 8, reads=[rs("o_out"), rs("offt")],
           writes=[rs("ARB", A_OT * 2, (A_OT + NH * NTOK) * 2)])
    def fwl(h):
        last = None
        for fc in range(NKC):
            c0 = A_WO + fc * D
            last = h.dma(arb[:, c0:c0 + D], t.s_wo[fc * 128:(fc + 1) * 128, :])
        return last
    kb.dma("sync", "wol", fwl, NKC, reads=[rs(t.s_wo.name)],
           writes=[rs("ARB", A_WO * 2, (A_WO + NKC * D) * 2)])

    for tc16 in range(16):
        row0 = tc16 * 128
        xslot = tc16 % 4

        def fxl(h, row0=row0, xslot=xslot):
            return h.dma(t.xt[:, xslot * D:(xslot + 1) * D],
                         x_src[row0:row0 + 128, :])
        kb.dma("scalar", f"xra{xslot}", fxl, 1,
               reads=[rs(x_src.name, row0 * D, (row0 + 128) * D)],
               writes=[rs("xt", xslot * D * 4, (xslot + 1) * D * 4)])

        for h_ in range(NH):
            oc0 = A_OT + h_ * NTOK + tc16 * 128
            wc0 = A_WO + h_ * D
            for fg in range(4):
                def fmm(h, oc0=oc0, wc0=wc0, fg=fg, h_=h_):
                    return h.eng.matmul(
                        pb[fg][:], arb[:, oc0:oc0 + 128],
                        arb[:, wc0 + fg * TT:wc0 + (fg + 1) * TT],
                        start=(h_ == 0), stop=(h_ == NH - 1))
                kb.op("tensor", fmm,
                      reads=[rs("ARB", oc0 * 2, (oc0 + 128) * 2),
                             rs("ARB", (wc0 + fg * TT) * 2, (wc0 + (fg + 1) * TT) * 2)],
                      writes=[rs(f"pb{fg}")])
        for fg in range(4):
            _stt_res_add(kb, t, pb[fg], f"pb{fg}", xslot * D, fg)

        def fxs(h, row0=row0, xslot=xslot):
            return h.dma(x_dst[row0:row0 + 128, :],
                         t.xt[:, xslot * D:(xslot + 1) * D])
        kb.dma("gpsimd", "xsa", fxs, 1,
               reads=[rs("xt", xslot * D * 4, (xslot + 1) * D * 4)],
               writes=[rs(x_dst.name, row0 * D, (row0 + 128) * D)])

    def ffence2(h):
        return h.eng.memset(t.fencet[:], 0.0)
    kb.fence("gpsimd", ffence2, [rs(x_dst.name)])


def _run(inputs, debug=False, phases=("ffn1", "attn", "ffn2")):
    shared = _prep_shared(inputs)
    in_maps = []
    for core in range(8):
        m = dict(shared)
        m.update(_prep_core(inputs, core))
        in_maps.append(m)
    nc = build_nc(debug=debug, phases=phases)
    res = run_bass_kernel_spmd(nc, in_maps, core_ids=list(range(8)))
    return res


def kernel(**inputs):
    res = _run(inputs)
    out = np.zeros((2, 8192, 2048), np.float32)
    for core in range(8):
        b, j = core // 4, core % 4
        o = np.asarray(res.results[core]["OUT"])
        out[b, j * 1024:(j + 1) * 1024, :] = o[0:1024, :]
        out[b, (7 - j) * 1024:(8 - j) * 1024, :] = o[1024:2048, :]
    return out


# version 09 (final)
# speedup vs baseline: 1.1050x; 1.1050x over previous
import math
import numpy as np
import ml_dtypes
import concourse.bass as bass
import concourse.mybir as mybir
from concourse.bass_utils import run_bass_kernel_spmd
from concourse.alu_op_type import AluOpType
from contextlib import ExitStack

F32 = mybir.dt.float32
BF16 = mybir.dt.bfloat16
I32 = mybir.dt.int32
AF = mybir.ActivationFunctionType
AX = mybir.AxisListType.X
bfloat16 = ml_dtypes.bfloat16

D = 2048
II = 7168
E = 8
R = 64
NH = 16
HD = 128
NTOK = 2048
TT = 512
NTT = NTOK // TT
NKC = D // 128
NIC = II // 128
NOC3 = 3 * NH
SCALING = 16.0 / 64.0
EXP_SHIFT = -12.0
INV_SQRT_HD = 1.0 / math.sqrt(HD)
LN_EPS = 1e-5
BIGNEG = -1.0e9

ENGINES = ("sync", "scalar", "vector", "tensor", "gpsimd")
TRUNC = {"ffn_tt": NTT, "parts": "ABC", "p0cast": True, "p0const": True}


class KB:
    def __init__(self):
        self.eng_ops = {e: [] for e in ENGINES}
        self.res = {}
        self.slot_count = {}
        self.marked = {e: set() for e in ENGINES}
        self.scope = None

    def _resolve(self, myeng, reads, writes):
        needs = set()
        for (name, lo, hi) in reads:
            for en in self.res.get(name, ()):
                if en[0] < hi and lo < en[1] and en[2] is not None:
                    needs.add(en[2])
        for (name, lo, hi) in writes:
            for en in self.res.get(name, ()):
                if en[0] < hi and lo < en[1]:
                    if en[2] is not None:
                        needs.add(en[2])
                    needs.update(en[3])
        return {n for n in needs
                if not (n[0] == "E" and n[1] == myeng and myeng == "tensor")}

    def _commit(self, reads, writes, ev):
        for (name, lo, hi) in reads:
            entries = self.res.setdefault(name, [])
            for en in entries:
                if en[0] == lo and en[1] == hi:
                    en[3].add(ev)
                    break
            else:
                entries.append([lo, hi, None, {ev}])
        for (name, lo, hi) in writes:
            entries = self.res.setdefault(name, [])
            keep = [en for en in entries if not (en[0] < hi and lo < en[1])]
            keep.append([lo, hi, ev, set()])
            self.res[name] = keep

    def op(self, eng, fn, reads=(), writes=()):
        needs = self._resolve(eng, reads, writes)
        idx = len(self.eng_ops[eng])
        ev = ("E", eng, idx)
        self._commit(reads, writes, ev)
        for n in needs:
            if n[0] == "E":
                self.marked[n[1]].add(n[2])
        self.eng_ops[eng].append(
            {"eng": eng, "fn": fn, "slot": None, "idx": idx,
             "needs": needs, "scope": self.scope})

    def dma(self, eng, slot, fn, ndma, reads=(), writes=()):
        needs = self._resolve(eng, reads, writes)
        cnt = self.slot_count.get(slot, 0) + 16 * ndma
        self.slot_count[slot] = cnt
        idx = len(self.eng_ops[eng])
        ev = ("S", slot, cnt)
        self._commit(reads, writes, ev)
        for n in needs:
            if n[0] == "E":
                self.marked[n[1]].add(n[2])
        self.eng_ops[eng].append(
            {"eng": eng, "fn": fn, "slot": slot, "idx": idx,
             "needs": needs, "scope": self.scope})

    def fence(self, eng, fn, resources):
        self.op(eng, fn, reads=resources, writes=resources)

    def emit(self, nc, block, sems_eng, sems_slot):
        val_at = {}
        for e in ENGINES:
            for rank, i in enumerate(sorted(self.marked[e])):
                val_at[(e, i)] = rank + 1

        for e in ENGINES:
            ops = self.eng_ops[e]
            if not ops:
                continue

            def make_section(e, ops):
                def section(engproxy):
                    know = {}
                    for op in ops:
                        for n in sorted(op["needs"], key=repr):
                            if n[0] == "E":
                                sem, val, key = sems_eng[n[1]], val_at[(n[1], n[2])], ("E", n[1])
                            else:
                                sem, val, key = sems_slot[n[1]], n[2], ("S", n[1])
                            if know.get(key, 0) >= val:
                                continue
                            engproxy.wait_ge(sem, val)
                            know[key] = val
                        h = EmitCtx(engproxy, sems_slot.get(op["slot"]))
                        last = op["fn"](h)
                        if op["slot"] is None and op["idx"] in self.marked[e]:
                            assert last is not None, (e, op["idx"], op["scope"])
                            last.then_inc(sems_eng[e], 1)
                return section
            getattr(block, e)(make_section(e, ops))


class EmitCtx:
    def __init__(self, eng, slot_sem):
        self.eng = eng
        self.slot_sem = slot_sem

    def dma(self, dst, src, **kw):
        inst = self.eng.dma_start(dst, src, **kw)
        inst.then_inc(self.slot_sem, 16)
        return inst


def rs(t, lo=0, hi=1 << 40):
    return (t, lo, hi)


def dump_arb(kb, t, c0, n, label, rows=128):
    if not hasattr(t, "DBG") or not TRUNC.get("dumps"):
        return
    off = t.dbgoff[0]
    assert off + n <= 65536, "DBG overflow"
    t.dbgoff[0] = off + n
    TRUNC.setdefault("dumplog", []).append((label, off, n, rows))

    def f(h, off=off, c0=c0, n=n, rows=rows):
        return h.dma(t.DBG[0:rows, off:off + n], t.ARB[0:rows, c0:c0 + n])
    kb.dma("gpsimd", "dbgd", f, 1,
           reads=[rs("ARB", c0 * 2, (c0 + n) * 2)], writes=[rs("DBG")])


def dump_t(kb, t, tile, res_name, n, label, rows=128):
    if not hasattr(t, "DBG") or not TRUNC.get("dumps"):
        return
    off = t.dbgoff[0]
    t.dbgoff[0] = off + n
    TRUNC.setdefault("dumplog", []).append((label, off, n, rows))

    def f(h, off=off, n=n, rows=rows, tile=tile):
        return h.dma(t.DBG[0:rows, off:off + n], tile[0:rows, 0:n])
    kb.dma("gpsimd", "dbgd", f, 1, reads=[rs(res_name)], writes=[rs("DBG")])


def _prep_shared(inp):
    d = {}
    wqkv = np.asarray(inp["L0_attn_w_qkv"], np.float32)
    d["p_qkv"] = np.ascontiguousarray(
        wqkv.reshape(NKC, 128, NOC3, 128).transpose(2, 1, 0, 3).reshape(NOC3, 128, D))
    d["p_wo"] = np.asarray(inp["L0_attn_w_o"], np.float32)
    gate = np.asarray(inp["L0_ffn_base_gate"], np.float32)
    up = np.asarray(inp["L0_ffn_base_up"], np.float32)
    d["p_gate"] = np.ascontiguousarray(
        gate.reshape(NKC, 128, NIC, 128).transpose(2, 1, 0, 3).reshape(NIC, 128, D))
    d["p_up"] = np.ascontiguousarray(
        up.reshape(NKC, 128, NIC, 128).transpose(2, 1, 0, 3).reshape(NIC, 128, D))
    d["p_down"] = np.asarray(inp["L0_ffn_base_down"], np.float32)
    gA = np.asarray(inp["L0_ffn_gate_A"], np.float32)
    uA = np.asarray(inp["L0_ffn_up_A"], np.float32)
    pgA = gA.reshape(E, NKC, 128, R).transpose(1, 2, 0, 3)
    puA = uA.reshape(E, NKC, 128, R).transpose(1, 2, 0, 3)
    tA = np.empty((NKC, 128, E, 128), np.float32)
    tA[:, :, :, 0:64] = pgA
    tA[:, :, :, 64:128] = puA
    d["p_tA"] = np.ascontiguousarray(tA.reshape(NKC, 128, E * 128))
    gB = np.asarray(inp["L0_ffn_gate_B"], np.float32)
    uB = np.asarray(inp["L0_ffn_up_B"], np.float32)
    pgB = gB.reshape(E, R, NIC, 128).transpose(2, 1, 0, 3)
    puB = uB.reshape(E, R, NIC, 128).transpose(2, 1, 0, 3)
    lq = np.zeros((NIC, 128, E, 256), np.float32)
    lq[:, 0:64, :, 0:128] = pgB
    lq[:, 64:128, :, 128:256] = puB
    d["p_lq"] = np.ascontiguousarray(lq.reshape(NIC, 128, E * 256))
    dA = np.asarray(inp["L0_ffn_down_A"], np.float32)
    d["p_dA"] = np.ascontiguousarray(
        dA.reshape(E, NIC, 128, R).transpose(1, 2, 0, 3).reshape(NIC, 128, E * R))
    d["p_dB"] = np.asarray(inp["L0_ffn_down_B"], np.float32)
    d["p_confWT"] = np.ascontiguousarray(
        np.asarray(inp["L0_ffn_conf_W"], np.float32).T.reshape(NKC, 128, E))
    d["p_confb"] = np.ascontiguousarray(np.broadcast_to(
        np.asarray(inp["L0_ffn_conf_b"], np.float32).reshape(1, E), (128, E)))
    d["p_wealth"] = np.ascontiguousarray(np.broadcast_to(
        np.asarray(inp["L0_ffn_expert_wealth"], np.float32).reshape(1, E), (128, E)))

    d["c_identbf"] = np.eye(128, dtype=bfloat16)
    d["c_ones128bf"] = np.ones((128, 128), dtype=bfloat16)
    d["c_ones1bf"] = np.ones((1, 128), dtype=bfloat16)
    d["c_iota8p8"] = np.broadcast_to(
        np.arange(E, dtype=np.float32)[None, :] + 8.0, (128, E)).copy()
    sel = np.zeros((E, E * 128), dtype=bfloat16)
    for e in range(E):
        sel[e, e * 128:(e + 1) * 128] = 1.0
    d["c_sel"] = sel
    Rl = np.zeros((128, 128), dtype=np.float32)
    for m in range(64):
        Rl[m + 64, m] = -1.0
        Rl[m, m + 64] = 1.0
    d["c_ropeR"] = Rl.astype(bfloat16)
    kk = np.arange(128)[:, None]
    qq = np.arange(TT)[None, :]
    masks = np.concatenate(
        [((m * 128 + kk) <= qq).astype(bfloat16) for m in range(4)], axis=1)
    d["c_masks"] = np.ascontiguousarray(masks)
    fc = np.zeros((128, 2), np.float32)
    fc[:, 0] = LN_EPS
    fc[:, 1] = EXP_SHIFT
    d["c_fconst"] = fc
    return d


def _chunk_owner(g):
    r = g if g < 4 else 7 - g
    return r, (0 if g == r else 8)


def _prep_core(inp, core):
    b, j = core // 4, core % 4
    chunks = (j, 7 - j)
    x = np.asarray(inp["L0_attn_x"], np.float32)
    xs = np.concatenate([x[b, c * 1024:(c + 1) * 1024, :] for c in chunks], axis=0)
    pos = np.concatenate([np.arange(c * 1024, (c + 1) * 1024) for c in chunks])
    inv_freq = 1.0 / (10000.0 ** (np.arange(0, HD, 2, dtype=np.float64) / HD))
    fr = pos[:, None].astype(np.float64) * inv_freq[None, :]
    cos_t = np.concatenate([np.cos(fr).T, np.cos(fr).T], 0).astype(bfloat16)
    sin_t = np.concatenate([np.sin(fr).T, np.sin(fr).T], 0).astype(bfloat16)

    offs = []
    for hh in range(4):
        offs.append((2048 + (4 * j + hh) * 128) * D)
    for hh in range(4):
        offs.append((4 * j + hh) * D)
    for hh in range(4):
        offs.append((4 * j + hh) * 128 * D)
    for r in range(4):
        for half in range(2):
            ch = chunks[half]
            offs.append((r * 512) * 8192 + ch * 1024)
    offtab = np.zeros((1, 256), np.int32)
    offtab[0, :len(offs)] = np.array(offs, np.int32)
    return {"x0": np.ascontiguousarray(xs), "ropec": np.ascontiguousarray(cos_t),
            "ropes": np.ascontiguousarray(sin_t), "offtab": offtab}


class T:
    pass


def build_nc(debug=False, phases=("ffn1", "attn", "ffn2")):
    nc = bass.Bass("TRN2", target_bir_lowering=False, detect_race_conditions=False)
    t = T()

    def din(name, shape, dt=F32):
        setattr(t, name, nc.dram_tensor(name, shape, dt, kind="ExternalInput"))

    din("x0", [NTOK, D])
    din("p_qkv", [NOC3, 128, D]); din("p_wo", [D, D])
    din("p_gate", [NIC, 128, D]); din("p_up", [NIC, 128, D]); din("p_down", [II, D])
    din("p_tA", [NKC, 128, E * 128])
    din("p_lq", [NIC, 128, E * 256])
    din("p_dA", [NIC, 128, E * R]); din("p_dB", [E, R, D])
    din("p_confWT", [NKC, 128, E]); din("p_confb", [128, E]); din("p_wealth", [128, E])
    din("c_identbf", [128, 128], BF16)
    din("c_ones128bf", [128, 128], BF16); din("c_ones1bf", [1, 128], BF16)
    din("c_iota8p8", [128, E]); din("c_sel", [E, E * 128], BF16)
    din("c_ropeR", [128, 128], BF16)
    din("c_masks", [128, 4 * TT], BF16); din("c_fconst", [128, 2])
    din("ropec", [128, NTOK], BF16); din("ropes", [128, NTOK], BF16)
    din("offtab", [1, 256], I32)

    t.s_qkv = nc.dram_tensor("s_qkv", [NOC3, 128, D], BF16)
    t.s_gate = nc.dram_tensor("s_gate", [NIC, 128, D], BF16)
    t.s_up = nc.dram_tensor("s_up", [NIC, 128, D], BF16)
    t.s_down = nc.dram_tensor("s_down", [II, D], BF16)
    t.s_wo = nc.dram_tensor("s_wo", [D, D], BF16)
    t.s_tA = nc.dram_tensor("s_tA", [NKC, 128, E * 128], BF16)
    t.s_lq = nc.dram_tensor("s_lq", [NIC, 128, E * 256], BF16)
    t.s_dA = nc.dram_tensor("s_dA", [NIC, 128, E * R], BF16)
    t.s_dB = nc.dram_tensor("s_dB", [E, R, D], BF16)
    if debug:
        t.xB = nc.dram_tensor("xB", [NTOK, D], F32, kind="ExternalOutput")
        t.xC = nc.dram_tensor("xC", [NTOK, D], F32, kind="ExternalOutput")
    else:
        t.xB = nc.dram_tensor("xB", [NTOK, D], F32)
        t.xC = nc.dram_tensor("xC", [NTOK, D], F32)
    t.qk_in = nc.dram_tensor("qk_in", [2 * D, NTOK], BF16)
    t.qk_out = nc.dram_tensor("qk_out", [8 * D, NTOK], BF16)
    t.v_in = nc.dram_tensor("v_in", [128, NH * NTOK], BF16)
    t.v_out = nc.dram_tensor("v_out", [512, NH * NTOK], BF16)
    t.o_in = nc.dram_tensor("o_in", [512, 8192], BF16)
    t.o_out = nc.dram_tensor("o_out", [2048, 8192], BF16)
    t.OUT = nc.dram_tensor("OUT", [NTOK, D], F32, kind="ExternalOutput")
    if debug:
        t.DBG = nc.dram_tensor("DBG", [128, 65536], BF16, kind="ExternalOutput")
        t.qk_dbg = nc.dram_tensor("qk_dbg", [2 * D, NTOK], BF16, kind="ExternalOutput")
        t.o_dbg = nc.dram_tensor("o_dbg", [512, 8192], BF16, kind="ExternalOutput")
    t.dbgoff = [0]

    kb = KB()

    with ExitStack() as st:
        en = st.enter_context

        def sb(name, shape, dt):
            setattr(t, name, en(nc.sbuf_tensor(name, shape, dt)))

        for n, sh, dt in [
            ("identbf", [128, 128], BF16),
            ("ones128bf", [128, 128], BF16), ("ones1bf", [1, 128], BF16),
            ("iota8p8", [128, E], F32), ("sel", [E, E * 128], BF16),
            ("ropeR", [128, 128], BF16), ("masks", [128, 4 * TT], BF16),
            ("cosb", [128, NTOK], BF16), ("sinb", [128, NTOK], BF16),
            ("confWT", [128, NKC * E], BF16), ("confb", [128, E], F32),
            ("fconst", [128, 2], F32),
            ("wealth", [128, E], F32), ("offt", [1, 256], I32),
        ]:
            sb(n, sh, dt)
        sb("ARB", [128, 67584], BF16)
        sb("fencet", [1, 8], F32)
        sb("xt", [128, 4 * D], F32)
        sb("ybf", [128, D], BF16)
        sb("sq", [128, TT], F32)
        sb("sf32", [128, 2 * TT], F32)
        sb("hf32", [128, TT], F32)
        sb("rlbl", [1, TT], BF16)
        sb("stats", [128, 64], F32)
        sb("rt", [128, 96], F32)
        sb("bids", [E, TT], F32)
        sb("lrow", [1, 2 * TT], F32)
        sb("rl", [128, TT], F32)
        sb("webf8", [128, E], BF16)
        sb("rlbf", [1, TT], BF16)
        t.pb = [en(nc.psum_tensor(f"pb{i}", [128, TT], F32)) for i in range(7)]
        t.pbh = en(nc.psum_tensor("pbh", [128, 1024], BF16))

        sems_eng = {e: en(nc.semaphore(f"se_{e}")) for e in ENGINES}
        t.roff = en(nc.gpsimd.register("roff"))

        _build_plan(kb, t, debug, phases)

        slots = sorted(kb.slot_count)
        sems_slot = {s: en(nc.semaphore(f"ss_{s}")) for s in slots}
        block = en(nc.Block())
        kb.emit(nc, block, sems_eng, sems_slot)

    return nc


A_YT = 0
A_H = 8192
A_WG = 36864
A_WU = 40960
A_GB = 45056
A_UB = 47104
A_DA = 49152
A_TG = 50176
A_TU = 54272
A_WEB = 58368
A_WER = 62464
A_BG = 62976
A_BU = 63488
A_SB = 64000
A_HB = 64512
A_WEH = 65024


def WEHS(e):
    return A_SB + (e % 3) * TT
A_GAS = 65536
A_UAS = 66560
A_DN = 65536
ARBW = 67584
def C_DBLOC(e):
    return A_WG + e * D if e < 6 else A_WEB + (e - 6) * D
A_Y2T = 0
A_KS = 0
A_VS = 16384
A_QS = 32768
A_OST = 49152
A_PT = 49664
A_QP = 50688
A_RT1 = 51712
A_VSB = 52736
A_WQ = 53760
A_OT = 0
A_WO = 32768


def _build_plan(kb, t, debug, phases):
    _p0(kb, t)
    if TRUNC.get("touch", True):
        def finit(h):
            return h.eng.memset(t.fencet[:], 0.0)
        kb.op("gpsimd", finit, reads=(), writes=[rs("fencet")])
        outs = [t.OUT] + ([t.xB, t.xC] if debug else [])
        def ftouch(h, outs=outs):
            last = None
            for o in outs:
                last = h.dma(o[0:1, 0:8], t.fencet[0:1, 0:8])
            return last
        kb.dma("gpsimd", "touch", ftouch, len(outs),
               reads=[rs("fencet")], writes=[])
    last = phases[-1]
    if "ffn1" in phases:
        kb.scope = "ffn1"
        dst = t.OUT if (last == "ffn1" and not debug) else t.xB
        _ffn(kb, t, t.x0, dst, tag="f1")
    if "attn" in phases:
        kb.scope = "attn"
        dst = t.OUT if (last == "attn" and not debug) else t.xC
        _attn(kb, t, t.xB, dst)
    if "ffn2" in phases:
        kb.scope = "ffn2"
        _ffn(kb, t, t.xC, t.OUT, tag="f2")
    if TRUNC.get("fin", True):
        def fin(h):
            return h.eng.engine_nop()
        reads = [rs("OUT")]
        if debug:
            reads += [rs("xB"), rs("xC")]
        kb.op("vector", fin, reads=reads)


def _p0(kb, t):
    kb.scope = None if TRUNC.get("noscope") else "p0"

    pairs = [
        (t.identbf, t.c_identbf),
        (t.ones128bf, t.c_ones128bf), (t.ones1bf, t.c_ones1bf),
        (t.iota8p8, t.c_iota8p8), (t.sel, t.c_sel), (t.ropeR, t.c_ropeR),
        (t.masks, t.c_masks), (t.cosb, t.ropec), (t.sinb, t.ropes),
        (t.confb, t.p_confb), (t.wealth, t.p_wealth), (t.offt, t.offtab),
        (t.fconst, t.c_fconst),
    ]
    for dst, src in (pairs if TRUNC.get("p0const", True) else []):
        def f(h, dst=dst, src=src):
            return h.dma(dst[:], src[:])
        kb.dma("sync", "const", f, 1, writes=[rs(dst.name)])
    for kc in range(NKC if TRUNC.get("p0const", True) else 0):
        def f(h, kc=kc):
            return h.dma(t.confWT[:, kc * E:(kc + 1) * E], t.p_confWT[kc])
        kb.dma("gpsimd", "const2", f, 1, writes=[rs("confWT", kc * E, (kc + 1) * E)])

    cast_groups = [
        [(t.s_tA, t.p_tA), (t.s_lq, t.p_lq), (t.s_dA, t.p_dA),
         (t.s_gate, t.p_gate), (t.s_up, t.p_up)],
        [(t.s_down, t.p_down), (t.s_dB, t.p_dB)],
        [(t.s_qkv, t.p_qkv), (t.s_wo, t.p_wo)],
    ]

    def chunks_of(dst, src):
        shape = [int(x) for x in src.shape]
        cols = shape[-1]
        rows = 1
        for s in shape[:-1]:
            rows *= s
        out = []
        o = 0
        while o < rows:
            rchunk = min(512, rows - o)
            out.append((dst, src, o, rchunk, cols))
            o += rchunk
        return out

    for gi, casts in enumerate(cast_groups):
        todo = []
        if TRUNC.get("p0cast", True):
            if gi == 0:
                for dst, src in casts[:3]:
                    todo += chunks_of(dst, src)
                ga = chunks_of(t.s_gate, t.p_gate)
                up = chunks_of(t.s_up, t.p_up)
                for a, b in zip(ga, up):
                    todo += [a, b]
            else:
                for dst, src in casts:
                    todo += chunks_of(dst, src)
        for dst, src, o, rchunk, cols in todo:
            def f(h, dst=dst, src=src, o=o, rchunk=rchunk, cols=cols):
                return h.dma(
                    bass.AP(dst, o * cols, [[cols, rchunk], [1, cols]]),
                    bass.AP(src, o * cols, [[cols, rchunk], [1, cols]]))
            kb.dma("gpsimd", "wcast", f, 1,
                   writes=[rs(dst.name, o * cols, (o + rchunk) * cols)])



def _ln_tile(kb, t, x_src, tt, tc, tag, out_bf16_direct):
    slot = tc
    row0 = tt * TT + tc * 128

    def fld(h):
        return h.dma(t.xt[:, slot * D:(slot + 1) * D], x_src[row0:row0 + 128, :])
    kb.dma("scalar", f"xld{tag}{slot}", fld, 1,
           reads=[rs(x_src.name, row0 * D, (row0 + 128) * D)],
           writes=[rs("xt", slot * D * 4, (slot + 1) * D * 4)])

    xr = [rs("xt", slot * D * 4, (slot + 1) * D * 4)]

    def fsum(h):
        return h.eng.tensor_reduce(
            t.stats[:, tc:tc + 1], t.xt[:, slot * D:(slot + 1) * D], axis=AX,
            op=AluOpType.add)
    kb.op("vector", fsum, reads=xr, writes=[rs("stats", tc * 4, tc * 4 + 4)])

    for j in range(4):
        def fsq(h, j=j):
            return h.eng.activation(
                t.sq[:], t.xt[:, slot * D + j * TT:slot * D + (j + 1) * TT],
                AF.Square, accum_out=t.stats[:, 32 + tc * 4 + j:33 + tc * 4 + j])
        kb.op("scalar", fsq, reads=xr,
              writes=[rs("sq"), rs("stats", (32 + tc * 4 + j) * 4,
                                   (33 + tc * 4 + j) * 4)])

    def fsqs(h):
        return h.eng.tensor_reduce(
            t.stats[:, 4 + tc:5 + tc], t.stats[:, 32 + tc * 4:32 + tc * 4 + 4],
            axis=AX, op=AluOpType.add)
    kb.op("vector", fsqs,
          reads=[rs("stats", (32 + tc * 4) * 4, (36 + tc * 4) * 4)],
          writes=[rs("stats", (4 + tc) * 4, (5 + tc) * 4)])

    def fm(h):
        return h.eng.tensor_scalar(
            t.stats[:, 8 + tc:9 + tc], t.stats[:, tc:tc + 1], 1.0 / D, None,
            AluOpType.mult)
    kb.op("vector", fm, reads=[rs("stats", tc * 4, tc * 4 + 4)],
          writes=[rs("stats", (8 + tc) * 4, (9 + tc) * 4)])

    def fq2(h):
        return h.eng.tensor_scalar(
            t.stats[:, 12 + tc:13 + tc], t.stats[:, 4 + tc:5 + tc], 1.0 / D, None,
            AluOpType.mult)
    kb.op("vector", fq2, reads=[rs("stats", (4 + tc) * 4, (5 + tc) * 4)],
          writes=[rs("stats", (12 + tc) * 4, (13 + tc) * 4)])

    def fvar(h):
        return h.eng.scalar_tensor_tensor(
            t.stats[:, 16 + tc:17 + tc], t.stats[:, 8 + tc:9 + tc],
            t.stats[:, 8 + tc:9 + tc], t.stats[:, 12 + tc:13 + tc],
            AluOpType.mult, AluOpType.subtract)
    kb.op("vector", fvar,
          reads=[rs("stats", (8 + tc) * 4, (9 + tc) * 4),
                 rs("stats", (12 + tc) * 4, (13 + tc) * 4)],
          writes=[rs("stats", (16 + tc) * 4, (17 + tc) * 4)])
    return slot


def _ln_finish(kb, t):
    def fsq(h):
        return h.eng.activation(t.stats[:, 20:24], t.stats[:, 16:20], AF.Sqrt,
                                bias=t.fconst[:, 0:1], scale=-1.0)
    kb.op("scalar", fsq, reads=[rs("stats", 64, 80), rs("fconst")],
          writes=[rs("stats", 80, 96)])

    def fr(h):
        return h.eng.reciprocal(t.stats[:, 24:28], t.stats[:, 20:24])
    kb.op("vector", fr, reads=[rs("stats", 80, 96)], writes=[rs("stats", 96, 112)])


def _ln_norm(kb, t, tc, slot):
    def fn(h):
        return h.eng.tensor_scalar(
            t.ybf[:], t.xt[:, slot * D:(slot + 1) * D],
            t.stats[:, 8 + tc:9 + tc], t.stats[:, 24 + tc:25 + tc],
            AluOpType.subtract, AluOpType.mult)
    kb.op("vector", fn,
          reads=[rs("xt", slot * D * 4, (slot + 1) * D * 4),
                 rs("stats", (8 + tc) * 4, (9 + tc) * 4),
                 rs("stats", (24 + tc) * 4, (25 + tc) * 4)],
          writes=[rs("ybf")])


def _stt_res_add(kb, t, psum, psum_key, xcol0, fg):
    c0 = xcol0 + fg * TT

    def fn(h):
        return h.eng.tensor_tensor(
            t.xt[:, c0:c0 + TT], psum[:], t.xt[:, c0:c0 + TT], AluOpType.add)
    kb.op("vector", fn,
          reads=[rs(psum_key), rs("xt", c0 * 4, (c0 + TT) * 4)],
          writes=[rs("xt", c0 * 4, (c0 + TT) * 4)])


def _ffn(kb, t, x_src, x_dst, tag):
    pb = t.pb
    arb = t.ARB

    def AB(c0, n=TT):
        return arb[:, c0:c0 + n], rs("ARB", c0 * 2, (c0 + n) * 2)

    for tt in range(min(NTT, TRUNC["ffn_tt"])):
        sc0 = kb.scope
        for tc in range(4):
            _ln_tile(kb, t, x_src, tt, tc, tag, False)
        _ln_finish(kb, t)
        AL = TRUNC.get("a_parts", 4)
        for tc in range(4):
            slot = tc
            _ln_norm(kb, t, tc, slot)
            zb = 5 + tc % 2
            for grp in range(2 if AL >= 2 else 0):
                for g8 in range(8):
                    fc = grp * 8 + g8

                    def ftr(h, fc=fc, g8=g8):
                        return h.eng.transpose(
                            t.pbh[:, g8 * 128:(g8 + 1) * 128],
                            t.ybf[:, fc * 128:(fc + 1) * 128], t.identbf[:])
                    kb.op("tensor", ftr, reads=[rs("ybf"), rs("identbf")],
                          writes=[rs("pbh", g8, g8 + 1)])

                yg0 = grp * 8 * TT + tc * 128

                def fe2(h, yg0=yg0):
                    return h.eng.copy(
                        bass.AP(arb, yg0, [[ARBW, 128], [TT, 8], [1, 128]]),
                        t.pbh[:].rearrange("p (a b) -> p a b", a=8))
                kb.op("scalar", fe2, reads=[rs("pbh", 0, 8)],
                      writes=[rs("ARB", yg0 * 2, (yg0 + 8 * TT) * 2)])

                if TRUNC.get("a_sub", 4) >= 4:
                    for g8 in range(8):
                        fc = grp * 8 + g8
                        ytc0 = fc * TT + tc * 128

                        def fz(h, fc=fc, ytc0=ytc0, zb=zb):
                            return h.eng.matmul(
                                pb[zb][:, 0:E],
                                arb[:, ytc0:ytc0 + 128],
                                t.confWT[:, fc * E:(fc + 1) * E],
                                start=(fc == 0), stop=(fc == NKC - 1))
                        kb.op("tensor", fz,
                              reads=[rs("confWT", fc * E, (fc + 1) * E),
                                     rs("ARB", ytc0 * 2, (ytc0 + 128) * 2)],
                              writes=[rs(f"pb{zb}", 0, 1)])

            if AL < 3:
                continue
            c = 32 + tc * 8
            bidsc = tc * 8
            selc = 64 + (tc % 2) * 16
            bids = t.rt[:, bidsc:bidsc + 8]
            m1 = t.rt[:, c:c + 1]; i1 = t.rt[:, c + 1:c + 2]
            m2 = t.rt[:, c + 2:c + 3]; i2 = t.rt[:, c + 3:c + 4]
            d12 = t.rt[:, c + 4:c + 6]; r12 = t.rt[:, c + 6:c + 8]
            sel1 = t.rt[:, selc:selc + 8]
            bids2 = t.rt[:, selc + 8:selc + 16]
            bidr = rs("rt", bidsc * 4, (bidsc + 8) * 4)
            rtr = rs("rt", c * 4, (c + 8) * 4)
            rtr2 = rs("rt", selc * 4, (selc + 16) * 4)

            def step(fn, reads, writes, eng="vector"):
                kb.op(eng, fn, reads=reads, writes=writes)

            step(lambda h, bids=bids, zb=zb: h.eng.tensor_tensor(
                bids, pb[zb][:, 0:E], t.confb[:], AluOpType.add),
                 [rs(f"pb{zb}", 0, 1), rs("confb")], [bidr])
            step(lambda h, bids=bids: h.eng.activation(bids, bids, AF.Sigmoid),
                 [bidr], [bidr], eng="scalar")
            step(lambda h, bids=bids: h.eng.tensor_tensor(
                bids, bids, t.wealth[:], AluOpType.mult),
                 [bidr, rs("wealth")], [bidr])
            step(lambda h, bids=bids, m1=m1: h.eng.tensor_reduce(
                m1, bids, axis=AX, op=AluOpType.max), [bidr], [rtr])
            step(lambda h, bids=bids, m1=m1, sel1=sel1: h.eng.tensor_scalar(
                sel1, bids, m1, None, AluOpType.is_equal), [bidr, rtr], [rtr2])
            step(lambda h, sel1=sel1, bids2=bids2: h.eng.scalar_tensor_tensor(
                bids2, sel1, -8.0, t.iota8p8[:], AluOpType.mult, AluOpType.add),
                 [rtr2, rs("iota8p8")], [rtr2])
            step(lambda h, bids2=bids2, i1=i1: h.eng.tensor_reduce(
                i1, bids2, axis=AX, op=AluOpType.min), [rtr2], [rtr])
            step(lambda h, i1=i1, sel1=sel1: h.eng.tensor_scalar(
                sel1, t.iota8p8[:], 8.0, i1, AluOpType.subtract, AluOpType.is_equal),
                 [rs("iota8p8"), rtr], [rtr2])
            step(lambda h, sel1=sel1, bids=bids, bids2=bids2: h.eng.scalar_tensor_tensor(
                bids2, sel1, BIGNEG, bids, AluOpType.mult, AluOpType.add),
                 [rtr2, bidr], [rtr2])
            step(lambda h, bids2=bids2, m2=m2: h.eng.tensor_reduce(
                m2, bids2, axis=AX, op=AluOpType.max), [rtr2], [rtr])
            step(lambda h, bids2=bids2, m2=m2: h.eng.tensor_scalar(
                bids2, bids2, m2, None, AluOpType.is_equal), [rtr2, rtr], [rtr2])
            step(lambda h, bids2=bids2: h.eng.scalar_tensor_tensor(
                bids2, bids2, -8.0, t.iota8p8[:], AluOpType.mult, AluOpType.add),
                 [rtr2, rs("iota8p8")], [rtr2])
            step(lambda h, bids2=bids2, i2=i2: h.eng.tensor_reduce(
                i2, bids2, axis=AX, op=AluOpType.min), [rtr2], [rtr])
            step(lambda h, i2=i2, bids2=bids2: h.eng.tensor_scalar(
                bids2, t.iota8p8[:], 8.0, i2, AluOpType.subtract, AluOpType.is_equal),
                 [rs("iota8p8"), rtr], [rtr2])
            step(lambda h, m1=m1, m2=m2, d12=d12: h.eng.tensor_tensor(
                d12[:, 0:1], m1, m2, AluOpType.subtract), [rtr], [rtr])
            step(lambda h, d12=d12: h.eng.tensor_scalar(
                d12[:, 1:2], d12[:, 0:1], -1.0, None, AluOpType.mult), [rtr], [rtr])
            step(lambda h, d12=d12, r12=r12: h.eng.activation(r12, d12, AF.Sigmoid),
                 [rtr], [rtr], eng="scalar")
            step(lambda h, r12=r12: h.eng.tensor_scalar(
                r12, r12, 0.5, None, AluOpType.mult), [rtr], [rtr])
            step(lambda h, sel1=sel1, r12=r12: h.eng.tensor_scalar(
                sel1, sel1, r12[:, 0:1], None, AluOpType.mult), [rtr2, rtr], [rtr2])
            step(lambda h, bids2=bids2, r12=r12, sel1=sel1: h.eng.scalar_tensor_tensor(
                sel1, bids2, r12[:, 1:2], sel1, AluOpType.mult, AluOpType.add),
                 [rtr2, rtr], [rtr2])
            step(lambda h, sel1=sel1: h.eng.tensor_copy(t.webf8[:], sel1),
                 [rtr2], [rs("webf8")])

            def fwt(h, tc=tc):
                return h.eng.transpose(
                    t.pbh[0:E, 0:128], t.webf8[:], t.identbf[:])
            kb.op("tensor", fwt, reads=[rs("webf8"), rs("identbf")],
                  writes=[rs("pbh", 0, 1)])

            def fwe(h, tc=tc):
                return h.eng.copy(arb[0:E, A_WER + tc * 128:A_WER + (tc + 1) * 128],
                                  t.pbh[0:E, 0:128])
            kb.op("scalar", fwe, reads=[rs("pbh", 0, 1)],
                  writes=[rs("ARB", (A_WER + tc * 128) * 2,
                             (A_WER + (tc + 1) * 128) * 2)])

        if AL < 3:
            continue
        for e in range(E):
            def fbc(h, e=e):
                return h.eng.matmul(
                    pb[6][:], t.sel[:, e * 128:(e + 1) * 128],
                    arb[0:E, A_WER:A_WER + TT], start=True, stop=True)
            kb.op("tensor", fbc,
                  reads=[rs("sel"), rs("ARB", A_WER * 2, (A_WER + TT) * 2)],
                  writes=[rs("pb6")])

            wc0 = A_WEB + e * TT

            def fbe(h, wc0=wc0):
                return h.eng.copy(arb[:, wc0:wc0 + TT], pb[6][:])
            kb.op("scalar", fbe, reads=[rs("pb6")],
                  writes=[rs("ARB", wc0 * 2, (wc0 + TT) * 2)])

        if AL < 4:
            continue
        for wave in range(2):
            for kc in range(NKC):
                buf = kc % 2
                sc = A_GAS + buf * 1024

                def fld(h, kc=kc, sc=sc):
                    return h.dma(arb[:, sc:sc + 1024], t.s_tA[kc])
                kb.dma("sync", f"tI{buf}", fld, 1,
                       reads=[rs(t.s_tA.name, kc * 128 * 1024,
                                 (kc + 1) * 128 * 1024)],
                       writes=[rs("ARB", sc * 2, (sc + 1024) * 2)])
                for e in range(wave * 4, wave * 4 + 4):
                    bank = e % 4

                    def fmm(h, kc=kc, e=e, sc=sc, bank=bank):
                        return h.eng.matmul(
                            pb[bank][:],
                            arb[:, sc + e * 128:sc + (e + 1) * 128],
                            arb[:, kc * TT:(kc + 1) * TT],
                            start=(kc == 0), stop=(kc == NKC - 1))
                    kb.op("tensor", fmm,
                          reads=[rs("ARB", sc * 2, (sc + 1024) * 2),
                                 rs("ARB", kc * TT * 2, (kc + 1) * TT * 2)],
                          writes=[rs(f"pb{bank}")])
            for e in range(wave * 4, wave * 4 + 4):
                bank = e % 4
                dc0 = A_TG + e * TT

                def fev(h, bank=bank, dc0=dc0):
                    return h.eng.mul(arb[:, dc0:dc0 + TT], pb[bank][:], SCALING)
                kb.op("scalar", fev, reads=[rs(f"pb{bank}")],
                      writes=[rs("ARB", dc0 * 2, (dc0 + TT) * 2)])

        if TRUNC.get("dumpffn"):
            dump_arb(kb, t, A_YT, 8192, f"yT_tt{tt}")
            dump_t(kb, t, t.stats, "stats", 28, f"stats_tt{tt}")
            dump_arb(kb, t, A_WER, 512, f"weR_tt{tt}", rows=8)
            dump_arb(kb, t, A_WEB, 4096, f"web_tt{tt}")
            dump_arb(kb, t, A_TG, 4096, f"tg_tt{tt}", rows=128)
        for ic in range(NIC if "B" in TRUNC["parts"] else 0):
            buf = ic % 2
            wg0 = A_WG + buf * D
            wu0 = A_WU + buf * D
            gb0 = A_GB + buf * 1024
            ub0 = A_UB + buf * 1024
            da0 = A_DA + buf * TT

            lpq = A_TU + buf * 2048

            def fldw(h, ic=ic, wg0=wg0, wu0=wu0, lpq=lpq, da0=da0):
                h.dma(arb[:, wg0:wg0 + D], t.s_gate[ic])
                h.dma(arb[:, wu0:wu0 + D], t.s_up[ic])
                h.dma(arb[:, lpq:lpq + 2048], t.s_lq[ic])
                return h.dma(arb[:, da0:da0 + TT], t.s_dA[ic])
            kb.dma("sync", f"wB{buf}", fldw, 4,
                   reads=[rs(t.s_gate.name, ic * 128 * D, (ic + 1) * 128 * D),
                          rs(t.s_up.name, ic * 128 * D, (ic + 1) * 128 * D),
                          rs(t.s_lq.name, ic * 128 * 2048,
                             (ic + 1) * 128 * 2048),
                          rs(t.s_dA.name, ic * 128 * TT, (ic + 1) * 128 * TT)],
                   writes=[rs("ARB", wg0 * 2, (wg0 + D) * 2),
                           rs("ARB", wu0 * 2, (wu0 + D) * 2),
                           rs("ARB", lpq * 2, (lpq + 2048) * 2),
                           rs("ARB", da0 * 2, (da0 + TT) * 2)])

            for bank, w0 in ((0, wg0), (1, wu0)):
                for kc in range(NKC):
                    def fmm(h, bank=bank, w0=w0, kc=kc):
                        return h.eng.matmul(
                            pb[bank][:], arb[:, w0 + kc * 128:w0 + (kc + 1) * 128],
                            arb[:, kc * TT:(kc + 1) * TT],
                            start=(kc == 0), stop=(kc == NKC - 1))
                    kb.op("tensor", fmm,
                          reads=[rs("ARB", w0 * 2, (w0 + D) * 2),
                                 rs("ARB", kc * TT * 2, (kc + 1) * TT * 2)],
                          writes=[rs(f"pb{bank}")])
            for bank, b0 in ((0, A_BG), (1, A_BU)):
                def fev(h, bank=bank, b0=b0):
                    return h.eng.copy(arb[:, b0:b0 + TT], pb[bank][:])
                kb.op("scalar", fev, reads=[rs(f"pb{bank}")],
                      writes=[rs("ARB", b0 * 2, (b0 + TT) * 2)])

            def emit_hatd(e, ic=ic, da0=da0):
                wc = WEHS(e)

                def fha(h, e=e, wc=wc):
                    return h.eng.matmul(pb[2][:], t.identbf[:],
                                        arb[:, wc:wc + TT],
                                        start=(e == 0), stop=(e == E - 1))
                kb.op("tensor", fha,
                      reads=[rs("identbf"), rs("ARB", wc * 2, (wc + TT) * 2)],
                      writes=[rs("pb2")])

                bank = 3 + e // 2
                p0 = (e % 2) * 64

                def ftd(h, e=e, ic=ic, da0=da0, bank=bank, p0=p0, wc=wc):
                    return h.eng.matmul(
                        pb[bank][p0:p0 + 64, :],
                        arb[:, da0 + e * R:da0 + (e + 1) * R],
                        arb[:, wc:wc + TT],
                        start=(ic == 0), stop=(ic == NIC - 1),
                        skip_group_check=True)
                kb.op("tensor", ftd,
                      reads=[rs("ARB", (da0 + e * R) * 2, (da0 + (e + 1) * R) * 2),
                             rs("ARB", wc * 2, (wc + TT) * 2)],
                      writes=[rs(f"pb{bank}", p0, p0 + 64)])

            for e in range(E):
                sfc = (e % 2) * TT
                qa = lpq + e * 256
                def fdg(h, e=e, qa=qa):
                    return h.eng.matmul(
                        pb[0][:], arb[:, qa:qa + 128],
                        arb[:, A_TG + e * TT:A_TG + (e + 1) * TT],
                        start=True, stop=False)
                kb.op("tensor", fdg,
                      reads=[rs("ARB", qa * 2, (qa + 128) * 2),
                             rs("ARB", (A_TG + e * TT) * 2, (A_TG + (e + 1) * TT) * 2)],
                      writes=[rs("pb0")])

                def fig(h):
                    return h.eng.matmul(pb[0][:], t.identbf[:],
                                        arb[:, A_BG:A_BG + TT],
                                        start=False, stop=True)
                kb.op("tensor", fig,
                      reads=[rs("identbf"), rs("ARB", A_BG * 2, (A_BG + TT) * 2)],
                      writes=[rs("pb0")])

                if e == 0 and ic == 0 and tt == 0 and TRUNC.get("dumpg"):
                    def fgd(h):
                        return h.eng.copy(t.sq[:], pb[0][:])
                    kb.op("scalar", fgd, reads=[rs("pb0")], writes=[rs("sq")])
                    dump_t(kb, t, t.sq, "sq", TT, "g_e0")

                def fsl(h, sfc=sfc):
                    return h.eng.activation(t.sf32[:, sfc:sfc + TT], pb[0][:],
                                            AF.Silu)
                kb.op("scalar", fsl, reads=[rs("pb0")],
                      writes=[rs("sf32", sfc, sfc + TT)])

                def fdu(h, e=e, qa=qa):
                    return h.eng.matmul(
                        pb[1][:], arb[:, qa + 128:qa + 256],
                        arb[:, A_TG + e * TT:A_TG + (e + 1) * TT],
                        start=True, stop=False)
                kb.op("tensor", fdu,
                      reads=[rs("ARB", (qa + 128) * 2, (qa + 256) * 2),
                             rs("ARB", (A_TG + e * TT) * 2, (A_TG + (e + 1) * TT) * 2)],
                      writes=[rs("pb1")])

                def fiu(h):
                    return h.eng.matmul(pb[1][:], t.identbf[:],
                                        arb[:, A_BU:A_BU + TT],
                                        start=False, stop=True)
                kb.op("tensor", fiu,
                      reads=[rs("identbf"), rs("ARB", A_BU * 2, (A_BU + TT) * 2)],
                      writes=[rs("pb1")])

                if e > 1:
                    emit_hatd(e - 2)

                def fh(h, sfc=sfc):
                    return h.eng.tensor_tensor(
                        t.hf32[:], t.sf32[:, sfc:sfc + TT], pb[1][:],
                        AluOpType.mult)
                kb.op("vector", fh, reads=[rs("sf32", sfc, sfc + TT), rs("pb1")],
                      writes=[rs("hf32")])

                def fweh(h, e=e):
                    wc = WEHS(e)
                    return h.eng.tensor_tensor(
                        arb[:, wc:wc + TT], t.hf32[:],
                        arb[:, A_WEB + e * TT:A_WEB + (e + 1) * TT], AluOpType.mult)
                kb.op("vector", fweh,
                      reads=[rs("hf32"),
                             rs("ARB", (A_WEB + e * TT) * 2, (A_WEB + (e + 1) * TT) * 2)],
                      writes=[rs("ARB", WEHS(e) * 2, (WEHS(e) + TT) * 2)])

                if e == 0 and ic == 0 and tt == 0 and TRUNC.get("dumpg"):
                    def fud(h):
                        return h.eng.copy(t.sq[:], pb[1][:])
                    kb.op("scalar", fud, reads=[rs("pb1")], writes=[rs("sq")])
                    dump_t(kb, t, t.sq, "sq", TT, "u_e0")
                    dump_arb(kb, t, WEHS(0), TT, "weh_e0")
                    dump_arb(kb, t, A_BG, TT, "bg_ic0")
                    dump_arb(kb, t, A_BU, TT, "bu_ic0")
            emit_hatd(E - 2)
            emit_hatd(E - 1)

            def fhe(h, ic=ic):
                return h.eng.copy(arb[:, A_H + ic * TT:A_H + (ic + 1) * TT], pb[2][:])
            kb.op("scalar", fhe, reads=[rs("pb2")],
                  writes=[rs("ARB", (A_H + ic * TT) * 2, (A_H + (ic + 1) * TT) * 2)])

        if TRUNC.get("dumpffn"):
            dump_arb(kb, t, A_H, NIC * TT, f"H_tt{tt}")
        if "C" not in TRUNC["parts"]:
            continue
        for e in range(E):
            bank = 3 + e // 2
            p0 = (e % 2) * 64
            dc0 = A_TG + e * TT

            def fte(h, bank=bank, p0=p0, dc0=dc0):
                return h.eng.mul(arb[0:64, dc0:dc0 + TT], pb[bank][p0:p0 + 64, :],
                                 SCALING)
            kb.op("scalar", fte, reads=[rs(f"pb{bank}", p0, p0 + 64)],
                  writes=[rs("ARB", dc0 * 2, (dc0 + TT) * 2)])
        if TRUNC.get("dumpffn"):
            dump_arb(kb, t, A_TG, 4096, f"td_tt{tt}", rows=64)
        def fdb(h):
            last = None
            for e in range(E):
                c0 = C_DBLOC(e)
                last = h.dma(arb[0:64, c0:c0 + D], t.s_dB[e])
            return last
        kb.dma("sync", "dB", fdb, E, reads=[rs(t.s_dB.name)],
               writes=[rs("ARB", A_WG * 2, (A_WG + 6 * D) * 2),
                       rs("ARB", A_WEB * 2, (A_WEB + 2 * D) * 2)])
        def fxl(h, tt=tt):
            last = None
            for tc in range(4):
                row0 = tt * TT + tc * 128
                last = h.dma(t.xt[:, tc * D:(tc + 1) * D],
                             x_src[row0:row0 + 128, :])
            return last
        kb.dma("scalar", f"xr{tag}", fxl, 4,
               reads=[rs(x_src.name, tt * TT * D, (tt + 1) * TT * D)],
               writes=[rs("xt", 0, 4 * D * 4)])

        for fg in range(4):
            for ic in range(NIC):
                buf = ic % 4
                dn0 = A_DN + buf * TT

                def fdl(h, ic=ic, fg=fg, dn0=dn0):
                    return h.dma(arb[:, dn0:dn0 + TT],
                                 t.s_down[ic * 128:(ic + 1) * 128,
                                          fg * TT:(fg + 1) * TT])
                kb.dma("sync", f"dn{buf}", fdl, 1,
                       reads=[rs(t.s_down.name, ic * 128 * D,
                                 (ic + 1) * 128 * D)],
                       writes=[rs("ARB", dn0 * 2, (dn0 + TT) * 2)])
                for tc in range(4):
                    def fmm(h, ic=ic, tc=tc, dn0=dn0):
                        return h.eng.matmul(
                            pb[tc][:],
                            arb[:, A_H + ic * TT + tc * 128:A_H + ic * TT + (tc + 1) * 128],
                            arb[:, dn0:dn0 + TT],
                            start=(ic == 0), stop=False)
                    kb.op("tensor", fmm,
                          reads=[rs("ARB", (A_H + ic * TT) * 2, (A_H + (ic + 1) * TT) * 2),
                                 rs("ARB", dn0 * 2, (dn0 + TT) * 2)],
                          writes=[rs(f"pb{tc}")])
            for e in range(E):
                tdc = A_TG + e * TT
                dbc = C_DBLOC(e) + fg * TT
                for tc in range(4):
                    def fmm(h, e=e, tc=tc, tdc=tdc, dbc=dbc):
                        return h.eng.matmul(
                            pb[tc][:],
                            arb[0:64, tdc + tc * 128:tdc + (tc + 1) * 128],
                            arb[0:64, dbc:dbc + TT],
                            start=False, stop=(e == E - 1))
                    kb.op("tensor", fmm,
                          reads=[rs("ARB", tdc * 2, (tdc + TT) * 2),
                                 rs("ARB", dbc * 2, (dbc + TT) * 2)],
                          writes=[rs(f"pb{tc}")])
            for tc in range(4):
                _stt_res_add(kb, t, pb[tc], f"pb{tc}", tc * D, fg)
        for tc in range(4):
            row0 = tt * TT + tc * 128

            def fxs(h, tc=tc, row0=row0):
                return h.dma(x_dst[row0:row0 + 128, :], t.xt[:, tc * D:(tc + 1) * D])
            kb.dma("gpsimd", f"xs{tag}", fxs, 1,
                   reads=[rs("xt", tc * D * 4, (tc + 1) * D * 4)],
                   writes=[rs(x_dst.name, row0 * D, (row0 + 128) * D)])

    def ffence(h):
        return h.eng.memset(t.fencet[:], 0.0)
    kb.fence("gpsimd", ffence, [rs(x_dst.name)])


def _attn(kb, t, x_src, x_dst):
    pb = t.pb
    arb = t.ARB

    for tt in range(NTT):
        for tc in range(4):
            _ln_tile(kb, t, x_src, tt, tc, "a", True)
        _ln_finish(kb, t)
        for tc in range(4):
            slot = tc
            _ln_norm(kb, t, tc, slot)
            for grp in range(2):
                for g8 in range(8):
                    fc = grp * 8 + g8

                    def ftr(h, fc=fc, g8=g8):
                        return h.eng.transpose(
                            t.pbh[:, g8 * 128:(g8 + 1) * 128],
                            t.ybf[:, fc * 128:(fc + 1) * 128], t.identbf[:])
                    kb.op("tensor", ftr, reads=[rs("ybf"), rs("identbf")],
                          writes=[rs("pbh", g8, g8 + 1)])

                yg0 = A_Y2T + grp * 8 * TT + tc * 128

                def fe2(h, yg0=yg0):
                    return h.eng.copy(
                        bass.AP(arb, yg0, [[ARBW, 128], [TT, 8], [1, 128]]),
                        t.pbh[:].rearrange("p (a b) -> p a b", a=8))
                kb.op("scalar", fe2, reads=[rs("pbh", 0, 8)],
                      writes=[rs("ARB", yg0 * 2, (yg0 + 8 * TT) * 2)])

        for oc in range(NOC3):
            buf = oc % 2
            wq0 = A_WQ + buf * D
            bank = buf

            def fldw(h, oc=oc, wq0=wq0):
                return h.dma(arb[:, wq0:wq0 + D], t.s_qkv[oc])
            kb.dma("sync", f"qw{buf}", fldw, 1, reads=[rs(t.s_qkv.name)],
                   writes=[rs("ARB", wq0 * 2, (wq0 + D) * 2)])
            for kc in range(NKC):
                def fmm(h, kc=kc, wq0=wq0, bank=bank):
                    return h.eng.matmul(
                        pb[bank][:], arb[:, wq0 + kc * 128:wq0 + (kc + 1) * 128],
                        arb[:, A_Y2T + kc * TT:A_Y2T + (kc + 1) * TT],
                        start=(kc == 0), stop=(kc == NKC - 1))
                kb.op("tensor", fmm,
                      reads=[rs("ARB", wq0 * 2, (wq0 + D) * 2),
                             rs("ARB", (A_Y2T + kc * TT) * 2, (A_Y2T + (kc + 1) * TT) * 2)],
                      writes=[rs(f"pb{bank}")])

            h_ = oc % NH
            kind = oc // NH
            qp0 = A_QP + buf * TT

            def fev(h, bank=bank, qp0=qp0):
                return h.eng.copy(arb[:, qp0:qp0 + TT], pb[bank][:])
            kb.op("scalar", fev, reads=[rs(f"pb{bank}")],
                  writes=[rs("ARB", qp0 * 2, (qp0 + TT) * 2)])

            if kind < 2:
                def frot(h, qp0=qp0):
                    return h.eng.matmul(pb[2][:], t.ropeR[:],
                                        arb[:, qp0:qp0 + TT],
                                        start=True, stop=True)
                kb.op("tensor", frot,
                      reads=[rs("ropeR"), rs("ARB", qp0 * 2, (qp0 + TT) * 2)],
                      writes=[rs("pb2")])

                rt0 = A_RT1 + buf * TT

                def ft1(h, qp0=qp0, rt0=rt0, tt=tt):
                    return h.eng.tensor_tensor(
                        arb[:, rt0:rt0 + TT], arb[:, qp0:qp0 + TT],
                        t.cosb[:, tt * TT:(tt + 1) * TT], AluOpType.mult)
                kb.op("vector", ft1,
                      reads=[rs("ARB", qp0 * 2, (qp0 + TT) * 2), rs("cosb")],
                      writes=[rs("ARB", rt0 * 2, (rt0 + TT) * 2)])

                def ft2(h, qp0=qp0, tt=tt):
                    return h.eng.tensor_tensor(
                        arb[:, qp0:qp0 + TT], pb[2][:],
                        t.sinb[:, tt * TT:(tt + 1) * TT], AluOpType.mult)
                kb.op("vector", ft2, reads=[rs("pb2"), rs("sinb")],
                      writes=[rs("ARB", qp0 * 2, (qp0 + TT) * 2)])

                def ft3(h, qp0=qp0, rt0=rt0):
                    return h.eng.tensor_tensor(
                        arb[:, rt0:rt0 + TT], arb[:, rt0:rt0 + TT],
                        arb[:, qp0:qp0 + TT], AluOpType.add)
                kb.op("vector", ft3,
                      reads=[rs("ARB", rt0 * 2, (rt0 + TT) * 2),
                             rs("ARB", qp0 * 2, (qp0 + TT) * 2)],
                      writes=[rs("ARB", rt0 * 2, (rt0 + TT) * 2)])

                row0 = h_ * 128 + (0 if kind == 0 else D)

                def fst(h, rt0=rt0, row0=row0, tt=tt):
                    return h.dma(t.qk_in[row0:row0 + 128, tt * TT:(tt + 1) * TT],
                                 arb[:, rt0:rt0 + TT])
                kb.dma("gpsimd", "qkst", fst, 1,
                       reads=[rs("ARB", rt0 * 2, (rt0 + TT) * 2)],
                       writes=[rs("qk_in", (row0 * NTOK + tt * TT) * 2,
                                  ((row0 + 128) * NTOK) * 2)])
            else:
                vsb0 = A_VSB + buf * TT
                for lt in range(4):
                    def fvt(h, lt=lt, qp0=qp0):
                        return h.eng.transpose(
                            t.pbh[:, lt * 128:(lt + 1) * 128],
                            arb[:, qp0 + lt * 128:qp0 + (lt + 1) * 128],
                            t.identbf[:])
                    kb.op("tensor", fvt,
                          reads=[rs("ARB", qp0 * 2, (qp0 + TT) * 2),
                                 rs("identbf")],
                          writes=[rs("pbh", lt, lt + 1)])

                def fve(h, vsb0=vsb0):
                    return h.eng.copy(
                        arb[:, vsb0:vsb0 + TT], t.pbh[:, 0:TT])
                kb.op("scalar", fve, reads=[rs("pbh", 0, 4)],
                      writes=[rs("ARB", vsb0 * 2, (vsb0 + TT) * 2)])

                def fst(h, vsb0=vsb0, h_=h_, tt=tt):
                    return h.dma(
                        t.v_in[:, h_ * NTOK + tt * TT:h_ * NTOK + (tt + 1) * TT],
                        arb[:, vsb0:vsb0 + TT])
                kb.dma("gpsimd", "vst", fst, 1,
                       reads=[rs("ARB", vsb0 * 2, (vsb0 + TT) * 2)],
                       writes=[rs("v_in", (h_ * NTOK + tt * TT) * 2,
                                  (h_ * NTOK + (tt + 1) * TT) * 2)])

    if hasattr(t, "qk_dbg"):
        def fqkd(h):
            last = None
            for rr in range(8):
                last = h.dma(t.qk_dbg[rr * 512:(rr + 1) * 512, :],
                             t.qk_in[rr * 512:(rr + 1) * 512, :])
            return last
        kb.dma("gpsimd", "qkdbg", fqkd, 8, reads=[rs("qk_in")], writes=[rs("qk_dbg")])

    def fcc1(h):
        return h.eng.collective_compute(
            "AllGather", mybir.AluOpType.bypass,
            replica_groups=[[0, 1, 2, 3], [4, 5, 6, 7]],
            ins=[t.qk_in.ap().opt()], outs=[t.qk_out.ap().opt()])
    kb.op("gpsimd", fcc1, reads=[rs("qk_in")], writes=[rs("qk_out")])

    def fcc2(h):
        return h.eng.collective_compute(
            "AllGather", mybir.AluOpType.bypass,
            replica_groups=[[0, 1, 2, 3], [4, 5, 6, 7]],
            ins=[t.v_in.ap().opt()], outs=[t.v_out.ap().opt()])
    kb.op("gpsimd", fcc2, reads=[rs("v_in")], writes=[rs("v_out")])

    OFF_K = 0
    OFF_V = 4
    OFF_Q = 8
    OFF_O = 12

    for hh in range(4):
        kbuf = hh % 2
        ks0 = A_KS + kbuf * 8192
        vs0 = A_VS + kbuf * 8192
        qs0 = A_QS + kbuf * 8192
        def fkl(h, hh=hh, ks0=ks0):
            h.eng.reg_load(t.roff, t.offt[0:1, OFF_K + hh:OFF_K + hh + 1])
            return h.dma(
                bass.AP(arb, ks0, [[ARBW, 128], [NTOK, 4], [1, NTOK]]),
                bass.AP(t.qk_out, t.roff, [[NTOK, 128], [2 * D * NTOK, 4], [1, NTOK]]))
        kb.dma("gpsimd", f"kld{kbuf}", fkl, 1,
               reads=[rs("qk_out"), rs("offt")],
               writes=[rs("ARB", ks0 * 2, (ks0 + 4 * NTOK) * 2)])

        def fvl(h, hh=hh, vs0=vs0):
            h.eng.reg_load(t.roff, t.offt[0:1, OFF_V + hh:OFF_V + hh + 1])
            return h.dma(
                bass.AP(arb, vs0, [[ARBW, 128], [NTOK, 4], [1, NTOK]]),
                bass.AP(t.v_out, t.roff,
                        [[NH * NTOK, 128], [128 * NH * NTOK, 4], [1, NTOK]]))
        kb.dma("gpsimd", f"vld{kbuf}", fvl, 1,
               reads=[rs("v_out"), rs("offt")],
               writes=[rs("ARB", vs0 * 2, (vs0 + 4 * NTOK) * 2)])

        def fql(h, hh=hh, qs0=qs0):
            h.eng.reg_load(t.roff, t.offt[0:1, OFF_Q + hh:OFF_Q + hh + 1])
            return h.dma(
                bass.AP(arb, qs0, [[ARBW, 128], [NTOK, 4], [1, NTOK]]),
                bass.AP(t.qk_out, t.roff, [[NTOK, 128], [2 * D * NTOK, 4], [1, NTOK]]))
        kb.dma("gpsimd", f"qld{kbuf}", fql, 1, reads=[rs("qk_out"), rs("offt")],
               writes=[rs("ARB", qs0 * 2, (qs0 + 4 * NTOK) * 2)])

        for qt in range(16):
            g_q = qt // 2
            r_q, ltb_q = _chunk_owner(g_q)
            qcol = qs0 + r_q * NTOK + (0 if ltb_q == 0 else 1024) + (qt % 2) * TT

            nkc_q = 4 * qt + 4
            lbank = 3 if qt % 2 == 0 else 5

            def emit_lav(kc, vs0=vs0, lbank=lbank, nkc_q=nkc_q):
                g = kc // 8
                r, ltb = _chunk_owner(g)
                scol = r * NTOK + (ltb + kc % 8) * 128
                pt0 = A_PT + (kc % 3) * TT
                last = (kc == nkc_q - 1)

                def fl(h, kc=kc, pt0=pt0, lbank=lbank, last=last):
                    return h.eng.matmul(
                        pb[lbank][:], t.ones128bf[:], arb[:, pt0:pt0 + TT],
                        start=(kc == 0), stop=last)
                kb.op("tensor", fl,
                      reads=[rs("ones128bf"), rs("ARB", pt0 * 2, (pt0 + TT) * 2)],
                      writes=[rs(f"pb{lbank}")])

                def fav(h, scol=scol, vs0=vs0, pt0=pt0, kc=kc, last=last):
                    return h.eng.matmul(
                        pb[2][:], arb[:, vs0 + scol:vs0 + scol + 128],
                        arb[:, pt0:pt0 + TT], start=(kc == 0), stop=last)
                kb.op("tensor", fav,
                      reads=[rs("ARB", (vs0 + scol) * 2, (vs0 + scol + 128) * 2),
                             rs("ARB", pt0 * 2, (pt0 + TT) * 2)],
                      writes=[rs("pb2")])

            for kc in range(nkc_q):
                g = kc // 8
                r, ltb = _chunk_owner(g)
                scol = r * NTOK + (ltb + kc % 8) * 128
                sbank = kc % 2
                pt0 = A_PT + (kc % 3) * TT

                def fsm(h, sbank=sbank, scol=scol, ks0=ks0, qcol=qcol):
                    return h.eng.matmul(
                        pb[sbank][:], arb[:, ks0 + scol:ks0 + scol + 128],
                        arb[:, qcol:qcol + TT], start=True, stop=True)
                kb.op("tensor", fsm,
                      reads=[rs("ARB", (ks0 + scol) * 2, (ks0 + scol + 128) * 2),
                             rs("ARB", qcol * 2, (qcol + TT) * 2)],
                      writes=[rs(f"pb{sbank}")])

                def fex(h, sbank=sbank, pt0=pt0):
                    return h.eng.activation(
                        arb[:, pt0:pt0 + TT], pb[sbank][:], AF.Exp,
                        bias=t.fconst[:, 1:2], scale=INV_SQRT_HD)
                kb.op("scalar", fex, reads=[rs(f"pb{sbank}"), rs("fconst")],
                      writes=[rs("ARB", pt0 * 2, (pt0 + TT) * 2)])

                if hh == 0 and qt == 0 and kc == 0 and TRUNC.get("dumps"):
                    dump_arb(kb, t, pt0, TT, "pt_q0k0_premask")
                m = kc - 4 * qt
                if m >= 0:
                    def fmk(h, pt0=pt0, m=m):
                        return h.eng.tensor_tensor(
                            arb[:, pt0:pt0 + TT], arb[:, pt0:pt0 + TT],
                            t.masks[:, m * TT:(m + 1) * TT], AluOpType.mult)
                    kb.op("vector", fmk,
                          reads=[rs("ARB", pt0 * 2, (pt0 + TT) * 2), rs("masks")],
                          writes=[rs("ARB", pt0 * 2, (pt0 + TT) * 2)])

                if hh == 0 and qt == 0 and kc == 0 and TRUNC.get("dumps"):
                    dump_arb(kb, t, pt0, TT, "pt_q0k0_postmask")
                    dump_arb(kb, t, ks0, TT, "k_h0")
                    dump_arb(kb, t, vs0, TT, "v_h0")
                    dump_arb(kb, t, qcol, TT, "q_h0")

                if kc > 0:
                    emit_lav(kc - 1)
            emit_lav(nkc_q - 1)

            def frc(h, lbank=lbank):
                return h.eng.reciprocal(t.rl[:], pb[lbank][:])
            kb.op("vector", frc, reads=[rs(f"pb{lbank}")], writes=[rs("rl")])

            def foe(h):
                return h.eng.tensor_tensor(
                    arb[:, A_OST:A_OST + TT], pb[2][:], t.rl[:], AluOpType.mult)
            kb.op("vector", foe, reads=[rs("pb2"), rs("rl")],
                  writes=[rs("ARB", A_OST * 2, (A_OST + TT) * 2)])

            row0 = hh * 128

            def fos(h, row0=row0, qt=qt):
                return h.dma(t.o_in[row0:row0 + 128, qt * TT:(qt + 1) * TT],
                             arb[:, A_OST:A_OST + TT])
            kb.dma("gpsimd", "ost", fos, 1,
                   reads=[rs("ARB", A_OST * 2, (A_OST + TT) * 2)],
                   writes=[rs("o_in", (row0 * 8192 + qt * TT) * 2,
                              ((row0 + 128) * 8192) * 2)])

    if hasattr(t, "o_dbg"):
        def fod(h):
            last = None
            for rr in range(2):
                last = h.dma(t.o_dbg[rr * 256:(rr + 1) * 256, :],
                             t.o_in[rr * 256:(rr + 1) * 256, :])
            return last
        kb.dma("gpsimd", "odbg", fod, 2, reads=[rs("o_in")], writes=[rs("o_dbg")])

    def fcc3(h):
        return h.eng.collective_compute(
            "AllGather", mybir.AluOpType.bypass,
            replica_groups=[[0, 1, 2, 3], [4, 5, 6, 7]],
            ins=[t.o_in.ap().opt()], outs=[t.o_out.ap().opt()])
    kb.op("gpsimd", fcc3, reads=[rs("o_in")], writes=[rs("o_out")])

    def fol(h):
        last = None
        for r in range(4):
            for half in range(2):
                oi = OFF_O + r * 2 + half
                c0 = A_OT + (4 * r) * NTOK + half * 1024
                h.eng.reg_load(t.roff, t.offt[0:1, oi:oi + 1])
                last = h.dma(
                    bass.AP(arb, c0, [[ARBW, 128], [NTOK, 4], [1, 1024]]),
                    bass.AP(t.o_out, t.roff,
                            [[8192, 128], [128 * 8192, 4], [1, 1024]]))
        return last
    kb.dma("gpsimd", "old", fol, 8, reads=[rs("o_out"), rs("offt")],
           writes=[rs("ARB", A_OT * 2, (A_OT + NH * NTOK) * 2)])
    def fwl(h):
        last = None
        for fc in range(NKC):
            c0 = A_WO + fc * D
            last = h.dma(arb[:, c0:c0 + D], t.s_wo[fc * 128:(fc + 1) * 128, :])
        return last
    kb.dma("sync", "wol", fwl, NKC, reads=[rs(t.s_wo.name)],
           writes=[rs("ARB", A_WO * 2, (A_WO + NKC * D) * 2)])

    for tc16 in range(16):
        row0 = tc16 * 128
        xslot = tc16 % 4

        def fxl(h, row0=row0, xslot=xslot):
            return h.dma(t.xt[:, xslot * D:(xslot + 1) * D],
                         x_src[row0:row0 + 128, :])
        kb.dma("scalar", f"xra{xslot}", fxl, 1,
               reads=[rs(x_src.name, row0 * D, (row0 + 128) * D)],
               writes=[rs("xt", xslot * D * 4, (xslot + 1) * D * 4)])

        for h_ in range(NH):
            oc0 = A_OT + h_ * NTOK + tc16 * 128
            wc0 = A_WO + h_ * D
            for fg in range(4):
                def fmm(h, oc0=oc0, wc0=wc0, fg=fg, h_=h_):
                    return h.eng.matmul(
                        pb[fg][:], arb[:, oc0:oc0 + 128],
                        arb[:, wc0 + fg * TT:wc0 + (fg + 1) * TT],
                        start=(h_ == 0), stop=(h_ == NH - 1))
                kb.op("tensor", fmm,
                      reads=[rs("ARB", oc0 * 2, (oc0 + 128) * 2),
                             rs("ARB", (wc0 + fg * TT) * 2, (wc0 + (fg + 1) * TT) * 2)],
                      writes=[rs(f"pb{fg}")])
        for fg in range(4):
            _stt_res_add(kb, t, pb[fg], f"pb{fg}", xslot * D, fg)

        def fxs(h, row0=row0, xslot=xslot):
            return h.dma(x_dst[row0:row0 + 128, :],
                         t.xt[:, xslot * D:(xslot + 1) * D])
        kb.dma("gpsimd", "xsa", fxs, 1,
               reads=[rs("xt", xslot * D * 4, (xslot + 1) * D * 4)],
               writes=[rs(x_dst.name, row0 * D, (row0 + 128) * D)])

    def ffence2(h):
        return h.eng.memset(t.fencet[:], 0.0)
    kb.fence("gpsimd", ffence2, [rs(x_dst.name)])


def _run(inputs, debug=False, phases=("ffn1", "attn", "ffn2")):
    shared = _prep_shared(inputs)
    in_maps = []
    for core in range(8):
        m = dict(shared)
        m.update(_prep_core(inputs, core))
        in_maps.append(m)
    nc = build_nc(debug=debug, phases=phases)
    res = run_bass_kernel_spmd(nc, in_maps, core_ids=list(range(8)))
    return res


def kernel(**inputs):
    res = _run(inputs)
    out = np.zeros((2, 8192, 2048), np.float32)
    for core in range(8):
        b, j = core // 4, core % 4
        o = np.asarray(res.results[core]["OUT"])
        out[b, j * 1024:(j + 1) * 1024, :] = o[0:1024, :]
        out[b, (7 - j) * 1024:(8 - j) * 1024, :] = o[1024:2048, :]
    return out


# version 10
# speedup vs baseline: 1.1326x; 1.0251x over previous
import math
import numpy as np
import ml_dtypes
import concourse.bass as bass
import concourse.mybir as mybir
from concourse.bass_utils import run_bass_kernel_spmd
from concourse.alu_op_type import AluOpType
from contextlib import ExitStack

F32 = mybir.dt.float32
BF16 = mybir.dt.bfloat16
I32 = mybir.dt.int32
AF = mybir.ActivationFunctionType
AX = mybir.AxisListType.X
bfloat16 = ml_dtypes.bfloat16

D = 2048
II = 7168
E = 8
R = 64
NH = 16
HD = 128
NTOK = 2048
TT = 512
NTT = NTOK // TT
NKC = D // 128
NIC = II // 128
NOC3 = 3 * NH
SCALING = 16.0 / 64.0
EXP_SHIFT = -12.0
INV_SQRT_HD = 1.0 / math.sqrt(HD)
LN_EPS = 1e-5
BIGNEG = -1.0e9

ENGINES = ("sync", "scalar", "vector", "tensor", "gpsimd")
TRUNC = {"ffn_tt": NTT, "parts": "ABC", "p0cast": True, "p0const": True}


class KB:
    def __init__(self):
        self.eng_ops = {e: [] for e in ENGINES}
        self.res = {}
        self.slot_count = {}
        self.marked = {e: set() for e in ENGINES}
        self.scope = None

    def _resolve(self, myeng, reads, writes):
        needs = set()
        for (name, lo, hi) in reads:
            for en in self.res.get(name, ()):
                if en[0] < hi and lo < en[1] and en[2] is not None:
                    needs.add(en[2])
        for (name, lo, hi) in writes:
            for en in self.res.get(name, ()):
                if en[0] < hi and lo < en[1]:
                    if en[2] is not None:
                        needs.add(en[2])
                    needs.update(en[3])
        return {n for n in needs
                if not (n[0] == "E" and n[1] == myeng and myeng == "tensor")}

    def _commit(self, reads, writes, ev):
        for (name, lo, hi) in reads:
            entries = self.res.setdefault(name, [])
            for en in entries:
                if en[0] == lo and en[1] == hi:
                    en[3].add(ev)
                    break
            else:
                entries.append([lo, hi, None, {ev}])
        for (name, lo, hi) in writes:
            entries = self.res.setdefault(name, [])
            keep = [en for en in entries if not (en[0] < hi and lo < en[1])]
            keep.append([lo, hi, ev, set()])
            self.res[name] = keep

    def op(self, eng, fn, reads=(), writes=()):
        needs = self._resolve(eng, reads, writes)
        idx = len(self.eng_ops[eng])
        ev = ("E", eng, idx)
        self._commit(reads, writes, ev)
        for n in needs:
            if n[0] == "E":
                self.marked[n[1]].add(n[2])
        self.eng_ops[eng].append(
            {"eng": eng, "fn": fn, "slot": None, "idx": idx,
             "needs": needs, "scope": self.scope})

    def dma(self, eng, slot, fn, ndma, reads=(), writes=()):
        needs = self._resolve(eng, reads, writes)
        cnt = self.slot_count.get(slot, 0) + 16 * ndma
        self.slot_count[slot] = cnt
        idx = len(self.eng_ops[eng])
        ev = ("S", slot, cnt)
        self._commit(reads, writes, ev)
        for n in needs:
            if n[0] == "E":
                self.marked[n[1]].add(n[2])
        self.eng_ops[eng].append(
            {"eng": eng, "fn": fn, "slot": slot, "idx": idx,
             "needs": needs, "scope": self.scope})

    def fence(self, eng, fn, resources):
        self.op(eng, fn, reads=resources, writes=resources)

    def emit(self, nc, block, sems_eng, sems_slot):
        val_at = {}
        for e in ENGINES:
            for rank, i in enumerate(sorted(self.marked[e])):
                val_at[(e, i)] = rank + 1

        for e in ENGINES:
            ops = self.eng_ops[e]
            if not ops:
                continue

            def make_section(e, ops):
                def section(engproxy):
                    know = {}
                    for op in ops:
                        for n in sorted(op["needs"], key=repr):
                            if n[0] == "E":
                                sem, val, key = sems_eng[n[1]], val_at[(n[1], n[2])], ("E", n[1])
                            else:
                                sem, val, key = sems_slot[n[1]], n[2], ("S", n[1])
                            if know.get(key, 0) >= val:
                                continue
                            engproxy.wait_ge(sem, val)
                            know[key] = val
                        h = EmitCtx(engproxy, sems_slot.get(op["slot"]))
                        last = op["fn"](h)
                        if op["slot"] is None and op["idx"] in self.marked[e]:
                            assert last is not None, (e, op["idx"], op["scope"])
                            last.then_inc(sems_eng[e], 1)
                return section
            getattr(block, e)(make_section(e, ops))


class EmitCtx:
    def __init__(self, eng, slot_sem):
        self.eng = eng
        self.slot_sem = slot_sem

    def dma(self, dst, src, **kw):
        inst = self.eng.dma_start(dst, src, **kw)
        inst.then_inc(self.slot_sem, 16)
        return inst


def rs(t, lo=0, hi=1 << 40):
    return (t, lo, hi)


def dump_arb(kb, t, c0, n, label, rows=128):
    if not hasattr(t, "DBG") or not TRUNC.get("dumps"):
        return
    off = t.dbgoff[0]
    assert off + n <= 65536, "DBG overflow"
    t.dbgoff[0] = off + n
    TRUNC.setdefault("dumplog", []).append((label, off, n, rows))

    def f(h, off=off, c0=c0, n=n, rows=rows):
        return h.dma(t.DBG[0:rows, off:off + n], t.ARB[0:rows, c0:c0 + n])
    kb.dma("gpsimd", "dbgd", f, 1,
           reads=[rs("ARB", c0 * 2, (c0 + n) * 2)], writes=[rs("DBG")])


def dump_t(kb, t, tile, res_name, n, label, rows=128):
    if not hasattr(t, "DBG") or not TRUNC.get("dumps"):
        return
    off = t.dbgoff[0]
    t.dbgoff[0] = off + n
    TRUNC.setdefault("dumplog", []).append((label, off, n, rows))

    def f(h, off=off, n=n, rows=rows, tile=tile):
        return h.dma(t.DBG[0:rows, off:off + n], tile[0:rows, 0:n])
    kb.dma("gpsimd", "dbgd", f, 1, reads=[rs(res_name)], writes=[rs("DBG")])


def _prep_shared(inp):
    d = {}
    wqkv = np.asarray(inp["L0_attn_w_qkv"], np.float32)
    d["p_qkv"] = np.ascontiguousarray(
        wqkv.reshape(NKC, 128, NOC3, 128).transpose(2, 1, 0, 3).reshape(NOC3, 128, D))
    d["p_wo"] = np.asarray(inp["L0_attn_w_o"], np.float32)
    gate = np.asarray(inp["L0_ffn_base_gate"], np.float32)
    up = np.asarray(inp["L0_ffn_base_up"], np.float32)
    d["p_gate"] = np.ascontiguousarray(
        gate.reshape(NKC, 128, NIC, 128).transpose(2, 1, 0, 3).reshape(NIC, 128, D))
    d["p_up"] = np.ascontiguousarray(
        up.reshape(NKC, 128, NIC, 128).transpose(2, 1, 0, 3).reshape(NIC, 128, D))
    d["p_down"] = np.asarray(inp["L0_ffn_base_down"], np.float32)
    gA = np.asarray(inp["L0_ffn_gate_A"], np.float32)
    uA = np.asarray(inp["L0_ffn_up_A"], np.float32)
    pgA = gA.reshape(E, NKC, 128, R).transpose(1, 2, 0, 3)
    puA = uA.reshape(E, NKC, 128, R).transpose(1, 2, 0, 3)
    tA = np.empty((NKC, 128, E, 128), np.float32)
    tA[:, :, :, 0:64] = pgA
    tA[:, :, :, 64:128] = puA
    d["p_tA"] = np.ascontiguousarray(tA.reshape(NKC, 128, E * 128))
    gB = np.asarray(inp["L0_ffn_gate_B"], np.float32)
    uB = np.asarray(inp["L0_ffn_up_B"], np.float32)
    pgB = gB.reshape(E, R, NIC, 128).transpose(2, 1, 0, 3)
    puB = uB.reshape(E, R, NIC, 128).transpose(2, 1, 0, 3)
    lq = np.zeros((NIC, 128, E, 256), np.float32)
    lq[:, 0:64, :, 0:128] = pgB
    lq[:, 64:128, :, 128:256] = puB
    d["p_lq"] = np.ascontiguousarray(lq.reshape(NIC, 128, E * 256))
    dA = np.asarray(inp["L0_ffn_down_A"], np.float32)
    d["p_dA"] = np.ascontiguousarray(
        dA.reshape(E, NIC, 128, R).transpose(1, 2, 0, 3).reshape(NIC, 128, E * R))
    d["p_dB"] = np.asarray(inp["L0_ffn_down_B"], np.float32)
    d["p_confWT"] = np.ascontiguousarray(
        np.asarray(inp["L0_ffn_conf_W"], np.float32).T.reshape(NKC, 128, E))
    d["p_confb"] = np.ascontiguousarray(np.broadcast_to(
        np.asarray(inp["L0_ffn_conf_b"], np.float32).reshape(1, E), (128, E)))
    d["p_wealth"] = np.ascontiguousarray(np.broadcast_to(
        np.asarray(inp["L0_ffn_expert_wealth"], np.float32).reshape(1, E), (128, E)))

    d["c_identbf"] = np.eye(128, dtype=bfloat16)
    d["c_ones128bf"] = np.ones((128, 128), dtype=bfloat16)
    d["c_ones1bf"] = np.ones((1, 128), dtype=bfloat16)
    d["c_iota8p8"] = np.broadcast_to(
        np.arange(E, dtype=np.float32)[None, :] + 8.0, (128, E)).copy()
    sel = np.zeros((E, E * 128), dtype=bfloat16)
    for e in range(E):
        sel[e, e * 128:(e + 1) * 128] = 1.0
    d["c_sel"] = sel
    Rl = np.zeros((128, 128), dtype=np.float32)
    for m in range(64):
        Rl[m + 64, m] = -1.0
        Rl[m, m + 64] = 1.0
    d["c_ropeR"] = Rl.astype(bfloat16)
    kk = np.arange(128)[:, None]
    qq = np.arange(TT)[None, :]
    masks = np.concatenate(
        [((m * 128 + kk) <= qq).astype(bfloat16) for m in range(4)], axis=1)
    d["c_masks"] = np.ascontiguousarray(masks)
    fc = np.zeros((128, 2), np.float32)
    fc[:, 0] = LN_EPS
    fc[:, 1] = EXP_SHIFT
    d["c_fconst"] = fc
    return d


def _chunk_owner(g):
    r = g if g < 4 else 7 - g
    return r, (0 if g == r else 8)


def _prep_core(inp, core):
    b, j = core // 4, core % 4
    chunks = (j, 7 - j)
    x = np.asarray(inp["L0_attn_x"], np.float32)
    xs = np.concatenate([x[b, c * 1024:(c + 1) * 1024, :] for c in chunks], axis=0)
    pos = np.concatenate([np.arange(c * 1024, (c + 1) * 1024) for c in chunks])
    inv_freq = 1.0 / (10000.0 ** (np.arange(0, HD, 2, dtype=np.float64) / HD))
    fr = pos[:, None].astype(np.float64) * inv_freq[None, :]
    cos_t = np.concatenate([np.cos(fr).T, np.cos(fr).T], 0).astype(bfloat16)
    sin_t = np.concatenate([np.sin(fr).T, np.sin(fr).T], 0).astype(bfloat16)

    offs = []
    for hh in range(4):
        offs.append((2048 + (4 * j + hh) * 128) * D)
    for hh in range(4):
        offs.append((4 * j + hh) * D)
    for hh in range(4):
        offs.append((4 * j + hh) * 128 * D)
    for r in range(4):
        for half in range(2):
            ch = chunks[half]
            offs.append((r * 128) * 8192 + ch * 1024)
    offtab = np.zeros((1, 256), np.int32)
    offtab[0, :len(offs)] = np.array(offs, np.int32)
    return {"x0": np.ascontiguousarray(xs), "ropec": np.ascontiguousarray(cos_t),
            "ropes": np.ascontiguousarray(sin_t), "offtab": offtab}


class T:
    pass


def build_nc(debug=False, phases=("ffn1", "attn", "ffn2")):
    nc = bass.Bass("TRN2", target_bir_lowering=False, detect_race_conditions=False)
    t = T()

    def din(name, shape, dt=F32):
        setattr(t, name, nc.dram_tensor(name, shape, dt, kind="ExternalInput"))

    din("x0", [NTOK, D])
    din("p_qkv", [NOC3, 128, D]); din("p_wo", [D, D])
    din("p_gate", [NIC, 128, D]); din("p_up", [NIC, 128, D]); din("p_down", [II, D])
    din("p_tA", [NKC, 128, E * 128])
    din("p_lq", [NIC, 128, E * 256])
    din("p_dA", [NIC, 128, E * R]); din("p_dB", [E, R, D])
    din("p_confWT", [NKC, 128, E]); din("p_confb", [128, E]); din("p_wealth", [128, E])
    din("c_identbf", [128, 128], BF16)
    din("c_ones128bf", [128, 128], BF16); din("c_ones1bf", [1, 128], BF16)
    din("c_iota8p8", [128, E]); din("c_sel", [E, E * 128], BF16)
    din("c_ropeR", [128, 128], BF16)
    din("c_masks", [128, 4 * TT], BF16); din("c_fconst", [128, 2])
    din("ropec", [128, NTOK], BF16); din("ropes", [128, NTOK], BF16)
    din("offtab", [1, 256], I32)

    t.s_qkv = nc.dram_tensor("s_qkv", [NOC3, 128, D], BF16)
    t.s_gate = nc.dram_tensor("s_gate", [NIC, 128, D], BF16)
    t.s_up = nc.dram_tensor("s_up", [NIC, 128, D], BF16)
    t.s_down = nc.dram_tensor("s_down", [II, D], BF16)
    t.s_wo = nc.dram_tensor("s_wo", [D, D], BF16)
    t.s_tA = nc.dram_tensor("s_tA", [NKC, 128, E * 128], BF16)
    t.s_lq = nc.dram_tensor("s_lq", [NIC, 128, E * 256], BF16)
    t.s_dA = nc.dram_tensor("s_dA", [NIC, 128, E * R], BF16)
    t.s_dB = nc.dram_tensor("s_dB", [E, R, D], BF16)
    if debug:
        t.xB = nc.dram_tensor("xB", [NTOK, D], F32, kind="ExternalOutput")
        t.xC = nc.dram_tensor("xC", [NTOK, D], F32, kind="ExternalOutput")
    else:
        t.xB = nc.dram_tensor("xB", [NTOK, D], F32)
        t.xC = nc.dram_tensor("xC", [NTOK, D], F32)
    t.qk_in = nc.dram_tensor("qk_in", [2 * D, NTOK], BF16)
    t.qk_out = nc.dram_tensor("qk_out", [8 * D, NTOK], BF16)
    t.v_in = nc.dram_tensor("v_in", [128, NH * NTOK], BF16)
    t.v_out = nc.dram_tensor("v_out", [512, NH * NTOK], BF16)
    t.o_in = nc.dram_tensor("o_in", [512, 8192], BF16)
    t.o_out = nc.dram_tensor("o_out", [2048, 8192], BF16)
    t.OUT = nc.dram_tensor("OUT", [NTOK, D], F32, kind="ExternalOutput")
    if debug:
        t.DBG = nc.dram_tensor("DBG", [128, 65536], BF16, kind="ExternalOutput")
        t.qk_dbg = nc.dram_tensor("qk_dbg", [2 * D, NTOK], BF16, kind="ExternalOutput")
        t.o_dbg = nc.dram_tensor("o_dbg", [512, 8192], BF16, kind="ExternalOutput")
    t.dbgoff = [0]

    kb = KB()

    with ExitStack() as st:
        en = st.enter_context

        def sb(name, shape, dt):
            setattr(t, name, en(nc.sbuf_tensor(name, shape, dt)))

        for n, sh, dt in [
            ("identbf", [128, 128], BF16),
            ("ones128bf", [128, 128], BF16), ("ones1bf", [1, 128], BF16),
            ("iota8p8", [128, E], F32), ("sel", [E, E * 128], BF16),
            ("ropeR", [128, 128], BF16), ("masks", [128, 4 * TT], BF16),
            ("cosb", [128, NTOK], BF16), ("sinb", [128, NTOK], BF16),
            ("confWT", [128, NKC * E], BF16), ("confb", [128, E], F32),
            ("fconst", [128, 2], F32),
            ("wealth", [128, E], F32), ("offt", [1, 256], I32),
        ]:
            sb(n, sh, dt)
        sb("ARB", [128, 67584], BF16)
        sb("fencet", [1, 8], F32)
        sb("xt", [128, 4 * D], F32)
        sb("ybf", [128, D], BF16)
        sb("sq", [128, TT], F32)
        sb("sf32", [128, 2 * TT], F32)
        sb("hf32", [128, TT], F32)
        sb("rlbl", [1, TT], BF16)
        sb("stats", [128, 64], F32)
        sb("rt", [128, 96], F32)
        sb("bids", [E, TT], F32)
        sb("lrow", [1, 2 * TT], F32)
        sb("rl", [128, TT], F32)
        sb("webf8", [128, E], BF16)
        sb("rlbf", [1, TT], BF16)
        t.pb = [en(nc.psum_tensor(f"pb{i}", [128, TT], F32)) for i in range(7)]
        t.pbh = en(nc.psum_tensor("pbh", [128, 1024], BF16))

        sems_eng = {e: en(nc.semaphore(f"se_{e}")) for e in ENGINES}
        t.roff = en(nc.gpsimd.register("roff"))

        _build_plan(kb, t, debug, phases)

        slots = sorted(kb.slot_count)
        sems_slot = {s: en(nc.semaphore(f"ss_{s}")) for s in slots}
        block = en(nc.Block())
        kb.emit(nc, block, sems_eng, sems_slot)

    return nc


A_YT = 0
A_H = 8192
A_WG = 36864
A_WU = 40960
A_GB = 45056
A_UB = 47104
A_DA = 49152
A_TG = 50176
A_TU = 54272
A_WEB = 58368
A_WER = 62464
A_BG = 62976
A_BU = 63488
A_SB = 64000
A_HB = 64512
A_WEH = 65024


def WEHS(e):
    return A_SB + (e % 3) * TT
A_GAS = 65536
A_UAS = 66560
A_DN = 65536
ARBW = 67584
def C_DBLOC(e):
    return A_WG + e * D if e < 6 else A_WEB + (e - 6) * D
A_Y2T = 0
A_KS = 0
A_VS = 16384
A_QS = 32768
A_OST = 49152
A_PT = 49664
A_QP = 50688
A_RT1 = 51712
A_VSB = 52736
A_WQ = 53760
A_OT = 0
A_WO = 32768


def _build_plan(kb, t, debug, phases):
    _p0(kb, t)
    if TRUNC.get("touch", True):
        def finit(h):
            return h.eng.memset(t.fencet[:], 0.0)
        kb.op("gpsimd", finit, reads=(), writes=[rs("fencet")])
        outs = [t.OUT] + ([t.xB, t.xC] if debug else [])
        def ftouch(h, outs=outs):
            last = None
            for o in outs:
                last = h.dma(o[0:1, 0:8], t.fencet[0:1, 0:8])
            return last
        kb.dma("gpsimd", "touch", ftouch, len(outs),
               reads=[rs("fencet")], writes=[])
    last = phases[-1]
    if "ffn1" in phases:
        kb.scope = "ffn1"
        dst = t.OUT if (last == "ffn1" and not debug) else t.xB
        _ffn(kb, t, t.x0, dst, tag="f1")
    if "attn" in phases:
        kb.scope = "attn"
        dst = t.OUT if (last == "attn" and not debug) else t.xC
        _attn(kb, t, t.xB, dst)
    if "ffn2" in phases:
        kb.scope = "ffn2"
        _ffn(kb, t, t.xC, t.OUT, tag="f2")
    if TRUNC.get("fin", True):
        def fin(h):
            return h.eng.engine_nop()
        reads = [rs("OUT")]
        if debug:
            reads += [rs("xB"), rs("xC")]
        kb.op("vector", fin, reads=reads)


def _p0(kb, t):
    kb.scope = None if TRUNC.get("noscope") else "p0"

    pairs = [
        (t.identbf, t.c_identbf),
        (t.ones128bf, t.c_ones128bf), (t.ones1bf, t.c_ones1bf),
        (t.iota8p8, t.c_iota8p8), (t.sel, t.c_sel), (t.ropeR, t.c_ropeR),
        (t.masks, t.c_masks), (t.cosb, t.ropec), (t.sinb, t.ropes),
        (t.confb, t.p_confb), (t.wealth, t.p_wealth), (t.offt, t.offtab),
        (t.fconst, t.c_fconst),
    ]
    for dst, src in (pairs if TRUNC.get("p0const", True) else []):
        def f(h, dst=dst, src=src):
            return h.dma(dst[:], src[:])
        kb.dma("sync", "const", f, 1, writes=[rs(dst.name)])
    for kc in range(NKC if TRUNC.get("p0const", True) else 0):
        def f(h, kc=kc):
            return h.dma(t.confWT[:, kc * E:(kc + 1) * E], t.p_confWT[kc])
        kb.dma("gpsimd", "const2", f, 1, writes=[rs("confWT", kc * E, (kc + 1) * E)])

    cast_groups = [
        [(t.s_tA, t.p_tA), (t.s_lq, t.p_lq), (t.s_dA, t.p_dA),
         (t.s_gate, t.p_gate), (t.s_up, t.p_up)],
        [(t.s_down, t.p_down), (t.s_dB, t.p_dB)],
        [(t.s_qkv, t.p_qkv), (t.s_wo, t.p_wo)],
    ]

    def chunks_of(dst, src):
        shape = [int(x) for x in src.shape]
        cols = shape[-1]
        rows = 1
        for s in shape[:-1]:
            rows *= s
        out = []
        o = 0
        while o < rows:
            rchunk = min(512, rows - o)
            out.append((dst, src, o, rchunk, cols))
            o += rchunk
        return out

    for gi, casts in enumerate(cast_groups):
        todo = []
        if TRUNC.get("p0cast", True):
            if gi == 0:
                for dst, src in casts[:3]:
                    todo += chunks_of(dst, src)
                ga = chunks_of(t.s_gate, t.p_gate)
                up = chunks_of(t.s_up, t.p_up)
                for a, b in zip(ga, up):
                    todo += [a, b]
            else:
                for dst, src in casts:
                    todo += chunks_of(dst, src)
        for dst, src, o, rchunk, cols in todo:
            def f(h, dst=dst, src=src, o=o, rchunk=rchunk, cols=cols):
                return h.dma(
                    bass.AP(dst, o * cols, [[cols, rchunk], [1, cols]]),
                    bass.AP(src, o * cols, [[cols, rchunk], [1, cols]]))
            kb.dma("gpsimd", "wcast", f, 1,
                   writes=[rs(dst.name, o * cols, (o + rchunk) * cols)])



def _ln_tile(kb, t, x_src, tt, tc, tag, out_bf16_direct):
    slot = tc
    row0 = tt * TT + tc * 128

    def fld(h):
        return h.dma(t.xt[:, slot * D:(slot + 1) * D], x_src[row0:row0 + 128, :])
    kb.dma("scalar", f"xld{tag}{slot}", fld, 1,
           reads=[rs(x_src.name, row0 * D, (row0 + 128) * D)],
           writes=[rs("xt", slot * D * 4, (slot + 1) * D * 4)])

    xr = [rs("xt", slot * D * 4, (slot + 1) * D * 4)]

    def fsum(h):
        return h.eng.tensor_reduce(
            t.stats[:, tc:tc + 1], t.xt[:, slot * D:(slot + 1) * D], axis=AX,
            op=AluOpType.add)
    kb.op("vector", fsum, reads=xr, writes=[rs("stats", tc * 4, tc * 4 + 4)])

    for j in range(4):
        def fsq(h, j=j):
            return h.eng.activation(
                t.sq[:], t.xt[:, slot * D + j * TT:slot * D + (j + 1) * TT],
                AF.Square, accum_out=t.stats[:, 32 + tc * 4 + j:33 + tc * 4 + j])
        kb.op("scalar", fsq, reads=xr,
              writes=[rs("sq"), rs("stats", (32 + tc * 4 + j) * 4,
                                   (33 + tc * 4 + j) * 4)])

    def fsqs(h):
        return h.eng.tensor_reduce(
            t.stats[:, 4 + tc:5 + tc], t.stats[:, 32 + tc * 4:32 + tc * 4 + 4],
            axis=AX, op=AluOpType.add)
    kb.op("vector", fsqs,
          reads=[rs("stats", (32 + tc * 4) * 4, (36 + tc * 4) * 4)],
          writes=[rs("stats", (4 + tc) * 4, (5 + tc) * 4)])

    def fm(h):
        return h.eng.tensor_scalar(
            t.stats[:, 8 + tc:9 + tc], t.stats[:, tc:tc + 1], 1.0 / D, None,
            AluOpType.mult)
    kb.op("vector", fm, reads=[rs("stats", tc * 4, tc * 4 + 4)],
          writes=[rs("stats", (8 + tc) * 4, (9 + tc) * 4)])

    def fq2(h):
        return h.eng.tensor_scalar(
            t.stats[:, 12 + tc:13 + tc], t.stats[:, 4 + tc:5 + tc], 1.0 / D, None,
            AluOpType.mult)
    kb.op("vector", fq2, reads=[rs("stats", (4 + tc) * 4, (5 + tc) * 4)],
          writes=[rs("stats", (12 + tc) * 4, (13 + tc) * 4)])

    def fvar(h):
        return h.eng.scalar_tensor_tensor(
            t.stats[:, 16 + tc:17 + tc], t.stats[:, 8 + tc:9 + tc],
            t.stats[:, 8 + tc:9 + tc], t.stats[:, 12 + tc:13 + tc],
            AluOpType.mult, AluOpType.subtract)
    kb.op("vector", fvar,
          reads=[rs("stats", (8 + tc) * 4, (9 + tc) * 4),
                 rs("stats", (12 + tc) * 4, (13 + tc) * 4)],
          writes=[rs("stats", (16 + tc) * 4, (17 + tc) * 4)])
    return slot


def _ln_finish(kb, t):
    def fsq(h):
        return h.eng.activation(t.stats[:, 20:24], t.stats[:, 16:20], AF.Sqrt,
                                bias=t.fconst[:, 0:1], scale=-1.0)
    kb.op("scalar", fsq, reads=[rs("stats", 64, 80), rs("fconst")],
          writes=[rs("stats", 80, 96)])

    def fr(h):
        return h.eng.reciprocal(t.stats[:, 24:28], t.stats[:, 20:24])
    kb.op("vector", fr, reads=[rs("stats", 80, 96)], writes=[rs("stats", 96, 112)])


def _ln_norm(kb, t, tc, slot):
    def fn(h):
        return h.eng.tensor_scalar(
            t.ybf[:], t.xt[:, slot * D:(slot + 1) * D],
            t.stats[:, 8 + tc:9 + tc], t.stats[:, 24 + tc:25 + tc],
            AluOpType.subtract, AluOpType.mult)
    kb.op("vector", fn,
          reads=[rs("xt", slot * D * 4, (slot + 1) * D * 4),
                 rs("stats", (8 + tc) * 4, (9 + tc) * 4),
                 rs("stats", (24 + tc) * 4, (25 + tc) * 4)],
          writes=[rs("ybf")])


def _stt_res_add(kb, t, psum, psum_key, xcol0, fg):
    c0 = xcol0 + fg * TT

    def fn(h):
        return h.eng.tensor_tensor(
            t.xt[:, c0:c0 + TT], psum[:], t.xt[:, c0:c0 + TT], AluOpType.add)
    kb.op("vector", fn,
          reads=[rs(psum_key), rs("xt", c0 * 4, (c0 + TT) * 4)],
          writes=[rs("xt", c0 * 4, (c0 + TT) * 4)])


def _ffn(kb, t, x_src, x_dst, tag):
    pb = t.pb
    arb = t.ARB

    def AB(c0, n=TT):
        return arb[:, c0:c0 + n], rs("ARB", c0 * 2, (c0 + n) * 2)

    for tt in range(min(NTT, TRUNC["ffn_tt"])):
        sc0 = kb.scope
        for tc in range(4):
            _ln_tile(kb, t, x_src, tt, tc, tag, False)
        _ln_finish(kb, t)
        AL = TRUNC.get("a_parts", 4)
        for tc in range(4):
            slot = tc
            _ln_norm(kb, t, tc, slot)
            zb = 5 + tc % 2
            for grp in range(2 if AL >= 2 else 0):
                for g8 in range(8):
                    fc = grp * 8 + g8

                    def ftr(h, fc=fc, g8=g8):
                        return h.eng.transpose(
                            t.pbh[:, g8 * 128:(g8 + 1) * 128],
                            t.ybf[:, fc * 128:(fc + 1) * 128], t.identbf[:])
                    kb.op("tensor", ftr, reads=[rs("ybf"), rs("identbf")],
                          writes=[rs("pbh", g8, g8 + 1)])

                yg0 = grp * 8 * TT + tc * 128

                def fe2(h, yg0=yg0):
                    return h.eng.copy(
                        bass.AP(arb, yg0, [[ARBW, 128], [TT, 8], [1, 128]]),
                        t.pbh[:].rearrange("p (a b) -> p a b", a=8))
                kb.op("scalar", fe2, reads=[rs("pbh", 0, 8)],
                      writes=[rs("ARB", yg0 * 2, (yg0 + 8 * TT) * 2)])

                if TRUNC.get("a_sub", 4) >= 4:
                    for g8 in range(8):
                        fc = grp * 8 + g8
                        ytc0 = fc * TT + tc * 128

                        def fz(h, fc=fc, ytc0=ytc0, zb=zb):
                            return h.eng.matmul(
                                pb[zb][:, 0:E],
                                arb[:, ytc0:ytc0 + 128],
                                t.confWT[:, fc * E:(fc + 1) * E],
                                start=(fc == 0), stop=(fc == NKC - 1))
                        kb.op("tensor", fz,
                              reads=[rs("confWT", fc * E, (fc + 1) * E),
                                     rs("ARB", ytc0 * 2, (ytc0 + 128) * 2)],
                              writes=[rs(f"pb{zb}", 0, 1)])

            if AL < 3:
                continue
            c = 32 + tc * 8
            bidsc = tc * 8
            selc = 64 + (tc % 2) * 16
            bids = t.rt[:, bidsc:bidsc + 8]
            m1 = t.rt[:, c:c + 1]; i1 = t.rt[:, c + 1:c + 2]
            m2 = t.rt[:, c + 2:c + 3]; i2 = t.rt[:, c + 3:c + 4]
            d12 = t.rt[:, c + 4:c + 6]; r12 = t.rt[:, c + 6:c + 8]
            sel1 = t.rt[:, selc:selc + 8]
            bids2 = t.rt[:, selc + 8:selc + 16]
            bidr = rs("rt", bidsc * 4, (bidsc + 8) * 4)
            rtr = rs("rt", c * 4, (c + 8) * 4)
            rtr2 = rs("rt", selc * 4, (selc + 16) * 4)

            def step(fn, reads, writes, eng="vector"):
                kb.op(eng, fn, reads=reads, writes=writes)

            step(lambda h, bids=bids, zb=zb: h.eng.tensor_tensor(
                bids, pb[zb][:, 0:E], t.confb[:], AluOpType.add),
                 [rs(f"pb{zb}", 0, 1), rs("confb")], [bidr])
            step(lambda h, bids=bids: h.eng.activation(bids, bids, AF.Sigmoid),
                 [bidr], [bidr], eng="scalar")
            step(lambda h, bids=bids: h.eng.tensor_tensor(
                bids, bids, t.wealth[:], AluOpType.mult),
                 [bidr, rs("wealth")], [bidr])
            step(lambda h, bids=bids, m1=m1: h.eng.tensor_reduce(
                m1, bids, axis=AX, op=AluOpType.max), [bidr], [rtr])
            step(lambda h, bids=bids, m1=m1, sel1=sel1: h.eng.tensor_scalar(
                sel1, bids, m1, None, AluOpType.is_equal), [bidr, rtr], [rtr2])
            step(lambda h, sel1=sel1, bids2=bids2: h.eng.scalar_tensor_tensor(
                bids2, sel1, -8.0, t.iota8p8[:], AluOpType.mult, AluOpType.add),
                 [rtr2, rs("iota8p8")], [rtr2])
            step(lambda h, bids2=bids2, i1=i1: h.eng.tensor_reduce(
                i1, bids2, axis=AX, op=AluOpType.min), [rtr2], [rtr])
            step(lambda h, i1=i1, sel1=sel1: h.eng.tensor_scalar(
                sel1, t.iota8p8[:], 8.0, i1, AluOpType.subtract, AluOpType.is_equal),
                 [rs("iota8p8"), rtr], [rtr2])
            step(lambda h, sel1=sel1, bids=bids, bids2=bids2: h.eng.scalar_tensor_tensor(
                bids2, sel1, BIGNEG, bids, AluOpType.mult, AluOpType.add),
                 [rtr2, bidr], [rtr2])
            step(lambda h, bids2=bids2, m2=m2: h.eng.tensor_reduce(
                m2, bids2, axis=AX, op=AluOpType.max), [rtr2], [rtr])
            step(lambda h, bids2=bids2, m2=m2: h.eng.tensor_scalar(
                bids2, bids2, m2, None, AluOpType.is_equal), [rtr2, rtr], [rtr2])
            step(lambda h, bids2=bids2: h.eng.scalar_tensor_tensor(
                bids2, bids2, -8.0, t.iota8p8[:], AluOpType.mult, AluOpType.add),
                 [rtr2, rs("iota8p8")], [rtr2])
            step(lambda h, bids2=bids2, i2=i2: h.eng.tensor_reduce(
                i2, bids2, axis=AX, op=AluOpType.min), [rtr2], [rtr])
            step(lambda h, i2=i2, bids2=bids2: h.eng.tensor_scalar(
                bids2, t.iota8p8[:], 8.0, i2, AluOpType.subtract, AluOpType.is_equal),
                 [rs("iota8p8"), rtr], [rtr2])
            step(lambda h, m1=m1, m2=m2, d12=d12: h.eng.tensor_tensor(
                d12[:, 0:1], m1, m2, AluOpType.subtract), [rtr], [rtr])
            step(lambda h, d12=d12: h.eng.tensor_scalar(
                d12[:, 1:2], d12[:, 0:1], -1.0, None, AluOpType.mult), [rtr], [rtr])
            step(lambda h, d12=d12, r12=r12: h.eng.activation(r12, d12, AF.Sigmoid),
                 [rtr], [rtr], eng="scalar")
            step(lambda h, r12=r12: h.eng.tensor_scalar(
                r12, r12, 0.5, None, AluOpType.mult), [rtr], [rtr])
            step(lambda h, sel1=sel1, r12=r12: h.eng.tensor_scalar(
                sel1, sel1, r12[:, 0:1], None, AluOpType.mult), [rtr2, rtr], [rtr2])
            step(lambda h, bids2=bids2, r12=r12, sel1=sel1: h.eng.scalar_tensor_tensor(
                sel1, bids2, r12[:, 1:2], sel1, AluOpType.mult, AluOpType.add),
                 [rtr2, rtr], [rtr2])
            step(lambda h, sel1=sel1: h.eng.tensor_copy(t.webf8[:], sel1),
                 [rtr2], [rs("webf8")])

            def fwt(h, tc=tc):
                return h.eng.transpose(
                    t.pbh[0:E, 0:128], t.webf8[:], t.identbf[:])
            kb.op("tensor", fwt, reads=[rs("webf8"), rs("identbf")],
                  writes=[rs("pbh", 0, 1)])

            def fwe(h, tc=tc):
                return h.eng.copy(arb[0:E, A_WER + tc * 128:A_WER + (tc + 1) * 128],
                                  t.pbh[0:E, 0:128])
            kb.op("scalar", fwe, reads=[rs("pbh", 0, 1)],
                  writes=[rs("ARB", (A_WER + tc * 128) * 2,
                             (A_WER + (tc + 1) * 128) * 2)])

        if AL < 3:
            continue
        for e in range(E):
            def fbc(h, e=e):
                return h.eng.matmul(
                    pb[6][:], t.sel[:, e * 128:(e + 1) * 128],
                    arb[0:E, A_WER:A_WER + TT], start=True, stop=True)
            kb.op("tensor", fbc,
                  reads=[rs("sel"), rs("ARB", A_WER * 2, (A_WER + TT) * 2)],
                  writes=[rs("pb6")])

            wc0 = A_WEB + e * TT

            def fbe(h, wc0=wc0):
                return h.eng.copy(arb[:, wc0:wc0 + TT], pb[6][:])
            kb.op("scalar", fbe, reads=[rs("pb6")],
                  writes=[rs("ARB", wc0 * 2, (wc0 + TT) * 2)])

        if AL < 4:
            continue
        for wave in range(2):
            for kc in range(NKC):
                buf = kc % 2
                sc = A_GAS + buf * 1024

                def fld(h, kc=kc, sc=sc):
                    return h.dma(arb[:, sc:sc + 1024], t.s_tA[kc])
                kb.dma("sync", f"tI{buf}", fld, 1,
                       reads=[rs(t.s_tA.name, kc * 128 * 1024,
                                 (kc + 1) * 128 * 1024)],
                       writes=[rs("ARB", sc * 2, (sc + 1024) * 2)])
                for e in range(wave * 4, wave * 4 + 4):
                    bank = e % 4

                    def fmm(h, kc=kc, e=e, sc=sc, bank=bank):
                        return h.eng.matmul(
                            pb[bank][:],
                            arb[:, sc + e * 128:sc + (e + 1) * 128],
                            arb[:, kc * TT:(kc + 1) * TT],
                            start=(kc == 0), stop=(kc == NKC - 1))
                    kb.op("tensor", fmm,
                          reads=[rs("ARB", sc * 2, (sc + 1024) * 2),
                                 rs("ARB", kc * TT * 2, (kc + 1) * TT * 2)],
                          writes=[rs(f"pb{bank}")])
            for e in range(wave * 4, wave * 4 + 4):
                bank = e % 4
                dc0 = A_TG + e * TT

                def fev(h, bank=bank, dc0=dc0):
                    return h.eng.mul(arb[:, dc0:dc0 + TT], pb[bank][:], SCALING)
                kb.op("scalar", fev, reads=[rs(f"pb{bank}")],
                      writes=[rs("ARB", dc0 * 2, (dc0 + TT) * 2)])

        if TRUNC.get("dumpffn"):
            dump_arb(kb, t, A_YT, 8192, f"yT_tt{tt}")
            dump_t(kb, t, t.stats, "stats", 28, f"stats_tt{tt}")
            dump_arb(kb, t, A_WER, 512, f"weR_tt{tt}", rows=8)
            dump_arb(kb, t, A_WEB, 4096, f"web_tt{tt}")
            dump_arb(kb, t, A_TG, 4096, f"tg_tt{tt}", rows=128)
        for ic in range(NIC if "B" in TRUNC["parts"] else 0):
            buf = ic % 2
            wg0 = A_WG + buf * D
            wu0 = A_WU + buf * D
            gb0 = A_GB + buf * 1024
            ub0 = A_UB + buf * 1024
            da0 = A_DA + buf * TT

            lpq = A_TU + buf * 2048

            def fldw(h, ic=ic, wg0=wg0, wu0=wu0, lpq=lpq, da0=da0):
                h.dma(arb[:, wg0:wg0 + D], t.s_gate[ic])
                h.dma(arb[:, wu0:wu0 + D], t.s_up[ic])
                h.dma(arb[:, lpq:lpq + 2048], t.s_lq[ic])
                return h.dma(arb[:, da0:da0 + TT], t.s_dA[ic])
            kb.dma("sync", f"wB{buf}", fldw, 4,
                   reads=[rs(t.s_gate.name, ic * 128 * D, (ic + 1) * 128 * D),
                          rs(t.s_up.name, ic * 128 * D, (ic + 1) * 128 * D),
                          rs(t.s_lq.name, ic * 128 * 2048,
                             (ic + 1) * 128 * 2048),
                          rs(t.s_dA.name, ic * 128 * TT, (ic + 1) * 128 * TT)],
                   writes=[rs("ARB", wg0 * 2, (wg0 + D) * 2),
                           rs("ARB", wu0 * 2, (wu0 + D) * 2),
                           rs("ARB", lpq * 2, (lpq + 2048) * 2),
                           rs("ARB", da0 * 2, (da0 + TT) * 2)])

            for bank, w0 in ((0, wg0), (1, wu0)):
                for kc in range(NKC):
                    def fmm(h, bank=bank, w0=w0, kc=kc):
                        return h.eng.matmul(
                            pb[bank][:], arb[:, w0 + kc * 128:w0 + (kc + 1) * 128],
                            arb[:, kc * TT:(kc + 1) * TT],
                            start=(kc == 0), stop=(kc == NKC - 1))
                    kb.op("tensor", fmm,
                          reads=[rs("ARB", w0 * 2, (w0 + D) * 2),
                                 rs("ARB", kc * TT * 2, (kc + 1) * TT * 2)],
                          writes=[rs(f"pb{bank}")])
            for bank, b0 in ((0, A_BG), (1, A_BU)):
                def fev(h, bank=bank, b0=b0):
                    return h.eng.copy(arb[:, b0:b0 + TT], pb[bank][:])
                kb.op("scalar", fev, reads=[rs(f"pb{bank}")],
                      writes=[rs("ARB", b0 * 2, (b0 + TT) * 2)])

            def emit_hatd(e, ic=ic, da0=da0):
                wc = WEHS(e)

                def fha(h, e=e, wc=wc):
                    return h.eng.matmul(pb[2][:], t.identbf[:],
                                        arb[:, wc:wc + TT],
                                        start=(e == 0), stop=(e == E - 1))
                kb.op("tensor", fha,
                      reads=[rs("identbf"), rs("ARB", wc * 2, (wc + TT) * 2)],
                      writes=[rs("pb2")])

                bank = 3 + e // 2
                p0 = (e % 2) * 64

                def ftd(h, e=e, ic=ic, da0=da0, bank=bank, p0=p0, wc=wc):
                    return h.eng.matmul(
                        pb[bank][p0:p0 + 64, :],
                        arb[:, da0 + e * R:da0 + (e + 1) * R],
                        arb[:, wc:wc + TT],
                        start=(ic == 0), stop=(ic == NIC - 1),
                        skip_group_check=True)
                kb.op("tensor", ftd,
                      reads=[rs("ARB", (da0 + e * R) * 2, (da0 + (e + 1) * R) * 2),
                             rs("ARB", wc * 2, (wc + TT) * 2)],
                      writes=[rs(f"pb{bank}", p0, p0 + 64)])

            for e in range(E):
                sfc = (e % 2) * TT
                qa = lpq + e * 256
                def fdg(h, e=e, qa=qa):
                    return h.eng.matmul(
                        pb[0][:], arb[:, qa:qa + 128],
                        arb[:, A_TG + e * TT:A_TG + (e + 1) * TT],
                        start=True, stop=False)
                kb.op("tensor", fdg,
                      reads=[rs("ARB", qa * 2, (qa + 128) * 2),
                             rs("ARB", (A_TG + e * TT) * 2, (A_TG + (e + 1) * TT) * 2)],
                      writes=[rs("pb0")])

                def fig(h):
                    return h.eng.matmul(pb[0][:], t.identbf[:],
                                        arb[:, A_BG:A_BG + TT],
                                        start=False, stop=True)
                kb.op("tensor", fig,
                      reads=[rs("identbf"), rs("ARB", A_BG * 2, (A_BG + TT) * 2)],
                      writes=[rs("pb0")])

                if e == 0 and ic == 0 and tt == 0 and TRUNC.get("dumpg"):
                    def fgd(h):
                        return h.eng.copy(t.sq[:], pb[0][:])
                    kb.op("scalar", fgd, reads=[rs("pb0")], writes=[rs("sq")])
                    dump_t(kb, t, t.sq, "sq", TT, "g_e0")

                def fsl(h, sfc=sfc):
                    return h.eng.activation(t.sf32[:, sfc:sfc + TT], pb[0][:],
                                            AF.Silu)
                kb.op("scalar", fsl, reads=[rs("pb0")],
                      writes=[rs("sf32", sfc, sfc + TT)])

                def fdu(h, e=e, qa=qa):
                    return h.eng.matmul(
                        pb[1][:], arb[:, qa + 128:qa + 256],
                        arb[:, A_TG + e * TT:A_TG + (e + 1) * TT],
                        start=True, stop=False)
                kb.op("tensor", fdu,
                      reads=[rs("ARB", (qa + 128) * 2, (qa + 256) * 2),
                             rs("ARB", (A_TG + e * TT) * 2, (A_TG + (e + 1) * TT) * 2)],
                      writes=[rs("pb1")])

                def fiu(h):
                    return h.eng.matmul(pb[1][:], t.identbf[:],
                                        arb[:, A_BU:A_BU + TT],
                                        start=False, stop=True)
                kb.op("tensor", fiu,
                      reads=[rs("identbf"), rs("ARB", A_BU * 2, (A_BU + TT) * 2)],
                      writes=[rs("pb1")])

                if e > 1:
                    emit_hatd(e - 2)

                def fh(h, sfc=sfc):
                    return h.eng.tensor_tensor(
                        t.hf32[:], t.sf32[:, sfc:sfc + TT], pb[1][:],
                        AluOpType.mult)
                kb.op("vector", fh, reads=[rs("sf32", sfc, sfc + TT), rs("pb1")],
                      writes=[rs("hf32")])

                def fweh(h, e=e):
                    wc = WEHS(e)
                    return h.eng.tensor_tensor(
                        arb[:, wc:wc + TT], t.hf32[:],
                        arb[:, A_WEB + e * TT:A_WEB + (e + 1) * TT], AluOpType.mult)
                kb.op("vector", fweh,
                      reads=[rs("hf32"),
                             rs("ARB", (A_WEB + e * TT) * 2, (A_WEB + (e + 1) * TT) * 2)],
                      writes=[rs("ARB", WEHS(e) * 2, (WEHS(e) + TT) * 2)])

                if e == 0 and ic == 0 and tt == 0 and TRUNC.get("dumpg"):
                    def fud(h):
                        return h.eng.copy(t.sq[:], pb[1][:])
                    kb.op("scalar", fud, reads=[rs("pb1")], writes=[rs("sq")])
                    dump_t(kb, t, t.sq, "sq", TT, "u_e0")
                    dump_arb(kb, t, WEHS(0), TT, "weh_e0")
                    dump_arb(kb, t, A_BG, TT, "bg_ic0")
                    dump_arb(kb, t, A_BU, TT, "bu_ic0")
            emit_hatd(E - 2)
            emit_hatd(E - 1)

            def fhe(h, ic=ic):
                return h.eng.copy(arb[:, A_H + ic * TT:A_H + (ic + 1) * TT], pb[2][:])
            kb.op("scalar", fhe, reads=[rs("pb2")],
                  writes=[rs("ARB", (A_H + ic * TT) * 2, (A_H + (ic + 1) * TT) * 2)])

        if TRUNC.get("dumpffn"):
            dump_arb(kb, t, A_H, NIC * TT, f"H_tt{tt}")
        if "C" not in TRUNC["parts"]:
            continue
        for e in range(E):
            bank = 3 + e // 2
            p0 = (e % 2) * 64
            dc0 = A_TG + e * TT

            def fte(h, bank=bank, p0=p0, dc0=dc0):
                return h.eng.mul(arb[0:64, dc0:dc0 + TT], pb[bank][p0:p0 + 64, :],
                                 SCALING)
            kb.op("scalar", fte, reads=[rs(f"pb{bank}", p0, p0 + 64)],
                  writes=[rs("ARB", dc0 * 2, (dc0 + TT) * 2)])
        if TRUNC.get("dumpffn"):
            dump_arb(kb, t, A_TG, 4096, f"td_tt{tt}", rows=64)
        def fdb(h):
            last = None
            for e in range(E):
                c0 = C_DBLOC(e)
                last = h.dma(arb[0:64, c0:c0 + D], t.s_dB[e])
            return last
        kb.dma("sync", "dB", fdb, E, reads=[rs(t.s_dB.name)],
               writes=[rs("ARB", A_WG * 2, (A_WG + 6 * D) * 2),
                       rs("ARB", A_WEB * 2, (A_WEB + 2 * D) * 2)])
        def fxl(h, tt=tt):
            last = None
            for tc in range(4):
                row0 = tt * TT + tc * 128
                last = h.dma(t.xt[:, tc * D:(tc + 1) * D],
                             x_src[row0:row0 + 128, :])
            return last
        kb.dma("scalar", f"xr{tag}", fxl, 4,
               reads=[rs(x_src.name, tt * TT * D, (tt + 1) * TT * D)],
               writes=[rs("xt", 0, 4 * D * 4)])

        for fg in range(4):
            for ic in range(NIC):
                buf = ic % 4
                dn0 = A_DN + buf * TT

                def fdl(h, ic=ic, fg=fg, dn0=dn0):
                    return h.dma(arb[:, dn0:dn0 + TT],
                                 t.s_down[ic * 128:(ic + 1) * 128,
                                          fg * TT:(fg + 1) * TT])
                kb.dma("sync", f"dn{buf}", fdl, 1,
                       reads=[rs(t.s_down.name, ic * 128 * D,
                                 (ic + 1) * 128 * D)],
                       writes=[rs("ARB", dn0 * 2, (dn0 + TT) * 2)])
                for tc in range(4):
                    def fmm(h, ic=ic, tc=tc, dn0=dn0):
                        return h.eng.matmul(
                            pb[tc][:],
                            arb[:, A_H + ic * TT + tc * 128:A_H + ic * TT + (tc + 1) * 128],
                            arb[:, dn0:dn0 + TT],
                            start=(ic == 0), stop=False)
                    kb.op("tensor", fmm,
                          reads=[rs("ARB", (A_H + ic * TT) * 2, (A_H + (ic + 1) * TT) * 2),
                                 rs("ARB", dn0 * 2, (dn0 + TT) * 2)],
                          writes=[rs(f"pb{tc}")])
            for e in range(E):
                tdc = A_TG + e * TT
                dbc = C_DBLOC(e) + fg * TT
                for tc in range(4):
                    def fmm(h, e=e, tc=tc, tdc=tdc, dbc=dbc):
                        return h.eng.matmul(
                            pb[tc][:],
                            arb[0:64, tdc + tc * 128:tdc + (tc + 1) * 128],
                            arb[0:64, dbc:dbc + TT],
                            start=False, stop=(e == E - 1))
                    kb.op("tensor", fmm,
                          reads=[rs("ARB", tdc * 2, (tdc + TT) * 2),
                                 rs("ARB", dbc * 2, (dbc + TT) * 2)],
                          writes=[rs(f"pb{tc}")])
            for tc in range(4):
                _stt_res_add(kb, t, pb[tc], f"pb{tc}", tc * D, fg)
        for tc in range(4):
            row0 = tt * TT + tc * 128

            def fxs(h, tc=tc, row0=row0):
                return h.dma(x_dst[row0:row0 + 128, :], t.xt[:, tc * D:(tc + 1) * D])
            kb.dma("gpsimd", f"xs{tag}", fxs, 1,
                   reads=[rs("xt", tc * D * 4, (tc + 1) * D * 4)],
                   writes=[rs(x_dst.name, row0 * D, (row0 + 128) * D)])

    def ffence(h):
        return h.eng.memset(t.fencet[:], 0.0)
    kb.fence("gpsimd", ffence, [rs(x_dst.name)])


def _attn(kb, t, x_src, x_dst):
    pb = t.pb
    arb = t.ARB

    for tt in range(NTT):
        for tc in range(4):
            _ln_tile(kb, t, x_src, tt, tc, "a", True)
        _ln_finish(kb, t)
        for tc in range(4):
            slot = tc
            _ln_norm(kb, t, tc, slot)
            for grp in range(2):
                for g8 in range(8):
                    fc = grp * 8 + g8

                    def ftr(h, fc=fc, g8=g8):
                        return h.eng.transpose(
                            t.pbh[:, g8 * 128:(g8 + 1) * 128],
                            t.ybf[:, fc * 128:(fc + 1) * 128], t.identbf[:])
                    kb.op("tensor", ftr, reads=[rs("ybf"), rs("identbf")],
                          writes=[rs("pbh", g8, g8 + 1)])

                yg0 = A_Y2T + grp * 8 * TT + tc * 128

                def fe2(h, yg0=yg0):
                    return h.eng.copy(
                        bass.AP(arb, yg0, [[ARBW, 128], [TT, 8], [1, 128]]),
                        t.pbh[:].rearrange("p (a b) -> p a b", a=8))
                kb.op("scalar", fe2, reads=[rs("pbh", 0, 8)],
                      writes=[rs("ARB", yg0 * 2, (yg0 + 8 * TT) * 2)])

        for oc in range(NOC3):
            buf = oc % 2
            wq0 = A_WQ + buf * D
            bank = buf

            def fldw(h, oc=oc, wq0=wq0):
                return h.dma(arb[:, wq0:wq0 + D], t.s_qkv[oc])
            kb.dma("sync", f"qw{buf}", fldw, 1, reads=[rs(t.s_qkv.name)],
                   writes=[rs("ARB", wq0 * 2, (wq0 + D) * 2)])
            for kc in range(NKC):
                def fmm(h, kc=kc, wq0=wq0, bank=bank):
                    return h.eng.matmul(
                        pb[bank][:], arb[:, wq0 + kc * 128:wq0 + (kc + 1) * 128],
                        arb[:, A_Y2T + kc * TT:A_Y2T + (kc + 1) * TT],
                        start=(kc == 0), stop=(kc == NKC - 1))
                kb.op("tensor", fmm,
                      reads=[rs("ARB", wq0 * 2, (wq0 + D) * 2),
                             rs("ARB", (A_Y2T + kc * TT) * 2, (A_Y2T + (kc + 1) * TT) * 2)],
                      writes=[rs(f"pb{bank}")])

            h_ = oc % NH
            kind = oc // NH
            qp0 = A_QP + buf * TT

            def fev(h, bank=bank, qp0=qp0):
                return h.eng.copy(arb[:, qp0:qp0 + TT], pb[bank][:])
            kb.op("scalar", fev, reads=[rs(f"pb{bank}")],
                  writes=[rs("ARB", qp0 * 2, (qp0 + TT) * 2)])

            if kind < 2:
                def frot(h, qp0=qp0):
                    return h.eng.matmul(pb[2][:], t.ropeR[:],
                                        arb[:, qp0:qp0 + TT],
                                        start=True, stop=True)
                kb.op("tensor", frot,
                      reads=[rs("ropeR"), rs("ARB", qp0 * 2, (qp0 + TT) * 2)],
                      writes=[rs("pb2")])

                rt0 = A_RT1 + buf * TT

                def ft1(h, qp0=qp0, rt0=rt0, tt=tt):
                    return h.eng.tensor_tensor(
                        arb[:, rt0:rt0 + TT], arb[:, qp0:qp0 + TT],
                        t.cosb[:, tt * TT:(tt + 1) * TT], AluOpType.mult)
                kb.op("vector", ft1,
                      reads=[rs("ARB", qp0 * 2, (qp0 + TT) * 2), rs("cosb")],
                      writes=[rs("ARB", rt0 * 2, (rt0 + TT) * 2)])

                def ft2(h, qp0=qp0, tt=tt):
                    return h.eng.tensor_tensor(
                        arb[:, qp0:qp0 + TT], pb[2][:],
                        t.sinb[:, tt * TT:(tt + 1) * TT], AluOpType.mult)
                kb.op("vector", ft2, reads=[rs("pb2"), rs("sinb")],
                      writes=[rs("ARB", qp0 * 2, (qp0 + TT) * 2)])

                def ft3(h, qp0=qp0, rt0=rt0):
                    return h.eng.tensor_tensor(
                        arb[:, rt0:rt0 + TT], arb[:, rt0:rt0 + TT],
                        arb[:, qp0:qp0 + TT], AluOpType.add)
                kb.op("vector", ft3,
                      reads=[rs("ARB", rt0 * 2, (rt0 + TT) * 2),
                             rs("ARB", qp0 * 2, (qp0 + TT) * 2)],
                      writes=[rs("ARB", rt0 * 2, (rt0 + TT) * 2)])

                row0 = h_ * 128 + (0 if kind == 0 else D)

                def fst(h, rt0=rt0, row0=row0, tt=tt):
                    return h.dma(t.qk_in[row0:row0 + 128, tt * TT:(tt + 1) * TT],
                                 arb[:, rt0:rt0 + TT])
                kb.dma("gpsimd", "qkst", fst, 1,
                       reads=[rs("ARB", rt0 * 2, (rt0 + TT) * 2)],
                       writes=[rs("qk_in", (row0 * NTOK + tt * TT) * 2,
                                  ((row0 + 128) * NTOK) * 2)])
            else:
                vsb0 = A_VSB + buf * TT
                for lt in range(4):
                    def fvt(h, lt=lt, qp0=qp0):
                        return h.eng.transpose(
                            t.pbh[:, lt * 128:(lt + 1) * 128],
                            arb[:, qp0 + lt * 128:qp0 + (lt + 1) * 128],
                            t.identbf[:])
                    kb.op("tensor", fvt,
                          reads=[rs("ARB", qp0 * 2, (qp0 + TT) * 2),
                                 rs("identbf")],
                          writes=[rs("pbh", lt, lt + 1)])

                def fve(h, vsb0=vsb0):
                    return h.eng.copy(
                        arb[:, vsb0:vsb0 + TT], t.pbh[:, 0:TT])
                kb.op("scalar", fve, reads=[rs("pbh", 0, 4)],
                      writes=[rs("ARB", vsb0 * 2, (vsb0 + TT) * 2)])

                def fst(h, vsb0=vsb0, h_=h_, tt=tt):
                    return h.dma(
                        t.v_in[:, h_ * NTOK + tt * TT:h_ * NTOK + (tt + 1) * TT],
                        arb[:, vsb0:vsb0 + TT])
                kb.dma("gpsimd", "vst", fst, 1,
                       reads=[rs("ARB", vsb0 * 2, (vsb0 + TT) * 2)],
                       writes=[rs("v_in", (h_ * NTOK + tt * TT) * 2,
                                  (h_ * NTOK + (tt + 1) * TT) * 2)])

    if hasattr(t, "qk_dbg"):
        def fqkd(h):
            last = None
            for rr in range(8):
                last = h.dma(t.qk_dbg[rr * 512:(rr + 1) * 512, :],
                             t.qk_in[rr * 512:(rr + 1) * 512, :])
            return last
        kb.dma("gpsimd", "qkdbg", fqkd, 8, reads=[rs("qk_in")], writes=[rs("qk_dbg")])

    def fcc1(h):
        return h.eng.collective_compute(
            "AllGather", mybir.AluOpType.bypass,
            replica_groups=[[0, 1, 2, 3], [4, 5, 6, 7]],
            ins=[t.qk_in.ap().opt()], outs=[t.qk_out.ap().opt()])
    kb.op("gpsimd", fcc1, reads=[rs("qk_in")], writes=[rs("qk_out")])

    def fcc2(h):
        return h.eng.collective_compute(
            "AllGather", mybir.AluOpType.bypass,
            replica_groups=[[0, 1, 2, 3], [4, 5, 6, 7]],
            ins=[t.v_in.ap().opt()], outs=[t.v_out.ap().opt()])
    kb.op("gpsimd", fcc2, reads=[rs("v_in")], writes=[rs("v_out")])

    OFF_K = 0
    OFF_V = 4
    OFF_Q = 8
    OFF_O = 12

    def emit_oag(hh):
        def fcc(h, hh=hh):
            return h.eng.collective_compute(
                "AllGather", mybir.AluOpType.bypass,
                replica_groups=[[0, 1, 2, 3], [4, 5, 6, 7]],
                ins=[t.o_in.ap()[hh * 128:(hh + 1) * 128, :].opt()],
                outs=[t.o_out.ap()[hh * 512:(hh + 1) * 512, :].opt()])
        kb.op("gpsimd", fcc,
              reads=[rs("o_in", hh * 128 * 8192 * 2, (hh + 1) * 128 * 8192 * 2)],
              writes=[rs("o_out", hh * 512 * 8192 * 2,
                         (hh + 1) * 512 * 8192 * 2)])

    for hh in range(4):
        kbuf = hh % 2
        ks0 = A_KS + kbuf * 8192
        vs0 = A_VS + kbuf * 8192
        qs0 = A_QS + kbuf * 8192
        def fkl(h, hh=hh, ks0=ks0):
            h.eng.reg_load(t.roff, t.offt[0:1, OFF_K + hh:OFF_K + hh + 1])
            return h.dma(
                bass.AP(arb, ks0, [[ARBW, 128], [NTOK, 4], [1, NTOK]]),
                bass.AP(t.qk_out, t.roff, [[NTOK, 128], [2 * D * NTOK, 4], [1, NTOK]]))
        kb.dma("gpsimd", f"kld{kbuf}", fkl, 1,
               reads=[rs("qk_out"), rs("offt")],
               writes=[rs("ARB", ks0 * 2, (ks0 + 4 * NTOK) * 2)])

        def fvl(h, hh=hh, vs0=vs0):
            h.eng.reg_load(t.roff, t.offt[0:1, OFF_V + hh:OFF_V + hh + 1])
            return h.dma(
                bass.AP(arb, vs0, [[ARBW, 128], [NTOK, 4], [1, NTOK]]),
                bass.AP(t.v_out, t.roff,
                        [[NH * NTOK, 128], [128 * NH * NTOK, 4], [1, NTOK]]))
        kb.dma("gpsimd", f"vld{kbuf}", fvl, 1,
               reads=[rs("v_out"), rs("offt")],
               writes=[rs("ARB", vs0 * 2, (vs0 + 4 * NTOK) * 2)])

        def fql(h, hh=hh, qs0=qs0):
            h.eng.reg_load(t.roff, t.offt[0:1, OFF_Q + hh:OFF_Q + hh + 1])
            return h.dma(
                bass.AP(arb, qs0, [[ARBW, 128], [NTOK, 4], [1, NTOK]]),
                bass.AP(t.qk_out, t.roff, [[NTOK, 128], [2 * D * NTOK, 4], [1, NTOK]]))
        kb.dma("gpsimd", f"qld{kbuf}", fql, 1, reads=[rs("qk_out"), rs("offt")],
               writes=[rs("ARB", qs0 * 2, (qs0 + 4 * NTOK) * 2)])

        if hh > 0:
            emit_oag(hh - 1)
        for qt in range(16):
            g_q = qt // 2
            r_q, ltb_q = _chunk_owner(g_q)
            qcol = qs0 + r_q * NTOK + (0 if ltb_q == 0 else 1024) + (qt % 2) * TT

            nkc_q = 4 * qt + 4
            lbank = 3 if qt % 2 == 0 else 5

            def emit_lav(kc, vs0=vs0, lbank=lbank, nkc_q=nkc_q):
                g = kc // 8
                r, ltb = _chunk_owner(g)
                scol = r * NTOK + (ltb + kc % 8) * 128
                pt0 = A_PT + (kc % 3) * TT
                last = (kc == nkc_q - 1)

                def fl(h, kc=kc, pt0=pt0, lbank=lbank, last=last):
                    return h.eng.matmul(
                        pb[lbank][:], t.ones128bf[:], arb[:, pt0:pt0 + TT],
                        start=(kc == 0), stop=last)
                kb.op("tensor", fl,
                      reads=[rs("ones128bf"), rs("ARB", pt0 * 2, (pt0 + TT) * 2)],
                      writes=[rs(f"pb{lbank}")])

                def fav(h, scol=scol, vs0=vs0, pt0=pt0, kc=kc, last=last):
                    return h.eng.matmul(
                        pb[2][:], arb[:, vs0 + scol:vs0 + scol + 128],
                        arb[:, pt0:pt0 + TT], start=(kc == 0), stop=last)
                kb.op("tensor", fav,
                      reads=[rs("ARB", (vs0 + scol) * 2, (vs0 + scol + 128) * 2),
                             rs("ARB", pt0 * 2, (pt0 + TT) * 2)],
                      writes=[rs("pb2")])

            for kc in range(nkc_q):
                g = kc // 8
                r, ltb = _chunk_owner(g)
                scol = r * NTOK + (ltb + kc % 8) * 128
                sbank = kc % 2
                pt0 = A_PT + (kc % 3) * TT

                def fsm(h, sbank=sbank, scol=scol, ks0=ks0, qcol=qcol):
                    return h.eng.matmul(
                        pb[sbank][:], arb[:, ks0 + scol:ks0 + scol + 128],
                        arb[:, qcol:qcol + TT], start=True, stop=True)
                kb.op("tensor", fsm,
                      reads=[rs("ARB", (ks0 + scol) * 2, (ks0 + scol + 128) * 2),
                             rs("ARB", qcol * 2, (qcol + TT) * 2)],
                      writes=[rs(f"pb{sbank}")])

                def fex(h, sbank=sbank, pt0=pt0):
                    return h.eng.activation(
                        arb[:, pt0:pt0 + TT], pb[sbank][:], AF.Exp,
                        bias=t.fconst[:, 1:2], scale=INV_SQRT_HD)
                kb.op("scalar", fex, reads=[rs(f"pb{sbank}"), rs("fconst")],
                      writes=[rs("ARB", pt0 * 2, (pt0 + TT) * 2)])

                if hh == 0 and qt == 0 and kc == 0 and TRUNC.get("dumps"):
                    dump_arb(kb, t, pt0, TT, "pt_q0k0_premask")
                m = kc - 4 * qt
                if m >= 0:
                    def fmk(h, pt0=pt0, m=m):
                        return h.eng.tensor_tensor(
                            arb[:, pt0:pt0 + TT], arb[:, pt0:pt0 + TT],
                            t.masks[:, m * TT:(m + 1) * TT], AluOpType.mult)
                    kb.op("vector", fmk,
                          reads=[rs("ARB", pt0 * 2, (pt0 + TT) * 2), rs("masks")],
                          writes=[rs("ARB", pt0 * 2, (pt0 + TT) * 2)])

                if hh == 0 and qt == 0 and kc == 0 and TRUNC.get("dumps"):
                    dump_arb(kb, t, pt0, TT, "pt_q0k0_postmask")
                    dump_arb(kb, t, ks0, TT, "k_h0")
                    dump_arb(kb, t, vs0, TT, "v_h0")
                    dump_arb(kb, t, qcol, TT, "q_h0")

                if kc > 0:
                    emit_lav(kc - 1)
            emit_lav(nkc_q - 1)

            def frc(h, lbank=lbank):
                return h.eng.reciprocal(t.rl[:], pb[lbank][:])
            kb.op("vector", frc, reads=[rs(f"pb{lbank}")], writes=[rs("rl")])

            os0 = A_OST if qt % 2 == 0 else 51200

            def foe(h, os0=os0):
                return h.eng.tensor_tensor(
                    arb[:, os0:os0 + TT], pb[2][:], t.rl[:], AluOpType.mult)
            kb.op("vector", foe, reads=[rs("pb2"), rs("rl")],
                  writes=[rs("ARB", os0 * 2, (os0 + TT) * 2)])

            row0 = hh * 128

            def fos(h, row0=row0, qt=qt, os0=os0):
                return h.dma(t.o_in[row0:row0 + 128, qt * TT:(qt + 1) * TT],
                             arb[:, os0:os0 + TT])
            kb.dma("gpsimd", "ost", fos, 1,
                   reads=[rs("ARB", os0 * 2, (os0 + TT) * 2)],
                   writes=[rs("o_in", (row0 * 8192 + qt * TT) * 2,
                              ((row0 + 128) * 8192) * 2)])

    if hasattr(t, "o_dbg"):
        def fod(h):
            last = None
            for rr in range(2):
                last = h.dma(t.o_dbg[rr * 256:(rr + 1) * 256, :],
                             t.o_in[rr * 256:(rr + 1) * 256, :])
            return last
        kb.dma("gpsimd", "odbg", fod, 2, reads=[rs("o_in")], writes=[rs("o_dbg")])

    emit_oag(3)

    def fol(h):
        last = None
        for r in range(4):
            for half in range(2):
                oi = OFF_O + r * 2 + half
                c0 = A_OT + (4 * r) * NTOK + half * 1024
                h.eng.reg_load(t.roff, t.offt[0:1, oi:oi + 1])
                last = h.dma(
                    bass.AP(arb, c0, [[ARBW, 128], [NTOK, 4], [1, 1024]]),
                    bass.AP(t.o_out, t.roff,
                            [[8192, 128], [512 * 8192, 4], [1, 1024]]))
        return last
    kb.dma("gpsimd", "old", fol, 8, reads=[rs("o_out"), rs("offt")],
           writes=[rs("ARB", A_OT * 2, (A_OT + NH * NTOK) * 2)])
    def fwl(h):
        last = None
        for fc in range(NKC):
            c0 = A_WO + fc * D
            last = h.dma(arb[:, c0:c0 + D], t.s_wo[fc * 128:(fc + 1) * 128, :])
        return last
    kb.dma("sync", "wol", fwl, NKC, reads=[rs(t.s_wo.name)],
           writes=[rs("ARB", A_WO * 2, (A_WO + NKC * D) * 2)])

    for tc16 in range(16):
        row0 = tc16 * 128
        xslot = tc16 % 4

        def fxl(h, row0=row0, xslot=xslot):
            return h.dma(t.xt[:, xslot * D:(xslot + 1) * D],
                         x_src[row0:row0 + 128, :])
        kb.dma("scalar", f"xra{xslot}", fxl, 1,
               reads=[rs(x_src.name, row0 * D, (row0 + 128) * D)],
               writes=[rs("xt", xslot * D * 4, (xslot + 1) * D * 4)])

        for h_ in range(NH):
            oc0 = A_OT + h_ * NTOK + tc16 * 128
            wc0 = A_WO + h_ * D
            for fg in range(4):
                def fmm(h, oc0=oc0, wc0=wc0, fg=fg, h_=h_):
                    return h.eng.matmul(
                        pb[fg][:], arb[:, oc0:oc0 + 128],
                        arb[:, wc0 + fg * TT:wc0 + (fg + 1) * TT],
                        start=(h_ == 0), stop=(h_ == NH - 1))
                kb.op("tensor", fmm,
                      reads=[rs("ARB", oc0 * 2, (oc0 + 128) * 2),
                             rs("ARB", (wc0 + fg * TT) * 2, (wc0 + (fg + 1) * TT) * 2)],
                      writes=[rs(f"pb{fg}")])
        for fg in range(4):
            _stt_res_add(kb, t, pb[fg], f"pb{fg}", xslot * D, fg)

        def fxs(h, row0=row0, xslot=xslot):
            return h.dma(x_dst[row0:row0 + 128, :],
                         t.xt[:, xslot * D:(xslot + 1) * D])
        kb.dma("gpsimd", "xsa", fxs, 1,
               reads=[rs("xt", xslot * D * 4, (xslot + 1) * D * 4)],
               writes=[rs(x_dst.name, row0 * D, (row0 + 128) * D)])

    def ffence2(h):
        return h.eng.memset(t.fencet[:], 0.0)
    kb.fence("gpsimd", ffence2, [rs(x_dst.name)])


def _run(inputs, debug=False, phases=("ffn1", "attn", "ffn2")):
    shared = _prep_shared(inputs)
    in_maps = []
    for core in range(8):
        m = dict(shared)
        m.update(_prep_core(inputs, core))
        in_maps.append(m)
    nc = build_nc(debug=debug, phases=phases)
    res = run_bass_kernel_spmd(nc, in_maps, core_ids=list(range(8)))
    return res


def kernel(**inputs):
    res = _run(inputs)
    out = np.zeros((2, 8192, 2048), np.float32)
    for core in range(8):
        b, j = core // 4, core % 4
        o = np.asarray(res.results[core]["OUT"])
        out[b, j * 1024:(j + 1) * 1024, :] = o[0:1024, :]
        out[b, (7 - j) * 1024:(8 - j) * 1024, :] = o[1024:2048, :]
    return out


# version 11
# speedup vs baseline: 1.1423x; 1.0085x over previous
import math
import numpy as np
import ml_dtypes
import concourse.bass as bass
import concourse.mybir as mybir
from concourse.bass_utils import run_bass_kernel_spmd
from concourse.alu_op_type import AluOpType
from contextlib import ExitStack

F32 = mybir.dt.float32
BF16 = mybir.dt.bfloat16
I32 = mybir.dt.int32
AF = mybir.ActivationFunctionType
AX = mybir.AxisListType.X
bfloat16 = ml_dtypes.bfloat16

D = 2048
II = 7168
E = 8
R = 64
NH = 16
HD = 128
NTOK = 2048
TT = 512
NTT = NTOK // TT
NKC = D // 128
NIC = II // 128
NOC3 = 3 * NH
SCALING = 16.0 / 64.0
EXP_SHIFT = -12.0
INV_SQRT_HD = 1.0 / math.sqrt(HD)
LN_EPS = 1e-5
BIGNEG = -1.0e9

ENGINES = ("sync", "scalar", "vector", "tensor", "gpsimd")
TRUNC = {"ffn_tt": NTT, "parts": "ABC", "p0cast": True, "p0const": True}


class KB:
    def __init__(self):
        self.eng_ops = {e: [] for e in ENGINES}
        self.res = {}
        self.slot_count = {}
        self.marked = {e: set() for e in ENGINES}
        self.scope = None

    def _resolve(self, myeng, reads, writes):
        needs = set()
        for (name, lo, hi) in reads:
            for en in self.res.get(name, ()):
                if en[0] < hi and lo < en[1] and en[2] is not None:
                    needs.add(en[2])
        for (name, lo, hi) in writes:
            for en in self.res.get(name, ()):
                if en[0] < hi and lo < en[1]:
                    if en[2] is not None:
                        needs.add(en[2])
                    needs.update(en[3])
        return {n for n in needs
                if not (n[0] == "E" and n[1] == myeng and myeng == "tensor")}

    def _commit(self, reads, writes, ev):
        for (name, lo, hi) in reads:
            entries = self.res.setdefault(name, [])
            for en in entries:
                if en[0] == lo and en[1] == hi:
                    en[3].add(ev)
                    break
            else:
                entries.append([lo, hi, None, {ev}])
        for (name, lo, hi) in writes:
            entries = self.res.setdefault(name, [])
            keep = [en for en in entries if not (en[0] < hi and lo < en[1])]
            keep.append([lo, hi, ev, set()])
            self.res[name] = keep

    def op(self, eng, fn, reads=(), writes=()):
        needs = self._resolve(eng, reads, writes)
        idx = len(self.eng_ops[eng])
        ev = ("E", eng, idx)
        self._commit(reads, writes, ev)
        for n in needs:
            if n[0] == "E":
                self.marked[n[1]].add(n[2])
        self.eng_ops[eng].append(
            {"eng": eng, "fn": fn, "slot": None, "idx": idx,
             "needs": needs, "scope": self.scope})

    def dma(self, eng, slot, fn, ndma, reads=(), writes=()):
        needs = self._resolve(eng, reads, writes)
        cnt = self.slot_count.get(slot, 0) + 16 * ndma
        self.slot_count[slot] = cnt
        idx = len(self.eng_ops[eng])
        ev = ("S", slot, cnt)
        self._commit(reads, writes, ev)
        for n in needs:
            if n[0] == "E":
                self.marked[n[1]].add(n[2])
        self.eng_ops[eng].append(
            {"eng": eng, "fn": fn, "slot": slot, "idx": idx,
             "needs": needs, "scope": self.scope})

    def fence(self, eng, fn, resources):
        self.op(eng, fn, reads=resources, writes=resources)

    def emit(self, nc, block, sems_eng, sems_slot):
        val_at = {}
        for e in ENGINES:
            for rank, i in enumerate(sorted(self.marked[e])):
                val_at[(e, i)] = rank + 1

        for e in ENGINES:
            ops = self.eng_ops[e]
            if not ops:
                continue

            def make_section(e, ops):
                def section(engproxy):
                    know = {}
                    for op in ops:
                        for n in sorted(op["needs"], key=repr):
                            if n[0] == "E":
                                sem, val, key = sems_eng[n[1]], val_at[(n[1], n[2])], ("E", n[1])
                            else:
                                sem, val, key = sems_slot[n[1]], n[2], ("S", n[1])
                            if know.get(key, 0) >= val:
                                continue
                            engproxy.wait_ge(sem, val)
                            know[key] = val
                        h = EmitCtx(engproxy, sems_slot.get(op["slot"]))
                        last = op["fn"](h)
                        if op["slot"] is None and op["idx"] in self.marked[e]:
                            assert last is not None, (e, op["idx"], op["scope"])
                            last.then_inc(sems_eng[e], 1)
                return section
            getattr(block, e)(make_section(e, ops))


class EmitCtx:
    def __init__(self, eng, slot_sem):
        self.eng = eng
        self.slot_sem = slot_sem

    def dma(self, dst, src, **kw):
        inst = self.eng.dma_start(dst, src, **kw)
        inst.then_inc(self.slot_sem, 16)
        return inst


def rs(t, lo=0, hi=1 << 40):
    return (t, lo, hi)


def dump_arb(kb, t, c0, n, label, rows=128):
    if not hasattr(t, "DBG") or not TRUNC.get("dumps"):
        return
    off = t.dbgoff[0]
    assert off + n <= 65536, "DBG overflow"
    t.dbgoff[0] = off + n
    TRUNC.setdefault("dumplog", []).append((label, off, n, rows))

    def f(h, off=off, c0=c0, n=n, rows=rows):
        return h.dma(t.DBG[0:rows, off:off + n], t.ARB[0:rows, c0:c0 + n])
    kb.dma("gpsimd", "dbgd", f, 1,
           reads=[rs("ARB", c0 * 2, (c0 + n) * 2)], writes=[rs("DBG")])


def dump_t(kb, t, tile, res_name, n, label, rows=128):
    if not hasattr(t, "DBG") or not TRUNC.get("dumps"):
        return
    off = t.dbgoff[0]
    t.dbgoff[0] = off + n
    TRUNC.setdefault("dumplog", []).append((label, off, n, rows))

    def f(h, off=off, n=n, rows=rows, tile=tile):
        return h.dma(t.DBG[0:rows, off:off + n], tile[0:rows, 0:n])
    kb.dma("gpsimd", "dbgd", f, 1, reads=[rs(res_name)], writes=[rs("DBG")])


def _prep_shared(inp):
    d = {}
    wqkv = np.asarray(inp["L0_attn_w_qkv"], np.float32)
    d["p_qkv"] = np.ascontiguousarray(
        wqkv.reshape(NKC, 128, NOC3, 128).transpose(2, 1, 0, 3).reshape(NOC3, 128, D))
    d["p_wo"] = np.asarray(inp["L0_attn_w_o"], np.float32)
    gate = np.asarray(inp["L0_ffn_base_gate"], np.float32)
    up = np.asarray(inp["L0_ffn_base_up"], np.float32)
    d["p_gate"] = np.ascontiguousarray(
        gate.reshape(NKC, 128, NIC, 128).transpose(2, 1, 0, 3).reshape(NIC, 128, D))
    d["p_up"] = np.ascontiguousarray(
        up.reshape(NKC, 128, NIC, 128).transpose(2, 1, 0, 3).reshape(NIC, 128, D))
    d["p_down"] = np.asarray(inp["L0_ffn_base_down"], np.float32)
    gA = np.asarray(inp["L0_ffn_gate_A"], np.float32)
    uA = np.asarray(inp["L0_ffn_up_A"], np.float32)
    pgA = gA.reshape(E, NKC, 128, R).transpose(1, 2, 0, 3)
    puA = uA.reshape(E, NKC, 128, R).transpose(1, 2, 0, 3)
    tA = np.empty((NKC, 128, E, 128), np.float32)
    tA[:, :, :, 0:64] = pgA
    tA[:, :, :, 64:128] = puA
    d["p_tA"] = np.ascontiguousarray(tA.reshape(NKC, 128, E * 128))
    gB = np.asarray(inp["L0_ffn_gate_B"], np.float32)
    uB = np.asarray(inp["L0_ffn_up_B"], np.float32)
    pgB = gB.reshape(E, R, NIC, 128).transpose(2, 1, 0, 3)
    puB = uB.reshape(E, R, NIC, 128).transpose(2, 1, 0, 3)
    lq = np.zeros((NIC, 128, E, 256), np.float32)
    lq[:, 0:64, :, 0:128] = pgB
    lq[:, 64:128, :, 128:256] = puB
    d["p_lq"] = np.ascontiguousarray(lq.reshape(NIC, 128, E * 256))
    dA = np.asarray(inp["L0_ffn_down_A"], np.float32)
    d["p_dA"] = np.ascontiguousarray(
        dA.reshape(E, NIC, 128, R).transpose(1, 2, 0, 3).reshape(NIC, 128, E * R))
    d["p_dB"] = np.asarray(inp["L0_ffn_down_B"], np.float32)
    d["p_confWT"] = np.ascontiguousarray(
        np.asarray(inp["L0_ffn_conf_W"], np.float32).T.reshape(NKC, 128, E))
    d["p_confb"] = np.ascontiguousarray(np.broadcast_to(
        np.asarray(inp["L0_ffn_conf_b"], np.float32).reshape(1, E), (128, E)))
    d["p_wealth"] = np.ascontiguousarray(np.broadcast_to(
        np.asarray(inp["L0_ffn_expert_wealth"], np.float32).reshape(1, E), (128, E)))

    d["c_identbf"] = np.eye(128, dtype=bfloat16)
    d["c_ones128bf"] = np.ones((128, 128), dtype=bfloat16)
    d["c_ones1bf"] = np.ones((1, 128), dtype=bfloat16)
    d["c_iota8p8"] = np.broadcast_to(
        np.arange(E, dtype=np.float32)[None, :] + 8.0, (128, E)).copy()
    sel = np.zeros((E, E * 128), dtype=bfloat16)
    for e in range(E):
        sel[e, e * 128:(e + 1) * 128] = 1.0
    d["c_sel"] = sel
    Rl = np.zeros((128, 128), dtype=np.float32)
    for m in range(64):
        Rl[m + 64, m] = -1.0
        Rl[m, m + 64] = 1.0
    d["c_ropeR"] = Rl.astype(bfloat16)
    kk = np.arange(128)[:, None]
    qq = np.arange(TT)[None, :]
    masks = np.concatenate(
        [((m * 128 + kk) <= qq).astype(bfloat16) for m in range(4)], axis=1)
    d["c_masks"] = np.ascontiguousarray(masks)
    fc = np.zeros((128, 2), np.float32)
    fc[:, 0] = LN_EPS
    fc[:, 1] = EXP_SHIFT
    d["c_fconst"] = fc
    return d


def _chunk_owner(g):
    r = g if g < 4 else 7 - g
    return r, (0 if g == r else 8)


def _prep_core(inp, core):
    b, j = core // 4, core % 4
    chunks = (j, 7 - j)
    x = np.asarray(inp["L0_attn_x"], np.float32)
    xs = np.concatenate([x[b, c * 1024:(c + 1) * 1024, :] for c in chunks], axis=0)
    pos = np.concatenate([np.arange(c * 1024, (c + 1) * 1024) for c in chunks])
    inv_freq = 1.0 / (10000.0 ** (np.arange(0, HD, 2, dtype=np.float64) / HD))
    fr = pos[:, None].astype(np.float64) * inv_freq[None, :]
    cos_t = np.concatenate([np.cos(fr).T, np.cos(fr).T], 0).astype(bfloat16)
    sin_t = np.concatenate([np.sin(fr).T, np.sin(fr).T], 0).astype(bfloat16)

    offs = []
    for hh in range(4):
        offs.append((2048 + (4 * j + hh) * 128) * D)
    for hh in range(4):
        offs.append((4 * j + hh) * D)
    for hh in range(4):
        offs.append((4 * j + hh) * 128 * D)
    for r in range(4):
        for half in range(2):
            ch = chunks[half]
            offs.append((r * 512) * 8192 + ch * 1024)
    offtab = np.zeros((1, 256), np.int32)
    offtab[0, :len(offs)] = np.array(offs, np.int32)
    return {"x0": np.ascontiguousarray(xs), "ropec": np.ascontiguousarray(cos_t),
            "ropes": np.ascontiguousarray(sin_t), "offtab": offtab}


class T:
    pass


def build_nc(debug=False, phases=("ffn1", "attn", "ffn2")):
    nc = bass.Bass("TRN2", target_bir_lowering=False, detect_race_conditions=False)
    t = T()

    def din(name, shape, dt=F32):
        setattr(t, name, nc.dram_tensor(name, shape, dt, kind="ExternalInput"))

    din("x0", [NTOK, D])
    din("p_qkv", [NOC3, 128, D]); din("p_wo", [D, D])
    din("p_gate", [NIC, 128, D]); din("p_up", [NIC, 128, D]); din("p_down", [II, D])
    din("p_tA", [NKC, 128, E * 128])
    din("p_lq", [NIC, 128, E * 256])
    din("p_dA", [NIC, 128, E * R]); din("p_dB", [E, R, D])
    din("p_confWT", [NKC, 128, E]); din("p_confb", [128, E]); din("p_wealth", [128, E])
    din("c_identbf", [128, 128], BF16)
    din("c_ones128bf", [128, 128], BF16); din("c_ones1bf", [1, 128], BF16)
    din("c_iota8p8", [128, E]); din("c_sel", [E, E * 128], BF16)
    din("c_ropeR", [128, 128], BF16)
    din("c_masks", [128, 4 * TT], BF16); din("c_fconst", [128, 2])
    din("ropec", [128, NTOK], BF16); din("ropes", [128, NTOK], BF16)
    din("offtab", [1, 256], I32)

    t.s_qkv = nc.dram_tensor("s_qkv", [NOC3, 128, D], BF16)
    t.s_gate = nc.dram_tensor("s_gate", [NIC, 128, D], BF16)
    t.s_up = nc.dram_tensor("s_up", [NIC, 128, D], BF16)
    t.s_down = nc.dram_tensor("s_down", [II, D], BF16)
    t.s_wo = nc.dram_tensor("s_wo", [D, D], BF16)
    t.s_tA = nc.dram_tensor("s_tA", [NKC, 128, E * 128], BF16)
    t.s_lq = nc.dram_tensor("s_lq", [NIC, 128, E * 256], BF16)
    t.s_dA = nc.dram_tensor("s_dA", [NIC, 128, E * R], BF16)
    t.s_dB = nc.dram_tensor("s_dB", [E, R, D], BF16)
    if debug:
        t.xB = nc.dram_tensor("xB", [NTOK, D], F32, kind="ExternalOutput")
        t.xC = nc.dram_tensor("xC", [NTOK, D], F32, kind="ExternalOutput")
    else:
        t.xB = nc.dram_tensor("xB", [NTOK, D], F32)
        t.xC = nc.dram_tensor("xC", [NTOK, D], F32)
    t.qk_in = nc.dram_tensor("qk_in", [2 * D, NTOK], BF16)
    t.qk_out = nc.dram_tensor("qk_out", [8 * D, NTOK], BF16)
    t.v_in = nc.dram_tensor("v_in", [128, NH * NTOK], BF16)
    t.v_out = nc.dram_tensor("v_out", [512, NH * NTOK], BF16)
    t.o_in = nc.dram_tensor("o_in", [512, 8192], BF16)
    t.o_out = nc.dram_tensor("o_out", [2048, 8192], BF16)
    t.OUT = nc.dram_tensor("OUT", [NTOK, D], F32, kind="ExternalOutput")
    if debug:
        t.DBG = nc.dram_tensor("DBG", [128, 65536], BF16, kind="ExternalOutput")
        t.qk_dbg = nc.dram_tensor("qk_dbg", [2 * D, NTOK], BF16, kind="ExternalOutput")
        t.o_dbg = nc.dram_tensor("o_dbg", [512, 8192], BF16, kind="ExternalOutput")
    t.dbgoff = [0]

    kb = KB()

    with ExitStack() as st:
        en = st.enter_context

        def sb(name, shape, dt):
            setattr(t, name, en(nc.sbuf_tensor(name, shape, dt)))

        for n, sh, dt in [
            ("identbf", [128, 128], BF16),
            ("ones128bf", [128, 128], BF16), ("ones1bf", [1, 128], BF16),
            ("iota8p8", [128, E], F32), ("sel", [E, E * 128], BF16),
            ("ropeR", [128, 128], BF16), ("masks", [128, 4 * TT], BF16),
            ("cosb", [128, NTOK], BF16), ("sinb", [128, NTOK], BF16),
            ("confWT", [128, NKC * E], BF16), ("confb", [128, E], F32),
            ("fconst", [128, 2], F32),
            ("wealth", [128, E], F32), ("offt", [1, 256], I32),
        ]:
            sb(n, sh, dt)
        sb("ARB", [128, 67584], BF16)
        sb("fencet", [1, 8], F32)
        sb("xt", [128, 4 * D], F32)
        sb("ybf", [128, D], BF16)
        sb("sq", [128, TT], F32)
        sb("sf32", [128, 2 * TT], F32)
        sb("hf32", [128, TT], F32)
        sb("rlbl", [1, TT], BF16)
        sb("stats", [128, 64], F32)
        sb("rt", [128, 96], F32)
        sb("bids", [E, TT], F32)
        sb("lrow", [1, 2 * TT], F32)
        sb("rl", [128, TT], F32)
        sb("webf8", [128, E], BF16)
        sb("rlbf", [1, TT], BF16)
        t.pb = [en(nc.psum_tensor(f"pb{i}", [128, TT], F32)) for i in range(7)]
        t.pbh = en(nc.psum_tensor("pbh", [128, 1024], BF16))

        sems_eng = {e: en(nc.semaphore(f"se_{e}")) for e in ENGINES}
        t.roff = en(nc.gpsimd.register("roff"))

        _build_plan(kb, t, debug, phases)

        slots = sorted(kb.slot_count)
        sems_slot = {s: en(nc.semaphore(f"ss_{s}")) for s in slots}
        block = en(nc.Block())
        kb.emit(nc, block, sems_eng, sems_slot)

    return nc


A_YT = 0
A_H = 8192
A_WG = 36864
A_WU = 40960
A_GB = 45056
A_UB = 47104
A_DA = 49152
A_TG = 50176
A_TU = 54272
A_WEB = 58368
A_WER = 62464
A_BG = 62976
A_BU = 63488
A_SB = 64000
A_HB = 64512
A_WEH = 65024


def WEHS(e):
    return A_SB + (e % 3) * TT
A_GAS = 65536
A_UAS = 66560
A_DN = 65536
ARBW = 67584
def C_DBLOC(e):
    return A_WG + e * D if e < 6 else A_WEB + (e - 6) * D
A_Y2T = 0
A_KS = 0
A_VS = 16384
A_QS = 32768
A_OST = 49152
A_PT = 49664
A_QP = 50688
A_RT1 = 51712
A_VSB = 52736
A_WQ = 53760
A_OT = 0
A_WO = 32768


def _build_plan(kb, t, debug, phases):
    _p0(kb, t)
    if TRUNC.get("touch", True):
        def finit(h):
            return h.eng.memset(t.fencet[:], 0.0)
        kb.op("gpsimd", finit, reads=(), writes=[rs("fencet")])
        outs = [t.OUT] + ([t.xB, t.xC] if debug else [])
        def ftouch(h, outs=outs):
            last = None
            for o in outs:
                last = h.dma(o[0:1, 0:8], t.fencet[0:1, 0:8])
            return last
        kb.dma("gpsimd", "touch", ftouch, len(outs),
               reads=[rs("fencet")], writes=[])
    last = phases[-1]
    if "ffn1" in phases:
        kb.scope = "ffn1"
        dst = t.OUT if (last == "ffn1" and not debug) else t.xB
        _ffn(kb, t, t.x0, dst, tag="f1")
    if "attn" in phases:
        kb.scope = "attn"
        dst = t.OUT if (last == "attn" and not debug) else t.xC
        _attn(kb, t, t.xB, dst)
    if "ffn2" in phases:
        kb.scope = "ffn2"
        _ffn(kb, t, t.xC, t.OUT, tag="f2")
    if TRUNC.get("fin", True):
        def fin(h):
            return h.eng.engine_nop()
        reads = [rs("OUT")]
        if debug:
            reads += [rs("xB"), rs("xC")]
        kb.op("vector", fin, reads=reads)


def _p0(kb, t):
    kb.scope = None if TRUNC.get("noscope") else "p0"

    pairs = [
        (t.identbf, t.c_identbf),
        (t.ones128bf, t.c_ones128bf), (t.ones1bf, t.c_ones1bf),
        (t.iota8p8, t.c_iota8p8), (t.sel, t.c_sel), (t.ropeR, t.c_ropeR),
        (t.masks, t.c_masks), (t.cosb, t.ropec), (t.sinb, t.ropes),
        (t.confb, t.p_confb), (t.wealth, t.p_wealth), (t.offt, t.offtab),
        (t.fconst, t.c_fconst),
    ]
    for dst, src in (pairs if TRUNC.get("p0const", True) else []):
        def f(h, dst=dst, src=src):
            return h.dma(dst[:], src[:])
        kb.dma("sync", "const", f, 1, writes=[rs(dst.name)])
    for kc in range(NKC if TRUNC.get("p0const", True) else 0):
        def f(h, kc=kc):
            return h.dma(t.confWT[:, kc * E:(kc + 1) * E], t.p_confWT[kc])
        kb.dma("gpsimd", "const2", f, 1, writes=[rs("confWT", kc * E, (kc + 1) * E)])

    cast_groups = [
        [(t.s_tA, t.p_tA), (t.s_lq, t.p_lq), (t.s_dA, t.p_dA),
         (t.s_gate, t.p_gate), (t.s_up, t.p_up)],
        [(t.s_down, t.p_down), (t.s_dB, t.p_dB)],
        [(t.s_qkv, t.p_qkv), (t.s_wo, t.p_wo)],
    ]

    def chunks_of(dst, src):
        shape = [int(x) for x in src.shape]
        cols = shape[-1]
        rows = 1
        for s in shape[:-1]:
            rows *= s
        out = []
        o = 0
        while o < rows:
            rchunk = min(512, rows - o)
            out.append((dst, src, o, rchunk, cols))
            o += rchunk
        return out

    for gi, casts in enumerate(cast_groups):
        todo = []
        if TRUNC.get("p0cast", True):
            if gi == 0:
                for dst, src in casts[:3]:
                    todo += chunks_of(dst, src)
                ga = chunks_of(t.s_gate, t.p_gate)
                up = chunks_of(t.s_up, t.p_up)
                for a, b in zip(ga, up):
                    todo += [a, b]
            else:
                for dst, src in casts:
                    todo += chunks_of(dst, src)
        for dst, src, o, rchunk, cols in todo:
            def f(h, dst=dst, src=src, o=o, rchunk=rchunk, cols=cols):
                return h.dma(
                    bass.AP(dst, o * cols, [[cols, rchunk], [1, cols]]),
                    bass.AP(src, o * cols, [[cols, rchunk], [1, cols]]))
            kb.dma("gpsimd", "wcast", f, 1,
                   writes=[rs(dst.name, o * cols, (o + rchunk) * cols)])



def _ln_tile(kb, t, x_src, tt, tc, tag, out_bf16_direct):
    slot = tc
    row0 = tt * TT + tc * 128

    def fld(h):
        return h.dma(t.xt[:, slot * D:(slot + 1) * D], x_src[row0:row0 + 128, :])
    kb.dma("scalar", f"xld{tag}{slot}", fld, 1,
           reads=[rs(x_src.name, row0 * D, (row0 + 128) * D)],
           writes=[rs("xt", slot * D * 4, (slot + 1) * D * 4)])

    xr = [rs("xt", slot * D * 4, (slot + 1) * D * 4)]

    def fsum(h):
        return h.eng.tensor_reduce(
            t.stats[:, tc:tc + 1], t.xt[:, slot * D:(slot + 1) * D], axis=AX,
            op=AluOpType.add)
    kb.op("vector", fsum, reads=xr, writes=[rs("stats", tc * 4, tc * 4 + 4)])

    for j in range(4):
        def fsq(h, j=j):
            return h.eng.activation(
                t.sq[:], t.xt[:, slot * D + j * TT:slot * D + (j + 1) * TT],
                AF.Square, accum_out=t.stats[:, 32 + tc * 4 + j:33 + tc * 4 + j])
        kb.op("scalar", fsq, reads=xr,
              writes=[rs("sq"), rs("stats", (32 + tc * 4 + j) * 4,
                                   (33 + tc * 4 + j) * 4)])

    def fsqs(h):
        return h.eng.tensor_reduce(
            t.stats[:, 4 + tc:5 + tc], t.stats[:, 32 + tc * 4:32 + tc * 4 + 4],
            axis=AX, op=AluOpType.add)
    kb.op("vector", fsqs,
          reads=[rs("stats", (32 + tc * 4) * 4, (36 + tc * 4) * 4)],
          writes=[rs("stats", (4 + tc) * 4, (5 + tc) * 4)])

    def fm(h):
        return h.eng.tensor_scalar(
            t.stats[:, 8 + tc:9 + tc], t.stats[:, tc:tc + 1], 1.0 / D, None,
            AluOpType.mult)
    kb.op("vector", fm, reads=[rs("stats", tc * 4, tc * 4 + 4)],
          writes=[rs("stats", (8 + tc) * 4, (9 + tc) * 4)])

    def fq2(h):
        return h.eng.tensor_scalar(
            t.stats[:, 12 + tc:13 + tc], t.stats[:, 4 + tc:5 + tc], 1.0 / D, None,
            AluOpType.mult)
    kb.op("vector", fq2, reads=[rs("stats", (4 + tc) * 4, (5 + tc) * 4)],
          writes=[rs("stats", (12 + tc) * 4, (13 + tc) * 4)])

    def fvar(h):
        return h.eng.scalar_tensor_tensor(
            t.stats[:, 16 + tc:17 + tc], t.stats[:, 8 + tc:9 + tc],
            t.stats[:, 8 + tc:9 + tc], t.stats[:, 12 + tc:13 + tc],
            AluOpType.mult, AluOpType.subtract)
    kb.op("vector", fvar,
          reads=[rs("stats", (8 + tc) * 4, (9 + tc) * 4),
                 rs("stats", (12 + tc) * 4, (13 + tc) * 4)],
          writes=[rs("stats", (16 + tc) * 4, (17 + tc) * 4)])
    return slot


def _ln_finish(kb, t):
    def fsq(h):
        return h.eng.activation(t.stats[:, 20:24], t.stats[:, 16:20], AF.Sqrt,
                                bias=t.fconst[:, 0:1], scale=-1.0)
    kb.op("scalar", fsq, reads=[rs("stats", 64, 80), rs("fconst")],
          writes=[rs("stats", 80, 96)])

    def fr(h):
        return h.eng.reciprocal(t.stats[:, 24:28], t.stats[:, 20:24])
    kb.op("vector", fr, reads=[rs("stats", 80, 96)], writes=[rs("stats", 96, 112)])


def _ln_norm(kb, t, tc, slot):
    def fn(h):
        return h.eng.tensor_scalar(
            t.ybf[:], t.xt[:, slot * D:(slot + 1) * D],
            t.stats[:, 8 + tc:9 + tc], t.stats[:, 24 + tc:25 + tc],
            AluOpType.subtract, AluOpType.mult)
    kb.op("vector", fn,
          reads=[rs("xt", slot * D * 4, (slot + 1) * D * 4),
                 rs("stats", (8 + tc) * 4, (9 + tc) * 4),
                 rs("stats", (24 + tc) * 4, (25 + tc) * 4)],
          writes=[rs("ybf")])


def _stt_res_add(kb, t, psum, psum_key, xcol0, fg):
    c0 = xcol0 + fg * TT

    def fn(h):
        return h.eng.tensor_tensor(
            t.xt[:, c0:c0 + TT], psum[:], t.xt[:, c0:c0 + TT], AluOpType.add)
    kb.op("vector", fn,
          reads=[rs(psum_key), rs("xt", c0 * 4, (c0 + TT) * 4)],
          writes=[rs("xt", c0 * 4, (c0 + TT) * 4)])


def _ffn(kb, t, x_src, x_dst, tag):
    pb = t.pb
    arb = t.ARB

    def AB(c0, n=TT):
        return arb[:, c0:c0 + n], rs("ARB", c0 * 2, (c0 + n) * 2)

    for tt in range(min(NTT, TRUNC["ffn_tt"])):
        sc0 = kb.scope
        for tc in range(4):
            _ln_tile(kb, t, x_src, tt, tc, tag, False)
        _ln_finish(kb, t)
        AL = TRUNC.get("a_parts", 4)
        for tc in range(4):
            slot = tc
            _ln_norm(kb, t, tc, slot)
            zb = 5 + tc % 2
            for grp in range(2 if AL >= 2 else 0):
                for g8 in range(8):
                    fc = grp * 8 + g8

                    def ftr(h, fc=fc, g8=g8):
                        return h.eng.transpose(
                            t.pbh[:, g8 * 128:(g8 + 1) * 128],
                            t.ybf[:, fc * 128:(fc + 1) * 128], t.identbf[:])
                    kb.op("tensor", ftr, reads=[rs("ybf"), rs("identbf")],
                          writes=[rs("pbh", g8, g8 + 1)])

                yg0 = grp * 8 * TT + tc * 128

                def fe2(h, yg0=yg0):
                    return h.eng.copy(
                        bass.AP(arb, yg0, [[ARBW, 128], [TT, 8], [1, 128]]),
                        t.pbh[:].rearrange("p (a b) -> p a b", a=8))
                kb.op("scalar", fe2, reads=[rs("pbh", 0, 8)],
                      writes=[rs("ARB", yg0 * 2, (yg0 + 8 * TT) * 2)])

                if TRUNC.get("a_sub", 4) >= 4:
                    for g8 in range(8):
                        fc = grp * 8 + g8
                        ytc0 = fc * TT + tc * 128

                        def fz(h, fc=fc, ytc0=ytc0, zb=zb):
                            return h.eng.matmul(
                                pb[zb][:, 0:E],
                                arb[:, ytc0:ytc0 + 128],
                                t.confWT[:, fc * E:(fc + 1) * E],
                                start=(fc == 0), stop=(fc == NKC - 1))
                        kb.op("tensor", fz,
                              reads=[rs("confWT", fc * E, (fc + 1) * E),
                                     rs("ARB", ytc0 * 2, (ytc0 + 128) * 2)],
                              writes=[rs(f"pb{zb}", 0, 1)])

            if AL < 3:
                continue
            c = 32 + tc * 8
            bidsc = tc * 8
            selc = 64 + (tc % 2) * 16
            bids = t.rt[:, bidsc:bidsc + 8]
            m1 = t.rt[:, c:c + 1]; i1 = t.rt[:, c + 1:c + 2]
            m2 = t.rt[:, c + 2:c + 3]; i2 = t.rt[:, c + 3:c + 4]
            d12 = t.rt[:, c + 4:c + 6]; r12 = t.rt[:, c + 6:c + 8]
            sel1 = t.rt[:, selc:selc + 8]
            bids2 = t.rt[:, selc + 8:selc + 16]
            bidr = rs("rt", bidsc * 4, (bidsc + 8) * 4)
            rtr = rs("rt", c * 4, (c + 8) * 4)
            rtr2 = rs("rt", selc * 4, (selc + 16) * 4)

            def step(fn, reads, writes, eng="vector"):
                kb.op(eng, fn, reads=reads, writes=writes)

            step(lambda h, bids=bids, zb=zb: h.eng.tensor_tensor(
                bids, pb[zb][:, 0:E], t.confb[:], AluOpType.add),
                 [rs(f"pb{zb}", 0, 1), rs("confb")], [bidr])
            step(lambda h, bids=bids: h.eng.activation(bids, bids, AF.Sigmoid),
                 [bidr], [bidr], eng="scalar")
            step(lambda h, bids=bids: h.eng.tensor_tensor(
                bids, bids, t.wealth[:], AluOpType.mult),
                 [bidr, rs("wealth")], [bidr])
            step(lambda h, bids=bids, m1=m1: h.eng.tensor_reduce(
                m1, bids, axis=AX, op=AluOpType.max), [bidr], [rtr])
            step(lambda h, bids=bids, m1=m1, sel1=sel1: h.eng.tensor_scalar(
                sel1, bids, m1, None, AluOpType.is_equal), [bidr, rtr], [rtr2])
            step(lambda h, sel1=sel1, bids2=bids2: h.eng.scalar_tensor_tensor(
                bids2, sel1, -8.0, t.iota8p8[:], AluOpType.mult, AluOpType.add),
                 [rtr2, rs("iota8p8")], [rtr2])
            step(lambda h, bids2=bids2, i1=i1: h.eng.tensor_reduce(
                i1, bids2, axis=AX, op=AluOpType.min), [rtr2], [rtr])
            step(lambda h, i1=i1, sel1=sel1: h.eng.tensor_scalar(
                sel1, t.iota8p8[:], 8.0, i1, AluOpType.subtract, AluOpType.is_equal),
                 [rs("iota8p8"), rtr], [rtr2])
            step(lambda h, sel1=sel1, bids=bids, bids2=bids2: h.eng.scalar_tensor_tensor(
                bids2, sel1, BIGNEG, bids, AluOpType.mult, AluOpType.add),
                 [rtr2, bidr], [rtr2])
            step(lambda h, bids2=bids2, m2=m2: h.eng.tensor_reduce(
                m2, bids2, axis=AX, op=AluOpType.max), [rtr2], [rtr])
            step(lambda h, bids2=bids2, m2=m2: h.eng.tensor_scalar(
                bids2, bids2, m2, None, AluOpType.is_equal), [rtr2, rtr], [rtr2])
            step(lambda h, bids2=bids2: h.eng.scalar_tensor_tensor(
                bids2, bids2, -8.0, t.iota8p8[:], AluOpType.mult, AluOpType.add),
                 [rtr2, rs("iota8p8")], [rtr2])
            step(lambda h, bids2=bids2, i2=i2: h.eng.tensor_reduce(
                i2, bids2, axis=AX, op=AluOpType.min), [rtr2], [rtr])
            step(lambda h, i2=i2, bids2=bids2: h.eng.tensor_scalar(
                bids2, t.iota8p8[:], 8.0, i2, AluOpType.subtract, AluOpType.is_equal),
                 [rs("iota8p8"), rtr], [rtr2])
            step(lambda h, m1=m1, m2=m2, d12=d12: h.eng.tensor_tensor(
                d12[:, 0:1], m1, m2, AluOpType.subtract), [rtr], [rtr])
            step(lambda h, d12=d12: h.eng.tensor_scalar(
                d12[:, 1:2], d12[:, 0:1], -1.0, None, AluOpType.mult), [rtr], [rtr])
            step(lambda h, d12=d12, r12=r12: h.eng.activation(r12, d12, AF.Sigmoid),
                 [rtr], [rtr], eng="scalar")
            step(lambda h, r12=r12: h.eng.tensor_scalar(
                r12, r12, 0.5, None, AluOpType.mult), [rtr], [rtr])
            step(lambda h, sel1=sel1, r12=r12: h.eng.tensor_scalar(
                sel1, sel1, r12[:, 0:1], None, AluOpType.mult), [rtr2, rtr], [rtr2])
            step(lambda h, bids2=bids2, r12=r12, sel1=sel1: h.eng.scalar_tensor_tensor(
                sel1, bids2, r12[:, 1:2], sel1, AluOpType.mult, AluOpType.add),
                 [rtr2, rtr], [rtr2])
            step(lambda h, sel1=sel1: h.eng.tensor_copy(t.webf8[:], sel1),
                 [rtr2], [rs("webf8")])

            def fwt(h, tc=tc):
                return h.eng.transpose(
                    t.pbh[0:E, 0:128], t.webf8[:], t.identbf[:])
            kb.op("tensor", fwt, reads=[rs("webf8"), rs("identbf")],
                  writes=[rs("pbh", 0, 1)])

            def fwe(h, tc=tc):
                return h.eng.copy(arb[0:E, A_WER + tc * 128:A_WER + (tc + 1) * 128],
                                  t.pbh[0:E, 0:128])
            kb.op("scalar", fwe, reads=[rs("pbh", 0, 1)],
                  writes=[rs("ARB", (A_WER + tc * 128) * 2,
                             (A_WER + (tc + 1) * 128) * 2)])

        if AL < 3:
            continue
        for e in range(E):
            def fbc(h, e=e):
                return h.eng.matmul(
                    pb[6][:], t.sel[:, e * 128:(e + 1) * 128],
                    arb[0:E, A_WER:A_WER + TT], start=True, stop=True)
            kb.op("tensor", fbc,
                  reads=[rs("sel"), rs("ARB", A_WER * 2, (A_WER + TT) * 2)],
                  writes=[rs("pb6")])

            wc0 = A_WEB + e * TT

            def fbe(h, wc0=wc0):
                return h.eng.copy(arb[:, wc0:wc0 + TT], pb[6][:])
            kb.op("scalar", fbe, reads=[rs("pb6")],
                  writes=[rs("ARB", wc0 * 2, (wc0 + TT) * 2)])

        if AL < 4:
            continue
        for wave in range(2):
            for kc in range(NKC):
                buf = kc % 2
                sc = A_GAS + buf * 1024

                def fld(h, kc=kc, sc=sc):
                    return h.dma(arb[:, sc:sc + 1024], t.s_tA[kc])
                kb.dma("sync", f"tI{buf}", fld, 1,
                       reads=[rs(t.s_tA.name, kc * 128 * 1024,
                                 (kc + 1) * 128 * 1024)],
                       writes=[rs("ARB", sc * 2, (sc + 1024) * 2)])
                for e in range(wave * 4, wave * 4 + 4):
                    bank = e % 4

                    def fmm(h, kc=kc, e=e, sc=sc, bank=bank):
                        return h.eng.matmul(
                            pb[bank][:],
                            arb[:, sc + e * 128:sc + (e + 1) * 128],
                            arb[:, kc * TT:(kc + 1) * TT],
                            start=(kc == 0), stop=(kc == NKC - 1))
                    kb.op("tensor", fmm,
                          reads=[rs("ARB", sc * 2, (sc + 1024) * 2),
                                 rs("ARB", kc * TT * 2, (kc + 1) * TT * 2)],
                          writes=[rs(f"pb{bank}")])
            for e in range(wave * 4, wave * 4 + 4):
                bank = e % 4
                dc0 = A_TG + e * TT

                def fev(h, bank=bank, dc0=dc0):
                    return h.eng.mul(arb[:, dc0:dc0 + TT], pb[bank][:], SCALING)
                kb.op("scalar", fev, reads=[rs(f"pb{bank}")],
                      writes=[rs("ARB", dc0 * 2, (dc0 + TT) * 2)])

        if TRUNC.get("dumpffn"):
            dump_arb(kb, t, A_YT, 8192, f"yT_tt{tt}")
            dump_t(kb, t, t.stats, "stats", 28, f"stats_tt{tt}")
            dump_arb(kb, t, A_WER, 512, f"weR_tt{tt}", rows=8)
            dump_arb(kb, t, A_WEB, 4096, f"web_tt{tt}")
            dump_arb(kb, t, A_TG, 4096, f"tg_tt{tt}", rows=128)
        for ic in range(NIC if "B" in TRUNC["parts"] else 0):
            buf = ic % 2
            wg0 = A_WG + buf * D
            wu0 = A_WU + buf * D
            gb0 = A_GB + buf * 1024
            ub0 = A_UB + buf * 1024
            da0 = A_DA + buf * TT

            lpq = A_TU + buf * 2048

            def fldw(h, ic=ic, wg0=wg0, wu0=wu0, lpq=lpq, da0=da0):
                h.dma(arb[:, wg0:wg0 + D], t.s_gate[ic])
                h.dma(arb[:, wu0:wu0 + D], t.s_up[ic])
                h.dma(arb[:, lpq:lpq + 2048], t.s_lq[ic])
                return h.dma(arb[:, da0:da0 + TT], t.s_dA[ic])
            kb.dma("sync", f"wB{buf}", fldw, 4,
                   reads=[rs(t.s_gate.name, ic * 128 * D, (ic + 1) * 128 * D),
                          rs(t.s_up.name, ic * 128 * D, (ic + 1) * 128 * D),
                          rs(t.s_lq.name, ic * 128 * 2048,
                             (ic + 1) * 128 * 2048),
                          rs(t.s_dA.name, ic * 128 * TT, (ic + 1) * 128 * TT)],
                   writes=[rs("ARB", wg0 * 2, (wg0 + D) * 2),
                           rs("ARB", wu0 * 2, (wu0 + D) * 2),
                           rs("ARB", lpq * 2, (lpq + 2048) * 2),
                           rs("ARB", da0 * 2, (da0 + TT) * 2)])

            for bank, w0 in ((0, wg0), (1, wu0)):
                for kc in range(NKC):
                    def fmm(h, bank=bank, w0=w0, kc=kc):
                        return h.eng.matmul(
                            pb[bank][:], arb[:, w0 + kc * 128:w0 + (kc + 1) * 128],
                            arb[:, kc * TT:(kc + 1) * TT],
                            start=(kc == 0), stop=(kc == NKC - 1))
                    kb.op("tensor", fmm,
                          reads=[rs("ARB", w0 * 2, (w0 + D) * 2),
                                 rs("ARB", kc * TT * 2, (kc + 1) * TT * 2)],
                          writes=[rs(f"pb{bank}")])
            for bank, b0 in ((0, A_BG), (1, A_BU)):
                def fev(h, bank=bank, b0=b0):
                    return h.eng.copy(arb[:, b0:b0 + TT], pb[bank][:])
                kb.op("scalar", fev, reads=[rs(f"pb{bank}")],
                      writes=[rs("ARB", b0 * 2, (b0 + TT) * 2)])

            def emit_hatd(e, ic=ic, da0=da0):
                wc = WEHS(e)

                def fha(h, e=e, wc=wc):
                    return h.eng.matmul(pb[2][:], t.identbf[:],
                                        arb[:, wc:wc + TT],
                                        start=(e == 0), stop=(e == E - 1))
                kb.op("tensor", fha,
                      reads=[rs("identbf"), rs("ARB", wc * 2, (wc + TT) * 2)],
                      writes=[rs("pb2")])

                bank = 3 + e // 2
                p0 = (e % 2) * 64

                def ftd(h, e=e, ic=ic, da0=da0, bank=bank, p0=p0, wc=wc):
                    return h.eng.matmul(
                        pb[bank][p0:p0 + 64, :],
                        arb[:, da0 + e * R:da0 + (e + 1) * R],
                        arb[:, wc:wc + TT],
                        start=(ic == 0), stop=(ic == NIC - 1),
                        skip_group_check=True)
                kb.op("tensor", ftd,
                      reads=[rs("ARB", (da0 + e * R) * 2, (da0 + (e + 1) * R) * 2),
                             rs("ARB", wc * 2, (wc + TT) * 2)],
                      writes=[rs(f"pb{bank}", p0, p0 + 64)])

            for e in range(E):
                sfc = (e % 2) * TT
                qa = lpq + e * 256
                def fdg(h, e=e, qa=qa):
                    return h.eng.matmul(
                        pb[0][:], arb[:, qa:qa + 128],
                        arb[:, A_TG + e * TT:A_TG + (e + 1) * TT],
                        start=True, stop=False)
                kb.op("tensor", fdg,
                      reads=[rs("ARB", qa * 2, (qa + 128) * 2),
                             rs("ARB", (A_TG + e * TT) * 2, (A_TG + (e + 1) * TT) * 2)],
                      writes=[rs("pb0")])

                def fig(h):
                    return h.eng.matmul(pb[0][:], t.identbf[:],
                                        arb[:, A_BG:A_BG + TT],
                                        start=False, stop=True)
                kb.op("tensor", fig,
                      reads=[rs("identbf"), rs("ARB", A_BG * 2, (A_BG + TT) * 2)],
                      writes=[rs("pb0")])

                if e == 0 and ic == 0 and tt == 0 and TRUNC.get("dumpg"):
                    def fgd(h):
                        return h.eng.copy(t.sq[:], pb[0][:])
                    kb.op("scalar", fgd, reads=[rs("pb0")], writes=[rs("sq")])
                    dump_t(kb, t, t.sq, "sq", TT, "g_e0")

                def fsl(h, sfc=sfc):
                    return h.eng.activation(t.sf32[:, sfc:sfc + TT], pb[0][:],
                                            AF.Silu)
                kb.op("scalar", fsl, reads=[rs("pb0")],
                      writes=[rs("sf32", sfc, sfc + TT)])

                def fdu(h, e=e, qa=qa):
                    return h.eng.matmul(
                        pb[1][:], arb[:, qa + 128:qa + 256],
                        arb[:, A_TG + e * TT:A_TG + (e + 1) * TT],
                        start=True, stop=False)
                kb.op("tensor", fdu,
                      reads=[rs("ARB", (qa + 128) * 2, (qa + 256) * 2),
                             rs("ARB", (A_TG + e * TT) * 2, (A_TG + (e + 1) * TT) * 2)],
                      writes=[rs("pb1")])

                def fiu(h):
                    return h.eng.matmul(pb[1][:], t.identbf[:],
                                        arb[:, A_BU:A_BU + TT],
                                        start=False, stop=True)
                kb.op("tensor", fiu,
                      reads=[rs("identbf"), rs("ARB", A_BU * 2, (A_BU + TT) * 2)],
                      writes=[rs("pb1")])

                if e > 1:
                    emit_hatd(e - 2)

                def fh(h, sfc=sfc):
                    return h.eng.tensor_tensor(
                        t.hf32[:], t.sf32[:, sfc:sfc + TT], pb[1][:],
                        AluOpType.mult)
                kb.op("vector", fh, reads=[rs("sf32", sfc, sfc + TT), rs("pb1")],
                      writes=[rs("hf32")])

                def fweh(h, e=e):
                    wc = WEHS(e)
                    return h.eng.tensor_tensor(
                        arb[:, wc:wc + TT], t.hf32[:],
                        arb[:, A_WEB + e * TT:A_WEB + (e + 1) * TT], AluOpType.mult)
                kb.op("vector", fweh,
                      reads=[rs("hf32"),
                             rs("ARB", (A_WEB + e * TT) * 2, (A_WEB + (e + 1) * TT) * 2)],
                      writes=[rs("ARB", WEHS(e) * 2, (WEHS(e) + TT) * 2)])

                if e == 0 and ic == 0 and tt == 0 and TRUNC.get("dumpg"):
                    def fud(h):
                        return h.eng.copy(t.sq[:], pb[1][:])
                    kb.op("scalar", fud, reads=[rs("pb1")], writes=[rs("sq")])
                    dump_t(kb, t, t.sq, "sq", TT, "u_e0")
                    dump_arb(kb, t, WEHS(0), TT, "weh_e0")
                    dump_arb(kb, t, A_BG, TT, "bg_ic0")
                    dump_arb(kb, t, A_BU, TT, "bu_ic0")
            emit_hatd(E - 2)
            emit_hatd(E - 1)

            def fhe(h, ic=ic):
                return h.eng.copy(arb[:, A_H + ic * TT:A_H + (ic + 1) * TT], pb[2][:])
            kb.op("scalar", fhe, reads=[rs("pb2")],
                  writes=[rs("ARB", (A_H + ic * TT) * 2, (A_H + (ic + 1) * TT) * 2)])

        if TRUNC.get("dumpffn"):
            dump_arb(kb, t, A_H, NIC * TT, f"H_tt{tt}")
        if "C" not in TRUNC["parts"]:
            continue
        for e in range(E):
            bank = 3 + e // 2
            p0 = (e % 2) * 64
            dc0 = A_TG + e * TT

            def fte(h, bank=bank, p0=p0, dc0=dc0):
                return h.eng.mul(arb[0:64, dc0:dc0 + TT], pb[bank][p0:p0 + 64, :],
                                 SCALING)
            kb.op("scalar", fte, reads=[rs(f"pb{bank}", p0, p0 + 64)],
                  writes=[rs("ARB", dc0 * 2, (dc0 + TT) * 2)])
        if TRUNC.get("dumpffn"):
            dump_arb(kb, t, A_TG, 4096, f"td_tt{tt}", rows=64)
        def fdb(h):
            last = None
            for e in range(E):
                c0 = C_DBLOC(e)
                last = h.dma(arb[0:64, c0:c0 + D], t.s_dB[e])
            return last
        kb.dma("sync", "dB", fdb, E, reads=[rs(t.s_dB.name)],
               writes=[rs("ARB", A_WG * 2, (A_WG + 6 * D) * 2),
                       rs("ARB", A_WEB * 2, (A_WEB + 2 * D) * 2)])
        def fxl(h, tt=tt):
            last = None
            for tc in range(4):
                row0 = tt * TT + tc * 128
                last = h.dma(t.xt[:, tc * D:(tc + 1) * D],
                             x_src[row0:row0 + 128, :])
            return last
        kb.dma("scalar", f"xr{tag}", fxl, 4,
               reads=[rs(x_src.name, tt * TT * D, (tt + 1) * TT * D)],
               writes=[rs("xt", 0, 4 * D * 4)])

        for fg in range(4):
            for ic in range(NIC):
                buf = ic % 4
                dn0 = A_DN + buf * TT

                def fdl(h, ic=ic, fg=fg, dn0=dn0):
                    return h.dma(arb[:, dn0:dn0 + TT],
                                 t.s_down[ic * 128:(ic + 1) * 128,
                                          fg * TT:(fg + 1) * TT])
                kb.dma("sync", f"dn{buf}", fdl, 1,
                       reads=[rs(t.s_down.name, ic * 128 * D,
                                 (ic + 1) * 128 * D)],
                       writes=[rs("ARB", dn0 * 2, (dn0 + TT) * 2)])
                for tc in range(4):
                    def fmm(h, ic=ic, tc=tc, dn0=dn0):
                        return h.eng.matmul(
                            pb[tc][:],
                            arb[:, A_H + ic * TT + tc * 128:A_H + ic * TT + (tc + 1) * 128],
                            arb[:, dn0:dn0 + TT],
                            start=(ic == 0), stop=False)
                    kb.op("tensor", fmm,
                          reads=[rs("ARB", (A_H + ic * TT) * 2, (A_H + (ic + 1) * TT) * 2),
                                 rs("ARB", dn0 * 2, (dn0 + TT) * 2)],
                          writes=[rs(f"pb{tc}")])
            for e in range(E):
                tdc = A_TG + e * TT
                dbc = C_DBLOC(e) + fg * TT
                for tc in range(4):
                    def fmm(h, e=e, tc=tc, tdc=tdc, dbc=dbc):
                        return h.eng.matmul(
                            pb[tc][:],
                            arb[0:64, tdc + tc * 128:tdc + (tc + 1) * 128],
                            arb[0:64, dbc:dbc + TT],
                            start=False, stop=(e == E - 1))
                    kb.op("tensor", fmm,
                          reads=[rs("ARB", tdc * 2, (tdc + TT) * 2),
                                 rs("ARB", dbc * 2, (dbc + TT) * 2)],
                          writes=[rs(f"pb{tc}")])
            for tc in range(4):
                _stt_res_add(kb, t, pb[tc], f"pb{tc}", tc * D, fg)
        for tc in range(4):
            row0 = tt * TT + tc * 128

            def fxs(h, tc=tc, row0=row0):
                return h.dma(x_dst[row0:row0 + 128, :], t.xt[:, tc * D:(tc + 1) * D])
            kb.dma("gpsimd", f"xs{tag}", fxs, 1,
                   reads=[rs("xt", tc * D * 4, (tc + 1) * D * 4)],
                   writes=[rs(x_dst.name, row0 * D, (row0 + 128) * D)])

    def ffence(h):
        return h.eng.memset(t.fencet[:], 0.0)
    kb.fence("gpsimd", ffence, [rs(x_dst.name)])


def _attn(kb, t, x_src, x_dst):
    pb = t.pb
    arb = t.ARB

    for tt in range(NTT):
        for tc in range(4):
            _ln_tile(kb, t, x_src, tt, tc, "a", True)
        _ln_finish(kb, t)
        for tc in range(4):
            slot = tc
            _ln_norm(kb, t, tc, slot)
            for grp in range(2):
                for g8 in range(8):
                    fc = grp * 8 + g8

                    def ftr(h, fc=fc, g8=g8):
                        return h.eng.transpose(
                            t.pbh[:, g8 * 128:(g8 + 1) * 128],
                            t.ybf[:, fc * 128:(fc + 1) * 128], t.identbf[:])
                    kb.op("tensor", ftr, reads=[rs("ybf"), rs("identbf")],
                          writes=[rs("pbh", g8, g8 + 1)])

                yg0 = A_Y2T + grp * 8 * TT + tc * 128

                def fe2(h, yg0=yg0):
                    return h.eng.copy(
                        bass.AP(arb, yg0, [[ARBW, 128], [TT, 8], [1, 128]]),
                        t.pbh[:].rearrange("p (a b) -> p a b", a=8))
                kb.op("scalar", fe2, reads=[rs("pbh", 0, 8)],
                      writes=[rs("ARB", yg0 * 2, (yg0 + 8 * TT) * 2)])

        for oc in range(NOC3):
            buf = oc % 2
            wq0 = A_WQ + buf * D
            bank = buf

            def fldw(h, oc=oc, wq0=wq0):
                return h.dma(arb[:, wq0:wq0 + D], t.s_qkv[oc])
            kb.dma("sync", f"qw{buf}", fldw, 1, reads=[rs(t.s_qkv.name)],
                   writes=[rs("ARB", wq0 * 2, (wq0 + D) * 2)])
            for kc in range(NKC):
                def fmm(h, kc=kc, wq0=wq0, bank=bank):
                    return h.eng.matmul(
                        pb[bank][:], arb[:, wq0 + kc * 128:wq0 + (kc + 1) * 128],
                        arb[:, A_Y2T + kc * TT:A_Y2T + (kc + 1) * TT],
                        start=(kc == 0), stop=(kc == NKC - 1))
                kb.op("tensor", fmm,
                      reads=[rs("ARB", wq0 * 2, (wq0 + D) * 2),
                             rs("ARB", (A_Y2T + kc * TT) * 2, (A_Y2T + (kc + 1) * TT) * 2)],
                      writes=[rs(f"pb{bank}")])

            h_ = oc % NH
            kind = oc // NH
            qp0 = A_QP + buf * TT

            def fev(h, bank=bank, qp0=qp0):
                return h.eng.copy(arb[:, qp0:qp0 + TT], pb[bank][:])
            kb.op("scalar", fev, reads=[rs(f"pb{bank}")],
                  writes=[rs("ARB", qp0 * 2, (qp0 + TT) * 2)])

            if kind < 2:
                def frot(h, qp0=qp0):
                    return h.eng.matmul(pb[2][:], t.ropeR[:],
                                        arb[:, qp0:qp0 + TT],
                                        start=True, stop=True)
                kb.op("tensor", frot,
                      reads=[rs("ropeR"), rs("ARB", qp0 * 2, (qp0 + TT) * 2)],
                      writes=[rs("pb2")])

                rt0 = A_RT1 + buf * TT

                def ft1(h, qp0=qp0, rt0=rt0, tt=tt):
                    return h.eng.tensor_tensor(
                        arb[:, rt0:rt0 + TT], arb[:, qp0:qp0 + TT],
                        t.cosb[:, tt * TT:(tt + 1) * TT], AluOpType.mult)
                kb.op("vector", ft1,
                      reads=[rs("ARB", qp0 * 2, (qp0 + TT) * 2), rs("cosb")],
                      writes=[rs("ARB", rt0 * 2, (rt0 + TT) * 2)])

                def ft2(h, qp0=qp0, tt=tt):
                    return h.eng.tensor_tensor(
                        arb[:, qp0:qp0 + TT], pb[2][:],
                        t.sinb[:, tt * TT:(tt + 1) * TT], AluOpType.mult)
                kb.op("vector", ft2, reads=[rs("pb2"), rs("sinb")],
                      writes=[rs("ARB", qp0 * 2, (qp0 + TT) * 2)])

                def ft3(h, qp0=qp0, rt0=rt0):
                    return h.eng.tensor_tensor(
                        arb[:, rt0:rt0 + TT], arb[:, rt0:rt0 + TT],
                        arb[:, qp0:qp0 + TT], AluOpType.add)
                kb.op("vector", ft3,
                      reads=[rs("ARB", rt0 * 2, (rt0 + TT) * 2),
                             rs("ARB", qp0 * 2, (qp0 + TT) * 2)],
                      writes=[rs("ARB", rt0 * 2, (rt0 + TT) * 2)])

                row0 = h_ * 128 + (0 if kind == 0 else D)

                def fst(h, rt0=rt0, row0=row0, tt=tt):
                    return h.dma(t.qk_in[row0:row0 + 128, tt * TT:(tt + 1) * TT],
                                 arb[:, rt0:rt0 + TT])
                kb.dma("gpsimd", "qkst", fst, 1,
                       reads=[rs("ARB", rt0 * 2, (rt0 + TT) * 2)],
                       writes=[rs("qk_in", (row0 * NTOK + tt * TT) * 2,
                                  ((row0 + 128) * NTOK) * 2)])
            else:
                vsb0 = A_VSB + buf * TT
                for lt in range(4):
                    def fvt(h, lt=lt, qp0=qp0):
                        return h.eng.transpose(
                            t.pbh[:, lt * 128:(lt + 1) * 128],
                            arb[:, qp0 + lt * 128:qp0 + (lt + 1) * 128],
                            t.identbf[:])
                    kb.op("tensor", fvt,
                          reads=[rs("ARB", qp0 * 2, (qp0 + TT) * 2),
                                 rs("identbf")],
                          writes=[rs("pbh", lt, lt + 1)])

                def fve(h, vsb0=vsb0):
                    return h.eng.copy(
                        arb[:, vsb0:vsb0 + TT], t.pbh[:, 0:TT])
                kb.op("scalar", fve, reads=[rs("pbh", 0, 4)],
                      writes=[rs("ARB", vsb0 * 2, (vsb0 + TT) * 2)])

                def fst(h, vsb0=vsb0, h_=h_, tt=tt):
                    return h.dma(
                        t.v_in[:, h_ * NTOK + tt * TT:h_ * NTOK + (tt + 1) * TT],
                        arb[:, vsb0:vsb0 + TT])
                kb.dma("gpsimd", "vst", fst, 1,
                       reads=[rs("ARB", vsb0 * 2, (vsb0 + TT) * 2)],
                       writes=[rs("v_in", (h_ * NTOK + tt * TT) * 2,
                                  (h_ * NTOK + (tt + 1) * TT) * 2)])

    if hasattr(t, "qk_dbg"):
        def fqkd(h):
            last = None
            for rr in range(8):
                last = h.dma(t.qk_dbg[rr * 512:(rr + 1) * 512, :],
                             t.qk_in[rr * 512:(rr + 1) * 512, :])
            return last
        kb.dma("gpsimd", "qkdbg", fqkd, 8, reads=[rs("qk_in")], writes=[rs("qk_dbg")])

    def fcc1(h):
        return h.eng.collective_compute(
            "AllGather", mybir.AluOpType.bypass,
            replica_groups=[[0, 1, 2, 3], [4, 5, 6, 7]],
            ins=[t.qk_in.ap().opt()], outs=[t.qk_out.ap().opt()])
    kb.op("gpsimd", fcc1, reads=[rs("qk_in")], writes=[rs("qk_out")])

    def fcc2(h):
        return h.eng.collective_compute(
            "AllGather", mybir.AluOpType.bypass,
            replica_groups=[[0, 1, 2, 3], [4, 5, 6, 7]],
            ins=[t.v_in.ap().opt()], outs=[t.v_out.ap().opt()])
    kb.op("gpsimd", fcc2, reads=[rs("v_in")], writes=[rs("v_out")])

    OFF_K = 0
    OFF_V = 4
    OFF_Q = 8
    OFF_O = 12

    for hh in range(4):
        kbuf = hh % 2
        ks0 = A_KS + kbuf * 8192
        vs0 = A_VS + kbuf * 8192
        qs0 = A_QS + kbuf * 8192
        def fkl(h, hh=hh, ks0=ks0):
            h.eng.reg_load(t.roff, t.offt[0:1, OFF_K + hh:OFF_K + hh + 1])
            return h.dma(
                bass.AP(arb, ks0, [[ARBW, 128], [NTOK, 4], [1, NTOK]]),
                bass.AP(t.qk_out, t.roff, [[NTOK, 128], [2 * D * NTOK, 4], [1, NTOK]]))
        kb.dma("gpsimd", f"kld{kbuf}", fkl, 1,
               reads=[rs("qk_out"), rs("offt")],
               writes=[rs("ARB", ks0 * 2, (ks0 + 4 * NTOK) * 2)])

        def fvl(h, hh=hh, vs0=vs0):
            h.eng.reg_load(t.roff, t.offt[0:1, OFF_V + hh:OFF_V + hh + 1])
            return h.dma(
                bass.AP(arb, vs0, [[ARBW, 128], [NTOK, 4], [1, NTOK]]),
                bass.AP(t.v_out, t.roff,
                        [[NH * NTOK, 128], [128 * NH * NTOK, 4], [1, NTOK]]))
        kb.dma("gpsimd", f"vld{kbuf}", fvl, 1,
               reads=[rs("v_out"), rs("offt")],
               writes=[rs("ARB", vs0 * 2, (vs0 + 4 * NTOK) * 2)])

        def fql(h, hh=hh, qs0=qs0):
            h.eng.reg_load(t.roff, t.offt[0:1, OFF_Q + hh:OFF_Q + hh + 1])
            return h.dma(
                bass.AP(arb, qs0, [[ARBW, 128], [NTOK, 4], [1, NTOK]]),
                bass.AP(t.qk_out, t.roff, [[NTOK, 128], [2 * D * NTOK, 4], [1, NTOK]]))
        kb.dma("gpsimd", f"qld{kbuf}", fql, 1, reads=[rs("qk_out"), rs("offt")],
               writes=[rs("ARB", qs0 * 2, (qs0 + 4 * NTOK) * 2)])

        for qt in range(16):
            g_q = qt // 2
            r_q, ltb_q = _chunk_owner(g_q)
            qcol = qs0 + r_q * NTOK + (0 if ltb_q == 0 else 1024) + (qt % 2) * TT

            nkc_q = 4 * qt + 4
            lbank = 3 if qt % 2 == 0 else 5

            def emit_lav(kc, vs0=vs0, lbank=lbank, nkc_q=nkc_q):
                g = kc // 8
                r, ltb = _chunk_owner(g)
                scol = r * NTOK + (ltb + kc % 8) * 128
                pt0 = A_PT + (kc % 3) * TT
                last = (kc == nkc_q - 1)

                def fl(h, kc=kc, pt0=pt0, lbank=lbank, last=last):
                    return h.eng.matmul(
                        pb[lbank][:], t.ones128bf[:], arb[:, pt0:pt0 + TT],
                        start=(kc == 0), stop=last)
                kb.op("tensor", fl,
                      reads=[rs("ones128bf"), rs("ARB", pt0 * 2, (pt0 + TT) * 2)],
                      writes=[rs(f"pb{lbank}")])

                def fav(h, scol=scol, vs0=vs0, pt0=pt0, kc=kc, last=last):
                    return h.eng.matmul(
                        pb[2][:], arb[:, vs0 + scol:vs0 + scol + 128],
                        arb[:, pt0:pt0 + TT], start=(kc == 0), stop=last)
                kb.op("tensor", fav,
                      reads=[rs("ARB", (vs0 + scol) * 2, (vs0 + scol + 128) * 2),
                             rs("ARB", pt0 * 2, (pt0 + TT) * 2)],
                      writes=[rs("pb2")])

            for kc in range(nkc_q):
                g = kc // 8
                r, ltb = _chunk_owner(g)
                scol = r * NTOK + (ltb + kc % 8) * 128
                sbank = kc % 2
                pt0 = A_PT + (kc % 3) * TT

                def fsm(h, sbank=sbank, scol=scol, ks0=ks0, qcol=qcol):
                    return h.eng.matmul(
                        pb[sbank][:], arb[:, ks0 + scol:ks0 + scol + 128],
                        arb[:, qcol:qcol + TT], start=True, stop=True)
                kb.op("tensor", fsm,
                      reads=[rs("ARB", (ks0 + scol) * 2, (ks0 + scol + 128) * 2),
                             rs("ARB", qcol * 2, (qcol + TT) * 2)],
                      writes=[rs(f"pb{sbank}")])

                def fex(h, sbank=sbank, pt0=pt0):
                    return h.eng.activation(
                        arb[:, pt0:pt0 + TT], pb[sbank][:], AF.Exp,
                        bias=t.fconst[:, 1:2], scale=INV_SQRT_HD)
                kb.op("scalar", fex, reads=[rs(f"pb{sbank}"), rs("fconst")],
                      writes=[rs("ARB", pt0 * 2, (pt0 + TT) * 2)])

                if hh == 0 and qt == 0 and kc == 0 and TRUNC.get("dumps"):
                    dump_arb(kb, t, pt0, TT, "pt_q0k0_premask")
                m = kc - 4 * qt
                if m >= 0:
                    def fmk(h, pt0=pt0, m=m):
                        return h.eng.tensor_tensor(
                            arb[:, pt0:pt0 + TT], arb[:, pt0:pt0 + TT],
                            t.masks[:, m * TT:(m + 1) * TT], AluOpType.mult)
                    kb.op("vector", fmk,
                          reads=[rs("ARB", pt0 * 2, (pt0 + TT) * 2), rs("masks")],
                          writes=[rs("ARB", pt0 * 2, (pt0 + TT) * 2)])

                if hh == 0 and qt == 0 and kc == 0 and TRUNC.get("dumps"):
                    dump_arb(kb, t, pt0, TT, "pt_q0k0_postmask")
                    dump_arb(kb, t, ks0, TT, "k_h0")
                    dump_arb(kb, t, vs0, TT, "v_h0")
                    dump_arb(kb, t, qcol, TT, "q_h0")

                if kc > 1:
                    emit_lav(kc - 2)
            emit_lav(nkc_q - 2)
            emit_lav(nkc_q - 1)

            def frc(h, lbank=lbank):
                return h.eng.reciprocal(t.rl[:], pb[lbank][:])
            kb.op("vector", frc, reads=[rs(f"pb{lbank}")], writes=[rs("rl")])

            def foe(h):
                return h.eng.tensor_tensor(
                    arb[:, A_OST:A_OST + TT], pb[2][:], t.rl[:], AluOpType.mult)
            kb.op("vector", foe, reads=[rs("pb2"), rs("rl")],
                  writes=[rs("ARB", A_OST * 2, (A_OST + TT) * 2)])

            row0 = hh * 128

            def fos(h, row0=row0, qt=qt):
                return h.dma(t.o_in[row0:row0 + 128, qt * TT:(qt + 1) * TT],
                             arb[:, A_OST:A_OST + TT])
            kb.dma("gpsimd", "ost", fos, 1,
                   reads=[rs("ARB", A_OST * 2, (A_OST + TT) * 2)],
                   writes=[rs("o_in", (row0 * 8192 + qt * TT) * 2,
                              ((row0 + 128) * 8192) * 2)])

    if hasattr(t, "o_dbg"):
        def fod(h):
            last = None
            for rr in range(2):
                last = h.dma(t.o_dbg[rr * 256:(rr + 1) * 256, :],
                             t.o_in[rr * 256:(rr + 1) * 256, :])
            return last
        kb.dma("gpsimd", "odbg", fod, 2, reads=[rs("o_in")], writes=[rs("o_dbg")])

    def fcc3(h):
        return h.eng.collective_compute(
            "AllGather", mybir.AluOpType.bypass,
            replica_groups=[[0, 1, 2, 3], [4, 5, 6, 7]],
            ins=[t.o_in.ap().opt()], outs=[t.o_out.ap().opt()])
    kb.op("gpsimd", fcc3, reads=[rs("o_in")], writes=[rs("o_out")])

    def fol(h):
        last = None
        for r in range(4):
            for half in range(2):
                oi = OFF_O + r * 2 + half
                c0 = A_OT + (4 * r) * NTOK + half * 1024
                h.eng.reg_load(t.roff, t.offt[0:1, oi:oi + 1])
                last = h.dma(
                    bass.AP(arb, c0, [[ARBW, 128], [NTOK, 4], [1, 1024]]),
                    bass.AP(t.o_out, t.roff,
                            [[8192, 128], [128 * 8192, 4], [1, 1024]]))
        return last
    kb.dma("gpsimd", "old", fol, 8, reads=[rs("o_out"), rs("offt")],
           writes=[rs("ARB", A_OT * 2, (A_OT + NH * NTOK) * 2)])
    def fwl(h):
        last = None
        for fc in range(NKC):
            c0 = A_WO + fc * D
            last = h.dma(arb[:, c0:c0 + D], t.s_wo[fc * 128:(fc + 1) * 128, :])
        return last
    kb.dma("sync", "wol", fwl, NKC, reads=[rs(t.s_wo.name)],
           writes=[rs("ARB", A_WO * 2, (A_WO + NKC * D) * 2)])

    for tc16 in range(16):
        row0 = tc16 * 128
        xslot = tc16 % 4

        def fxl(h, row0=row0, xslot=xslot):
            return h.dma(t.xt[:, xslot * D:(xslot + 1) * D],
                         x_src[row0:row0 + 128, :])
        kb.dma("scalar", f"xra{xslot}", fxl, 1,
               reads=[rs(x_src.name, row0 * D, (row0 + 128) * D)],
               writes=[rs("xt", xslot * D * 4, (xslot + 1) * D * 4)])

        for h_ in range(NH):
            oc0 = A_OT + h_ * NTOK + tc16 * 128
            wc0 = A_WO + h_ * D
            for fg in range(4):
                def fmm(h, oc0=oc0, wc0=wc0, fg=fg, h_=h_):
                    return h.eng.matmul(
                        pb[fg][:], arb[:, oc0:oc0 + 128],
                        arb[:, wc0 + fg * TT:wc0 + (fg + 1) * TT],
                        start=(h_ == 0), stop=(h_ == NH - 1))
                kb.op("tensor", fmm,
                      reads=[rs("ARB", oc0 * 2, (oc0 + 128) * 2),
                             rs("ARB", (wc0 + fg * TT) * 2, (wc0 + (fg + 1) * TT) * 2)],
                      writes=[rs(f"pb{fg}")])
        for fg in range(4):
            _stt_res_add(kb, t, pb[fg], f"pb{fg}", xslot * D, fg)

        def fxs(h, row0=row0, xslot=xslot):
            return h.dma(x_dst[row0:row0 + 128, :],
                         t.xt[:, xslot * D:(xslot + 1) * D])
        kb.dma("gpsimd", "xsa", fxs, 1,
               reads=[rs("xt", xslot * D * 4, (xslot + 1) * D * 4)],
               writes=[rs(x_dst.name, row0 * D, (row0 + 128) * D)])

    def ffence2(h):
        return h.eng.memset(t.fencet[:], 0.0)
    kb.fence("gpsimd", ffence2, [rs(x_dst.name)])


def _run(inputs, debug=False, phases=("ffn1", "attn", "ffn2")):
    shared = _prep_shared(inputs)
    in_maps = []
    for core in range(8):
        m = dict(shared)
        m.update(_prep_core(inputs, core))
        in_maps.append(m)
    nc = build_nc(debug=debug, phases=phases)
    res = run_bass_kernel_spmd(nc, in_maps, core_ids=list(range(8)))
    return res


def kernel(**inputs):
    res = _run(inputs)
    out = np.zeros((2, 8192, 2048), np.float32)
    for core in range(8):
        b, j = core // 4, core % 4
        o = np.asarray(res.results[core]["OUT"])
        out[b, j * 1024:(j + 1) * 1024, :] = o[0:1024, :]
        out[b, (7 - j) * 1024:(8 - j) * 1024, :] = o[1024:2048, :]
    return out
